# version 09
# baseline (speedup 1.0000x reference)
import contextlib
import numpy as np
import ml_dtypes

import concourse.bass as bass
import concourse.mybir as mybir
from concourse.bass_utils import run_bass_kernel_spmd

F32 = mybir.dt.float32
F32R = mybir.dt.float32r
BF16 = mybir.dt.bfloat16
AF = mybir.ActivationFunctionType
ALU = mybir.AluOpType

NEG_BIG = -3.0e30


def make_cfg(D=1024, H=16, TG=4096, FF=4096, NE=8, ncores=8):
    c = {}
    c["D"], c["H"], c["TG"], c["FF"], c["NE"], c["ncores"] = D, H, TG, FF, NE, ncores
    c["DC"] = D // 128
    c["HC"] = H // 2
    c["F3"] = 3 * D
    c["FC3"] = 3 * D // 128
    c["TQ"] = TG // 4
    c["TL"] = TG // 2
    c["KC"] = TG // 128
    c["KCS"] = c["TQ"] // 128
    c["FFC"] = FF // 128
    c["TB"] = min(256, c["TL"])
    c["NTB"] = c["TL"] // c["TB"]
    c["QB"] = min(256, c["TQ"])
    c["NQB"] = c["TL"] // c["QB"]
    c["NQBQ"] = c["TQ"] // c["QB"]
    c["NJ"] = c["QB"] // 128
    c["MTB"] = min(256, c["TL"])
    c["NMTB"] = c["TL"] // c["MTB"]
    c["DH"] = min(512, D)
    c["NDH"] = D // c["DH"]
    c["VW"] = 65
    c["NS"] = c["F3"] // 256
    c["NMASK"] = 4 * c["NJ"]
    c["PTN"] = min(16, c["KC"])
    return c


def attn_sched(cfg, att, i):
    KCS, KC, NJ, NQBQ = cfg["KCS"], cfg["KC"], cfg["NJ"], cfg["NQBQ"]
    if att == 0:
        return [(k, None) for k in range(KC)], []
    s0, s1, s2, s3 = 0, KCS, 2 * KCS, 3 * KCS
    if i < NQBQ:
        p = i * NJ
        main = [(s0 + k, None) for k in range(p)] + \
               [(s0 + p + j, 0 * NJ + j) for j in range(NJ)] + \
               [(s2 + p + j, 1 * NJ + j) for j in range(NJ)]
        plist = [s0 + k for k in range(p + NJ, KCS)] + [s2 + k for k in range(p)]
        parts = [(plist, 0)] if plist else []
        return main, parts
    ii = i - NQBQ
    p = ii * NJ
    main = [(s0 + k, None) for k in range(KCS)] + \
           [(s2 + k, None) for k in range(KCS)] + \
           [(s3 + k, None) for k in range(p)] + \
           [(s1 + p + j, 2 * NJ + j) for j in range(NJ)] + \
           [(s3 + p + j, 3 * NJ + j) for j in range(NJ)]
    plist = [s1 + k for k in range(p)] + [s3 + k for k in range(p + NJ, KCS)]
    parts = [(plist, 1)] if plist else []
    return main, parts


class Sem:
    def __init__(self, es, nc, name):
        self.h = es.enter_context(nc.semaphore(name))
        self.n = 0

    def inc(self, ins, k=1):
        ins.then_inc(self.h, k)
        self.n += k
        return self.n

    def issue(self, eng, out, in_, **kw):
        eng.wait_ge(self.h, self.n)
        return self.inc(eng.dma_start(out=out, in_=in_, **kw), 16)


class B:
    pass


def build_program(cfg, stop_after=None, dbg_out=False, dbg_moe=False):
    b = B()
    b.cfg = cfg
    c = cfg
    nc = bass.Bass("TRN2", target_bir_lowering=False, num_devices=c["ncores"],
                   dynamic_dma_scratch_size=16384)
    b.nc = nc
    es = contextlib.ExitStack()
    b.es = es
    D, DC, H, HC, F3, FC3 = c["D"], c["DC"], c["H"], c["HC"], c["F3"], c["FC3"]
    TG, TQ, TL, KC, KCS, VW = c["TG"], c["TQ"], c["TL"], c["KC"], c["KCS"], c["VW"]
    FF, FFC, NE = c["FF"], c["FFC"], c["NE"]
    TB, NTB, QB, NQB, NQBQ, NJ = c["TB"], c["NTB"], c["QB"], c["NQB"], c["NQBQ"], c["NJ"]
    MTB, NMTB, DH, NDH = c["MTB"], c["NMTB"], c["DH"], c["NDH"]
    NS, NMASK, PTN = c["NS"], c["NMASK"], c["PTN"]

    NBC = 2 * FC3 + 2 * DC + NE * FFC + 4 * DC
    OB_QKV1 = 0
    OB_OUT1 = FC3
    OB_QKV2 = FC3 + DC
    OB_OUT2 = 2 * FC3 + DC
    OB_FC1 = 2 * FC3 + 2 * DC
    OB_LNB = 2 * FC3 + 2 * DC + NE * FFC
    BET_PLAIN, BET_ATTN, BET_LN1, BET_LN2 = 0, DC, 2 * DC, 3 * DC
    GAM_PLAIN, GAM_ATTN, GAM_LN1, GAM_LN2 = 0, D, 2 * D, 3 * D

    d = {}
    d["xfm"] = nc.dram_tensor("xfm", [D, TL], F32, kind="ExternalInput").ap()
    d["wqkv1"] = nc.dram_tensor("wqkv1", [NS, 128, DC * 256], F32, kind="ExternalInput").ap()
    d["wqkv2"] = nc.dram_tensor("wqkv2", [NS, 128, DC * 256], F32, kind="ExternalInput").ap()
    d["wout1"] = nc.dram_tensor("wout1", [128, DC * D], F32, kind="ExternalInput").ap()
    d["wout2"] = nc.dram_tensor("wout2", [128, DC * D], F32, kind="ExternalInput").ap()
    d["fc1"] = nc.dram_tensor("fc1", [NE, 4, 128, DC * FF // 4], F32, kind="ExternalInput").ap()
    d["fc2"] = nc.dram_tensor("fc2", [NE, 4, 128, FFC * D // 4], F32, kind="ExternalInput").ap()
    d["biasp"] = nc.dram_tensor("biasp", [128, NBC], F32, kind="ExternalInput").ap()
    d["lnrows"] = nc.dram_tensor("lnrows", [1, 4 * D], F32, kind="ExternalInput").ap()
    d["fc2b"] = nc.dram_tensor("fc2b", [NE, D], F32, kind="ExternalInput").ap()
    d["gatew"] = nc.dram_tensor("gatew", [128, DC * NE], F32, kind="ExternalInput").ap()
    d["identb"] = nc.dram_tensor("identb", [128, 128], BF16, kind="ExternalInput").ap()
    d["identf"] = nc.dram_tensor("identf", [128, 128], F32, kind="ExternalInput").ap()
    d["shifteye"] = nc.dram_tensor("shifteye", [64, 128], BF16, kind="ExternalInput").ap()
    d["masks"] = nc.dram_tensor("masks", [128, NMASK * QB], BF16, kind="ExternalInput").ap()
    d["onescol"] = nc.dram_tensor("onescol", [128, 1], BF16, kind="ExternalInput").ap()
    d["onesat64"] = nc.dram_tensor("onesat64", [65, 128], F32, kind="ExternalInput").ap()
    d["fcol"] = nc.dram_tensor("fcol", [128, 2], F32, kind="ExternalInput").ap()
    d["bdiag"] = nc.dram_tensor("bdiag", [128, DC * H], BF16, kind="ExternalInput").ap()
    out_d = nc.dram_tensor("o", [TL // 128, 128, NDH * DH], F32, kind="ExternalOutput").ap()
    if dbg_moe:
        dbg_nx2 = nc.dram_tensor("dbg_nx2", [128, DC * TL], BF16, kind="ExternalOutput").ap()
        dbg_gts = nc.dram_tensor("dbg_gts", [128, (TL // 128) * NE], F32, kind="ExternalOutput").ap()
        dbg_gts2 = nc.dram_tensor("dbg_gts2", [128, (TL // 128) * NE], F32, kind="ExternalOutput").ap()
        dbg_h = nc.dram_tensor("dbg_h", [128, FFC * MTB], BF16, kind="ExternalOutput").ap()
        dbg_sw0 = nc.dram_tensor("dbg_sw0", [128, 4192], F32, kind="ExternalOutput").ap()
        dbg_sw1 = nc.dram_tensor("dbg_sw1", [128, 4192], F32, kind="ExternalOutput").ap()
        dbg_fc1 = nc.dram_tensor("dbg_fc1", [128, DC * FF], BF16, kind="ExternalOutput").ap()
        dbg_fc2 = nc.dram_tensor("dbg_fc2", [128, FFC * D], BF16, kind="ExternalOutput").ap()
    _scrkind = "ExternalOutput" if dbg_out else "Internal"
    scr_s = nc.dram_tensor("scr_s", [D, TL], F32, kind=_scrkind).ap()
    scr_x1 = nc.dram_tensor("scr_x1", [D, TL], F32, kind=_scrkind).ap()
    scr_y2 = nc.dram_tensor("scr_y2", [D, TL], F32, kind=_scrkind).ap()
    scr_xb = nc.dram_tensor("scr_xb", [D, TL], F32, kind=_scrkind).ap()
    scr_st = nc.dram_tensor("scr_st", [NTB, 2, TB], F32, kind="Internal").ap()
    scr_q = [nc.dram_tensor(f"scr_q{a}", [DC, 128, TL], BF16, kind="Internal").ap()
             for a in range(2)]
    EXW = DC * TL + 2 * KCS * H * VW
    send = [nc.dram_tensor(f"send{a}", [128, EXW], BF16, kind="Internal").ap() for a in range(2)]
    recv = [nc.dram_tensor(f"recv{a}", [2, 128, EXW], BF16, kind="Internal").ap() for a in range(2)]

    import math as _m
    cur = [16896]

    def place(name, shape, dtype, at=None):
        if at is None:
            at = cur[0]
        t = nc.alloc_sbuf_tensor_at(name, shape, dtype, offset=at)
        esz = 2 if dtype == BF16 else 4
        end = at + _m.prod(shape[1:]) * esz if len(shape) > 1 else at + esz
        cur[0] = max(cur[0], (end + 63) // 64 * 64)
        return t.ap()

    smallswin = place("smallswin", [128, 4192], F32, 16896) if dbg_moe else None
    cur[0] = 16896
    identb = place("identb", [128, 128], BF16)
    identf = place("identf", [128, 128], F32)
    shifteye = place("shifteye", [64, 128], BF16)
    onescol = place("onescol", [128, 1], BF16)
    onesat64 = place("onesat64", [65, 128], F32)
    fcol = place("fcol", [128, 2], F32)
    bdiag = place("bdiag", [128, DC, H], BF16)
    biasp = place("biasp", [128, NBC], F32)
    gatewb = place("gatewb", [128, DC, NE], BF16)
    gatewf = place("gatewf", [128, DC, NE], F32)
    lnbuf = place("lnbuf", [1, 2, 128], BF16)
    lnbuf32 = place("lnbuf32", [1, 2, 128], F32)
    mrow = place("mrow", [1, 3, TB], F32)
    mrowb = place("mrowb", [1, 2, TB], BF16)
    XR = place("XR", [128, 2, 256], BF16)
    qkn = place("qkn", [H, 8], F32)
    nbrow = place("nbrow", [1, H], F32)
    bb = place("bb", [128, 2 * H], F32)
    lrow65 = place("lrow65", [65, QB], F32)
    gts = place("gts", [128, TL // 128, NE], F32)
    aotmp = place("aotmp", [64, QB], BF16)
    QS = place("QS", [128, 2, TB], BF16)
    qtmp = place("qtmp", [128, 2], F32)
    qmaxcol = place("qmaxcol", [128, DC], F32)
    qmaxb = place("qmaxb", [128, DC], BF16)
    STR = place("STR", [33, 2, TB], F32)
    STG = place("STG", [128, 3, TB], F32)
    SQ = place("SQ", [128, TB], F32)
    VSTb = SQ.bitcast(BF16).rearrange("p (r t) -> p r t", r=2)
    FS = cur[0]
    WSL = place("WSL", [128, 2, DC, 256], BF16, FS)
    AO128 = place("AO128", [128, HC, QB], BF16, FS)
    OSTG = place("OSTG", [128, 2, NDH * DH], F32, FS)
    cur[0] = (FS + max(2 * DC * 256 * 2, HC * QB * 2, 2 * NDH * DH * 4) + 63) // 64 * 64
    CS = cur[0]
    Qt = place("Qt", [128, 2, DC, QB], BF16, CS)
    gerow = place("gerow", [NE, TL], BF16, CS)
    fc2bs = place("fc2bs", [NE, D], BF16, CS + TL * 2)
    fc2bs32 = place("fc2bs32", [NE, D], F32, CS + TL * 2 + D * 2)
    h2 = place("h2", [128, 2, FFC // 2, MTB], BF16, CS)
    cur[0] = (CS + max(2 * DC * QB * 2, TL * 2 + D * 2 + D * 4, FFC * MTB * 2) + 63) // 64 * 64
    DS = cur[0]
    Z = place("Z", [128, DC, TL], BF16, DS)
    WOUT = place("WOUT", [128, DC, D], BF16, DS)
    PT = place("PT", [128, PTN, QB], BF16, DS + DC * D * 2)
    masks = place("masks", [128, NMASK, QB], BF16, DS + DC * D * 2 + PTN * QB * 2)
    cur[0] = (DS + max(DC * TL * 2, DC * D * 2 + PTN * QB * 2 + NMASK * QB * 2) + 63) // 64 * 64
    AS = cur[0]
    X = place("X", [128, DC, TL], F32, AS)
    K = place("K", [128, DC, TG], BF16, AS)
    Y2 = place("Y2", [128, DC, TL], F32, AS)
    FC1 = place("FC1", [128, 4, DC, FF // 4], BF16, AS)
    cur[0] = (AS + max(DC * TL * 4, DC * TG * 2, DC * FF * 2) + 63) // 64 * 64
    BS = cur[0]
    Y = place("Y", [128, DC, TL], F32, BS)
    V = place("V", [128, KC, H * VW], BF16, BS)
    FC2 = place("FC2", [128, FFC, D], BF16, BS)
    cur[0] = BS + max(DC * TL * 4, KC * H * VW * 2, FFC * D * 2)
    assert cur[0] <= nc.SBUF_PARTITION_SIZE_BYTES, f"SBUF over budget: {cur[0]}"
    b.sbuf_used = cur[0]

    pB = [es.enter_context(nc.psum_tensor(f"pB{i}", [128, 512], F32)) for i in range(6)]
    sA1 = es.enter_context(nc.psum_tensor("sA1", [128, 512], F32))
    sA2 = es.enter_context(nc.psum_tensor("sA2", [128, 512], F32))
    pTRb = pB[3].ap().bitcast(BF16)

    sSY = Sem(es, nc, "sSY")
    sW = Sem(es, nc, "sW")
    sWC = Sem(es, nc, "sWC")
    sCC = Sem(es, nc, "sCC")
    sMM = Sem(es, nc, "sMM")
    sSC = Sem(es, nc, "sSC")
    sVE = Sem(es, nc, "sVE")
    sGC = Sem(es, nc, "sGC")
    sT = [Sem(es, nc, f"sT{i}") for i in range(3)]
    sLN = Sem(es, nc, "sLN")
    sRW = Sem(es, nc, "sRW")
    sOS = Sem(es, nc, "sOS")
    os_last = [0, 0]
    sQS = Sem(es, nc, "sQS")
    sQT = Sem(es, nc, "sQT")
    qt_free = [0, 0]
    blk = es.enter_context(nc.Block())

    def f32r(ap):
        return ap.bitcast(F32R)

    def T(fn): blk.tensor(fn)
    def VEC(fn): blk.vector(fn)
    def SCL(fn): blk.scalar(fn)
    def GP(fn): blk.gpsimd(fn)
    def SY(fn): blk.sync(fn)

    def ph0(sy):
        for name, dst in (("identb", identb), ("identf", identf),
                          ("shifteye", shifteye), ("onescol", onescol),
                          ("onesat64", onesat64), ("fcol", fcol),
                          ("biasp", biasp)):
            sSY.issue(sy, dst, d[name])
        sSY.issue(sy, bdiag, d["bdiag"].rearrange("p (c h) -> p c h", c=DC))
        sSY.issue(sy, X, d["xfm"].rearrange("(c p) t -> p c t", p=128))
    SY(ph0)
    CONSTS = sSY.n

    def ph0g(sy):
        sSY.issue(sy, gatewf, d["gatew"].rearrange("p (c e) -> p c e", c=DC))
    SY(ph0g)
    GATEWF_AT = sSY.n

    def ph0gc(s):
        s.wait_ge(sSY.h, GATEWF_AT)
        sSC.inc(s.activation(gatewb[:, :, :].rearrange("p c e -> p (c e)"),
                             gatewf[:, :, :].rearrange("p c e -> p (c e)"), AF.Identity))
    SCL(ph0gc)
    GATEWB_AT = sSC.n

    def ph0s(s):
        s.wait_ge(sSY.h, CONSTS)
    SCL(ph0s)

    def ph0v(v):
        v.wait_ge(sSY.h, CONSTS)
    VEC(ph0v)

    stat_flush = {}

    def emit_stats(src_ap, tbi, colofs, w, first, last):
        flush_ve = stat_flush.get(tbi % 4, 0) if first else None
        mm_war = sMM.n

        def fs(v):
            v.wait_ge(sVE.h, sVE.n)
            v.wait_ge(sMM.h, mm_war)
            v.tensor_copy(XR[:, 0, 0:w], src_ap)
            sVE.inc(v.tensor_mul(XR[:, 1, 0:w], src_ap, src_ap))
        VEC(fs)
        sc_at = sVE.n
        p0 = 32 * (tbi % 2)
        bc = 256 * ((tbi // 2) % 2)
        a1 = sA1[p0:p0 + 1, bc + colofs:bc + colofs + w]
        a2 = sA2[p0:p0 + 1, bc + colofs:bc + colofs + w]

        def ft(t):
            t.wait_ge(sVE.h, sc_at)
            if first and flush_ve is not None:
                t.wait_ge(sVE.h, flush_ve)
            t.matmul(a1, onescol, XR[:, 0, 0:w], start=first, stop=last)
            sMM.inc(t.matmul(a2, onescol, XR[:, 1, 0:w], start=first, stop=last))
        T(ft)
        return sMM.n

    flush_rw = {}

    def emit_stats_flush(tbi, stats_mm):
        p0 = 32 * (tbi % 2)
        bc = 256 * ((tbi // 2) % 2)
        rw_war = flush_rw.get(p0, 0)

        def fvc(v):
            v.wait_ge(sMM.h, stats_mm)
            v.wait_ge(sRW.h, rw_war)
            v.wait_ge(sVE.h, sVE.n)
            v.tensor_copy(STR[p0:p0 + 1, 0, :], sA1[p0:p0 + 1, bc:bc + TB])
            v.wait_ge(sVE.h, sVE.n)
            sVE.inc(v.tensor_copy(STR[p0:p0 + 1, 1, :], sA2[p0:p0 + 1, bc:bc + TB]))
        VEC(fvc)
        ve_c = sVE.n

        def fy(sy):
            sy.wait_ge(sVE.h, ve_c)
            sRW.issue(sy, scr_st[tbi, 0, :], STR[p0:p0 + 1, 0, :])
            sRW.issue(sy, scr_st[tbi, 1, :], STR[p0:p0 + 1, 1, :])
        SY(fy)
        stat_flush[tbi % 4] = sVE.n
        flush_rw[p0] = sRW.n

    def emit_ln_rows(tbi, stats_ve):
        ivD = 1.0 / D
        mm_war = sMM.n

        def fy(sy):
            sy.wait_ge(sMM.h, mm_war)
            sRW.issue(sy, mrow[0:1, 0, :], scr_st[tbi, 0, :])
            sRW.issue(sy, mrow[0:1, 1, :], scr_st[tbi, 1, :])
        SY(fy)
        sy_at = sRW.n

        def fv(v):
            v.wait_ge(sRW.h, sy_at)
            v.wait_ge(sVE.h, sVE.n)
            v.tensor_scalar(mrow[0:1, 0, :], mrow[0:1, 0, :], ivD, None, ALU.mult)
            v.wait_ge(sVE.h, sVE.n)
            sVE.inc(v.tensor_scalar(mrow[0:1, 1, :], mrow[0:1, 1, :], ivD, None, ALU.mult))
        VEC(fv)
        ve1 = sVE.n

        def fsq(s):
            s.wait_ge(sVE.h, ve1)
            sSC.inc(s.activation(mrow[0:1, 2, :], mrow[0:1, 0, :], AF.Square))
        SCL(fsq)
        sc1 = sSC.n

        def fv2(v):
            v.wait_ge(sSC.h, sc1)
            sVE.inc(v.scalar_tensor_tensor(mrow[0:1, 1, :], mrow[0:1, 1, :], 1e-5,
                                           mrow[0:1, 2, :], ALU.add, ALU.subtract))
        VEC(fv2)
        ve2 = sVE.n

        def fs2(s):
            s.wait_ge(sVE.h, ve2)
            sSC.inc(s.activation(mrow[0:1, 2, :], mrow[0:1, 1, :], AF.Sqrt))
        SCL(fs2)
        sc2 = sSC.n

        def fv3(v):
            v.wait_ge(sSC.h, sc2)
            sVE.inc(v.reciprocal(mrow[0:1, 2, :], mrow[0:1, 2, :]))
            v.wait_ge(sVE.h, sVE.n)
            sVE.inc(v.tensor_mul(mrow[0:1, 0, :], mrow[0:1, 0, :], mrow[0:1, 2, :]))
            v.wait_ge(sVE.h, sVE.n)
            sVE.inc(v.tensor_scalar(mrow[0:1, 0, :], mrow[0:1, 0, :], -1.0, None, ALU.mult))
        VEC(fv3)
        ve3 = sVE.n

        def fs3(s):
            s.wait_ge(sVE.h, ve3)
            s.activation(mrowb[0:1, 0, :], mrow[0:1, 2, :], AF.Identity)
            sSC.inc(s.activation(mrowb[0:1, 1, :], mrow[0:1, 0, :], AF.Identity))
        SCL(fs3)
        rows_sc = sSC.n

        def fv4(v):
            v.wait_ge(sSC.h, rows_sc)
            sVE.inc(v.engine_nop())
        VEC(fv4)
        return sVE.n

    lncast = {}

    def emit_lnbuf(slot, grow, ci, mm_war):
        cast_war = lncast.get(slot, 0)

        def fy(sy):
            sy.wait_ge(sSC.h, cast_war)
            sLN.issue(sy, lnbuf32[0:1, slot, :],
                      d["lnrows"][0:1, grow + ci * 128:grow + (ci + 1) * 128])
        SY(fy)
        ln_at = sLN.n

        def fc(s):
            s.wait_ge(sLN.h, ln_at)
            s.wait_ge(sMM.h, mm_war)
            sSC.inc(s.activation(lnbuf[0:1, slot, :], lnbuf32[0:1, slot, :], AF.Identity))
        SCL(fc)
        lncast[slot] = sSC.n
        return sSC.n

    def emit_ln_chunk(tbi, ci, rows_ve, src_ap, dst_ap, grow, bset, w=None,
                      colofs=0, wait_sy=None):
        w = w or TB
        slot = ci % 2
        mm_war = sMM.n
        sy_at = emit_lnbuf(slot, grow, ci, mm_war)
        pR = pB[slot][0:128, 0:w]
        pA = pB[2 + slot][0:128, 0:w]
        ve_war = sVE.n

        def ft(t):
            t.wait_ge(sSC.h, sy_at)
            t.wait_ge(sVE.h, max(rows_ve, ve_war))
            t.matmul(pR, lnbuf[0:1, slot, :], mrowb[0:1, 0, colofs:colofs + w],
                     start=True, stop=True)
            sMM.inc(t.matmul(pA, lnbuf[0:1, slot, :], mrowb[0:1, 1, colofs:colofs + w],
                             start=True, stop=True))
        T(ft)
        mm_at = sMM.n
        bcol = biasp[:, OB_LNB + bset + ci:OB_LNB + bset + ci + 1]

        def fv(v):
            v.wait_ge(sMM.h, mm_at)
            if wait_sy is not None:
                v.wait_ge(wait_sy[0].h, wait_sy[1])
            v.wait_ge(sVE.h, sVE.n)
            sVE.inc(v.tensor_mul(STG[:, 2, 0:w], src_ap, pR))
            v.wait_ge(sVE.h, sVE.n)
            sVE.inc(v.scalar_tensor_tensor(dst_ap, STG[:, 2, 0:w], bcol, pA, ALU.add, ALU.add))
        VEC(fv)
        return sVE.n

    NHF = TB // 256
    ve_stats = 0
    for tbi in range(NTB):
        for ci in range(DC):
            for hf in range(NHF):
                ve_stats = emit_stats(X[:, ci, tbi * TB + hf * 256:tbi * TB + (hf + 1) * 256],
                                      tbi, hf * 256, 256, ci == 0 and hf == 0,
                                      ci == DC - 1 and hf == NHF - 1)
        emit_stats_flush(tbi, ve_stats)

    ystats = []
    for tbi in range(NTB):
        rows_ve = emit_ln_rows(tbi, ve_stats if tbi == 0 else sVE.n)
        sl = slice(tbi * TB, (tbi + 1) * TB)
        yv = 0
        for ci in range(DC):
            emit_ln_chunk(tbi, ci, rows_ve, X[:, ci, sl], Y[:, ci, sl], GAM_PLAIN, BET_PLAIN)
            slot_war = sT[ci % 2].n

            def fv(v, ci=ci, sl=sl, slot_war=slot_war):
                v.wait_ge(sVE.h, sVE.n)
                v.wait_ge(sT[ci % 2].h, slot_war)
                sVE.inc(v.tensor_add(STG[:, ci % 2, :], X[:, ci, sl], Y[:, ci, sl]))
            VEC(fv)
            ve_s = sVE.n

            def fy(sy, ci=ci, sl=sl, ve_s=ve_s):
                sy.wait_ge(sVE.h, ve_s)
                sT[ci % 2].issue(sy, scr_s[ci * 128:(ci + 1) * 128, sl], STG[:, ci % 2, :])
            SY(fy)
            for hf in range(NHF):
                yv = emit_stats(Y[:, ci, tbi * TB + hf * 256:tbi * TB + (hf + 1) * 256],
                                tbi, hf * 256, 256, ci == 0 and hf == 0,
                                ci == DC - 1 and hf == NHF - 1)
        emit_stats_flush(tbi, yv)
        ystats.append(yv)

    for tbi in range(NTB):
        rows_ve = emit_ln_rows(tbi, ystats[tbi] if tbi == 0 else sVE.n)
        sl = slice(tbi * TB, (tbi + 1) * TB)
        for ci in range(DC):
            emit_ln_chunk(tbi, ci, rows_ve, Y[:, ci, sl], Z[:, ci, sl], GAM_ATTN, BET_ATTN)

    def emit_qkv(att, entry_dma_waits=()):
        wq = d["wqkv1"] if att == 0 else d["wqkv2"]
        OBQ = OB_QKV1 if att == 0 else OB_QKV2
        qs_last = [0, 0]
        po_free = [0, 0]
        ve_free = sVE.n
        mm_free = sMM.n

        def fg0(g):
            g.wait_ge(sVE.h, ve_free)
            g.wait_ge(sMM.h, mm_free)
            sGC.inc(g.memset(V.rearrange("p k (h w) -> p k h w", w=VW)[:, :, :, VW - 1:VW], 1.0))
        GP(fg0)
        gc_ones = sGC.n
        wsl_free = [sMM.n, sMM.n]
        z_ready = sVE.n
        for sl_i in range(NS):
            slot = sl_i % 2

            def fg(g, sl_i=sl_i, slot=slot, war=wsl_free[slot]):
                g.wait_ge(sMM.h, war)
                sW.issue(g, WSL[:, slot, :, :].rearrange("p c f -> p (c f)"),
                         wq[sl_i])
            GP(fg)
            w_at = sW.n
            for fi in range(2):
                fc = 2 * sl_i + fi
                for tbi in range(NTB):
                    pslot = (fc * NTB + tbi) % 2
                    po = pB[4 + pslot][0:128, 0:TB]
                    sc_war = po_free[pslot]

                    def ft(t, slot=slot, fi=fi, tbi=tbi, w_at=w_at, po=po, sc_war=sc_war):
                        t.wait_ge(sW.h, w_at)
                        t.wait_ge(sVE.h, max(z_ready, sc_war))
                        for sem_, val_ in entry_dma_waits:
                            t.wait_ge(sem_.h, val_)
                        for ci in range(DC):
                            ins = t.matmul(po, WSL[:, slot, ci, fi * 128:(fi + 1) * 128],
                                           Z[:, ci, tbi * TB:(tbi + 1) * TB],
                                           start=(ci == 0), stop=(ci == DC - 1))
                        sMM.inc(ins)
                    T(ft)
                    mm_at = sMM.n
                    bcol = biasp[:, OBQ + fc:OBQ + fc + 1]
                    sl2 = slice(tbi * TB, (tbi + 1) * TB)
                    if fc < DC:
                        qslot = (fc * NTB + tbi) % 2
                        qwar = qs_last[qslot]

                        def fv_(v, po=po, mm_at=mm_at, bcol=bcol, qslot=qslot, qwar=qwar,
                                fc=fc, tbi=tbi):
                            v.wait_ge(sMM.h, mm_at)
                            v.wait_ge(sQS.h, qwar)
                            v.wait_ge(sVE.h, sVE.n)
                            sVE.inc(v.tensor_scalar(QS[:, qslot, :], po, bcol, None, ALU.add))
                            v.wait_ge(sVE.h, sVE.n)
                            sVE.inc(v.tensor_mul(XR[:, 1, 0:TB], QS[:, qslot, :],
                                                 QS[:, qslot, :]))
                            v.wait_ge(sVE.h, sVE.n)
                            sVE.inc(v.reduce_max(qtmp[:, 0:1], XR[:, 1, 0:TB],
                                                 mybir.AxisListType.X))
                            v.wait_ge(sVE.h, sVE.n)
                            if tbi == 0:
                                sVE.inc(v.tensor_copy(qmaxcol[:, fc:fc + 1], qtmp[:, 0:1]))
                            else:
                                sVE.inc(v.tensor_max(qmaxcol[:, fc:fc + 1],
                                                     qmaxcol[:, fc:fc + 1], qtmp[:, 0:1]))
                        VEC(fv_)
                        ve_q = sVE.n

                        def fyq(sy, att=att, fc=fc, sl2=sl2, qslot=qslot, ve_q=ve_q):
                            sy.wait_ge(sVE.h, ve_q)
                            sQS.issue(sy, scr_q[att][fc, :, sl2], QS[:, qslot, :])
                        SY(fyq)
                        qs_last[qslot] = sQS.n
                        po_free[pslot] = sVE.n
                    elif fc < 2 * DC:
                        def fv_(v, po=po, mm_at=mm_at, bcol=bcol, fc=fc, sl2=sl2):
                            v.wait_ge(sMM.h, mm_at)
                            v.wait_ge(sVE.h, sVE.n)
                            sVE.inc(v.tensor_scalar(K[:, fc - DC, sl2], po, bcol, None, ALU.add))
                        VEC(fv_)
                        po_free[pslot] = sVE.n
                    else:
                        vslot = (fc * NTB + tbi) % 2

                        def fv_(v, po=po, mm_at=mm_at, bcol=bcol, vslot=vslot):
                            v.wait_ge(sMM.h, mm_at)
                            v.wait_ge(sVE.h, sVE.n)
                            sVE.inc(v.tensor_scalar(VSTb[:, vslot, 0:TB], po, bcol, None, ALU.add))
                        VEC(fv_)
                        sc_at = sVE.n
                        po_free[pslot] = sVE.n
                        vc = fc - 2 * DC
                        for ts in range(TB // 128):
                            tpos = (ts % 4) * 128
                            ve_war2 = sVE.n

                            def ft2(t, vslot=vslot, ts=ts, tpos=tpos, sc_at=sc_at, ve_war2=ve_war2):
                                t.wait_ge(sVE.h, max(sc_at, ve_war2))
                                sMM.inc(t.transpose(pTRb[:, tpos:tpos + 128],
                                                    VSTb[:, vslot, ts * 128:(ts + 1) * 128], identb))
                            T(ft2)
                            mm2 = sMM.n
                            kct = (tbi * TB) // 128 + ts
                            dst = V.rearrange("p k (h w) -> p k h w", w=VW)[:, kct, 2 * vc:2 * vc + 2, 0:64]

                            def fv(v, tpos=tpos, mm2=mm2, dst=dst):
                                v.wait_ge(sMM.h, mm2)
                                v.wait_ge(sVE.h, sVE.n)
                                src = pTRb[:, tpos:tpos + 128].rearrange("p (a f) -> p a f", a=2)
                                sVE.inc(v.tensor_copy(dst, src))
                            VEC(fv)
                    if fi == 1 and tbi == NTB - 1:
                        wsl_free[slot] = sMM.n
        return gc_ones

    def emit_bound(src, col):
        first = [True]
        for tbi in range(NTB):
            for ci in range(DC):
                mm_war = sMM.n

                def fv0(v, ci=ci, tbi=tbi, mm_war=mm_war):
                    v.wait_ge(sVE.h, sVE.n)
                    v.wait_ge(sMM.h, mm_war)
                    s_ = src[:, ci, tbi * TB:(tbi + 1) * TB]
                    sVE.inc(v.tensor_mul(XR[:, 1, 0:TB], s_, s_))
                VEC(fv0)
                sc_at2 = sVE.n

                ve_war2 = sVE.n

                def ft(t, ci=ci, sc_at2=sc_at2, ve_war2=ve_war2):
                    t.wait_ge(sVE.h, max(sc_at2, ve_war2 if ci == 0 else 0))
                    sMM.inc(t.matmul(pB[2][0:H, 0:TB], bdiag[:, ci, :], XR[:, 1, 0:TB],
                                     start=(ci == 0), stop=(ci == DC - 1)))
                T(ft)
            mm_at = sMM.n

            def fv(v, mm_at=mm_at, fr=first[0]):
                v.wait_ge(sMM.h, mm_at)
                v.wait_ge(sVE.h, sVE.n)
                sVE.inc(v.reduce_max(qkn[0:H, 4 + col:5 + col], pB[2][0:H, 0:TB], mybir.AxisListType.X))
                v.wait_ge(sVE.h, sVE.n)
                if fr:
                    sVE.inc(v.tensor_copy(qkn[0:H, col:col + 1], qkn[0:H, 4 + col:5 + col]))
                else:
                    sVE.inc(v.tensor_max(qkn[0:H, col:col + 1], qkn[0:H, col:col + 1],
                                         qkn[0:H, 4 + col:5 + col]))
            VEC(fv)
            first[0] = False
        return sVE.n

    def emit_qbound():
        ve_at = sVE.n

        def fsc(s, ve_at=ve_at):
            s.wait_ge(sVE.h, ve_at)
            s.wait_ge(sSC.h, sSC.n)
            sSC.inc(s.activation(qmaxb[:, :], qmaxcol[:, :], AF.Identity))
        SCL(fsc)
        sc_at = sSC.n
        mm_war = sMM.n

        def ftq(t, sc_at=sc_at, mm_war=mm_war):
            t.wait_ge(sSC.h, sc_at)
            t.wait_ge(sMM.h, mm_war)
            for hc in range(DC):
                ins = t.matmul(pB[2][0:H, 0:1], bdiag[:, hc, :], qmaxb[:, hc:hc + 1],
                               start=(hc == 0), stop=(hc == DC - 1))
            sMM.inc(ins)
        T(ftq)
        mm_at = sMM.n

        def fvq2(v, mm_at=mm_at):
            v.wait_ge(sMM.h, mm_at)
            v.wait_ge(sVE.h, sVE.n)
            sVE.inc(v.tensor_copy(qkn[0:H, 0:1], pB[2][0:H, 0:1]))
        VEC(fvq2)

    def emit_bias_from_bounds(att):
        def fv(v):
            v.wait_ge(sVE.h, sVE.n)
            sVE.inc(v.tensor_mul(qkn[0:H, 2:3], qkn[0:H, 0:1], qkn[0:H, 1:2]))
        VEC(fv)
        ve_at = sVE.n

        def fs(s):
            s.wait_ge(sVE.h, ve_at)
            sSC.inc(s.activation(qkn[0:H, 3:4], qkn[0:H, 2:3], AF.Sqrt))
        SCL(fs)
        sc_at = sSC.n

        def fv2(v):
            v.wait_ge(sSC.h, sc_at)
            sVE.inc(v.tensor_scalar(qkn[0:H, 3:4], qkn[0:H, 3:4], -0.1875, None, ALU.mult))
        VEC(fv2)
        ve2 = sVE.n

        def ft(t):
            t.wait_ge(sVE.h, ve2)
            sMM.inc(t.transpose(pB[2][0:1, 0:H], qkn[0:H, 3:4], identf[0:H, 0:H]))
        T(ft)
        mm1 = sMM.n

        def fv3(v):
            v.wait_ge(sMM.h, mm1)
            v.wait_ge(sVE.h, sVE.n)
            sVE.inc(v.tensor_copy(nbrow[0:1, :], pB[2][0:1, 0:H]))
        VEC(fv3)
        ve3 = sVE.n

        def ft2(t):
            t.wait_ge(sVE.h, ve3)
            sMM.inc(t.matmul(pB[2][0:128, 0:H], onesat64[0:1, :], nbrow[0:1, :],
                             start=True, stop=True))
        T(ft2)
        mm2 = sMM.n

        def fv4(v):
            v.wait_ge(sMM.h, mm2)
            v.wait_ge(sVE.h, sVE.n)
            sVE.inc(v.tensor_copy(bb[:, att * H:(att + 1) * H], pB[2][0:128, 0:H]))
        VEC(fv4)
        return sVE.n

    def emit_exchange(att, gc_ones):
        sd, rc = send[att], recv[att]
        sc_done = sSC.n
        ve_done = sVE.n

        def fy(sy):
            sy.wait_ge(sSC.h, sc_done)
            sy.wait_ge(sVE.h, ve_done)
            sy.wait_ge(sGC.h, gc_ones)
            sSY.issue(sy, sd[:, 0:DC * TL].rearrange("p (c t) -> p c t", c=DC),
                      K[:, :, 0:TL])
            sSY.issue(sy, sd[:, DC * TL:].rearrange("p (k w) -> p k w", k=2 * KCS),
                      V[:, 0:2 * KCS, :])
        SY(fy)
        sy_at = sSY.n

        def fg(g):
            g.wait_ge(sSY.h, sy_at)
            sCC.inc(g.collective_compute(
                "AllGather", ALU.bypass,
                replica_groups=[[2 * i, 2 * i + 1] for i in range(cfg["ncores"] // 2)],
                ins=[sd], outs=[rc]))
        GP(fg)
        cc_at = sCC.n

        def fy2(sy):
            sy.wait_ge(sCC.h, cc_at)
            for r in range(2):
                sSY.issue(sy, K[:, :, r * TL:(r + 1) * TL],
                          rc[r][:, 0:DC * TL].rearrange("p (c t) -> p c t", c=DC))
                sSY.issue(sy, V[:, r * 2 * KCS:(r + 1) * 2 * KCS, :],
                          rc[r][:, DC * TL:].rearrange("p (k w) -> p k w", k=2 * KCS))
        SY(fy2)
        return sSY.n

    def emit_attn(att, kv_ready_sy, bias_ve, wout_at, mask_at=0, qs_done=0):
        scr_in = scr_s if att == 0 else scr_y2
        scr_out = scr_x1 if att == 0 else scr_xb
        OBO = OB_OUT1 if att == 0 else OB_OUT2
        for qb in range(NQB):
            main, parts = attn_sched(cfg, att, qb)
            order = [kc for kc, m in main] + [kc for pl, fx in parts for kc in pl]
            mask_of = {pos: m for pos, (kc, m) in enumerate(main) if m is not None}
            U = len(order)
            nm = len(main)
            qsl = slice(qb * QB, (qb + 1) * QB)
            qslot_t = qb % 2
            war_mm = qt_free[qslot_t]

            def fql(sy, att=att, qsl=qsl, qslot_t=qslot_t, war_mm=war_mm, qs_done=qs_done):
                sy.wait_ge(sQS.h, qs_done)
                sy.wait_ge(sMM.h, war_mm)
                sQT.issue(sy, Qt[:, qslot_t],
                          scr_q[att][:, :, qsl].rearrange("c p t -> p c t"))
            SY(fql)
            qt_load = sQT.n
            for h in range(H):
                hp = (h % 2) * 64
                hc = h // 2
                qh = Qt[hp:hp + 64, qslot_t, hc, 0:QB]
                pAV = pB[2 + h % 2][0:65, 0:QB]
                ve_war0 = sVE.n
                grp_sc = []
                for r0 in range(0, U, PTN):
                    rend = min(r0 + PTN, U)
                    for g0 in range(r0, rend, 4):
                        grp = list(range(g0, min(g0 + 4, rend)))
                        war_sc = grp_sc[-1] if grp_sc else 0

                        def ft(t, grp=grp, war_sc=war_sc, h=h, hp=hp, hc=hc, qh=qh,
                               qt_load=qt_load):
                            t.wait_ge(sSC.h, war_sc)
                            if grp[0] == 0:
                                t.wait_ge(sSY.h, kv_ready_sy)
                                t.wait_ge(sVE.h, max(bias_ve, ve_war0))
                                if h == 0:
                                    t.wait_ge(sQT.h, qt_load)
                            for gi in grp:
                                kc = order[gi]
                                slot = gi % 4
                                ps = pB[slot // 2][0:128, (slot % 2) * QB:(slot % 2 + 1) * QB]
                                ins = t.matmul(ps, K[hp:hp + 64, hc, kc * 128:(kc + 1) * 128],
                                               qh, start=True, stop=True)
                            sMM.inc(ins)
                        T(ft)
                        mm_at = sMM.n
                        ve_mask = 0
                        if any(gi in mask_of for gi in grp):
                            def fvm(v, grp=grp, mm_at=mm_at):
                                v.wait_ge(sMM.h, mm_at)
                                v.wait_ge(sWC.h, mask_at)
                                v.wait_ge(sVE.h, sVE.n)
                                ins = None
                                for gi in grp:
                                    if gi in mask_of:
                                        slot = gi % 4
                                        ps = pB[slot // 2][0:128, (slot % 2) * QB:(slot % 2 + 1) * QB]
                                        ins = v.tensor_add(ps, ps, masks[:, mask_of[gi], :])
                                sVE.inc(ins)
                            VEC(fvm)
                            ve_mask = sVE.n

                        def fs(s, grp=grp, mm_at=mm_at, ve_mask=ve_mask, h=h, r0=r0):
                            s.wait_ge(sMM.h, mm_at)
                            if ve_mask:
                                s.wait_ge(sVE.h, ve_mask)
                            bcol = bb[:, att * H + h:att * H + h + 1]
                            i2 = 0
                            while i2 < len(grp):
                                gi = grp[i2]
                                slot = gi % 4
                                if i2 + 1 < len(grp) and slot % 2 == 0:
                                    ps = pB[slot // 2][0:128, 0:2 * QB]
                                    dst = PT[:, gi - r0:gi - r0 + 2, 0:QB].rearrange(
                                        "p j q -> p (j q)")
                                    ins = s.activation(dst, ps, AF.Exp, bias=bcol, scale=0.125)
                                    i2 += 2
                                else:
                                    ps = pB[slot // 2][0:128, (slot % 2) * QB:(slot % 2 + 1) * QB]
                                    ins = s.activation(PT[:, gi - r0, 0:QB], ps, AF.Exp,
                                                       bias=bcol, scale=0.125)
                                    i2 += 1
                            sSC.inc(ins)
                        SCL(fs)
                        grp_sc.append(sSC.n)
                    exp_done = sSC.n
                    mainpos = [i2 for i2 in range(r0, rend) if i2 < nm]
                    partpos = [i2 for i2 in range(r0, rend) if i2 >= nm]
                    if mainpos:
                        def ftav(t, h=h, pAV=pAV, exp_done=exp_done, mainpos=mainpos, r0=r0):
                            t.wait_ge(sSC.h, exp_done)
                            if mainpos[0] == 0:
                                t.wait_ge(sVE.h, ve_war0)
                            ins = None
                            for i2 in mainpos:
                                kc = order[i2]
                                ins = t.matmul(pAV, V[:, kc, 65 * h:65 * h + 65], PT[:, i2 - r0, 0:QB],
                                               start=(i2 == 0), stop=(i2 == nm - 1))
                            sMM.inc(ins)
                        T(ftav)
                    if partpos:
                        def ftp(t, h=h, exp_done=exp_done, partpos=partpos, r0=r0):
                            t.wait_ge(sSC.h, exp_done)
                            if partpos[0] == nm:
                                t.wait_ge(sVE.h, ve_war0)
                            ins = None
                            for i2 in partpos:
                                kc = order[i2]
                                ins = t.matmul(pB[4][0:65, 0:QB], V[:, kc, 65 * h:65 * h + 65],
                                               PT[:, i2 - r0, 0:QB],
                                               start=(i2 == nm), stop=(i2 == U - 1))
                            sMM.inc(ins)
                        T(ftp)
                if U > nm:
                    mmp = sMM.n
                    fx = parts[0][1]

                    def fvp(v, mmp=mmp, fx=fx, pAV=pAV):
                        v.wait_ge(sMM.h, mmp)
                        v.wait_ge(sVE.h, sVE.n)
                        sVE.inc(v.tensor_scalar(lrow65[:, 0:QB], pB[4][0:65, 0:QB],
                                                fcol[0:65, fx:fx + 1], None, ALU.mult))
                        v.wait_ge(sVE.h, sVE.n)
                        sVE.inc(v.tensor_add(pAV, pAV, lrow65[:, 0:QB]))
                    VEC(fvp)
                merged_ve = sVE.n
                av_mm = sMM.n

                def fvr(v, pAV=pAV, merged_ve=merged_ve, av_mm=av_mm):
                    v.wait_ge(sMM.h, av_mm)
                    v.wait_ge(sVE.h, max(merged_ve, sVE.n))
                    sVE.inc(v.tensor_copy(lrow65[64:65, :], pAV[64:65, :]))
                    v.wait_ge(sVE.h, sVE.n)
                    sVE.inc(v.reciprocal(lrow65[64:65, :], lrow65[64:65, :]))
                VEC(fvr)
                ve_r = sVE.n

                def ftl(t, ve_r=ve_r):
                    t.wait_ge(sVE.h, ve_r)
                    sMM.inc(t.matmul(pB[4][0:128, 256:256 + QB], onesat64[64:65, :],
                                     lrow65[64:65, :], start=True, stop=True))
                T(ftl)
                mm_l = sMM.n
                ve_bc = sVE.n

                def fvo(v, pAV=pAV, ve_bc=ve_bc):
                    v.wait_ge(sVE.h, max(ve_bc, sVE.n))
                    sVE.inc(v.tensor_copy(aotmp[0:64, :], pAV[0:64, :]))
                VEC(fvo)
                ve_o = sVE.n
                if h % 2 == 0:
                    def fvd(v, mm_l=mm_l, ve_o=ve_o, hc=hc):
                        v.wait_ge(sMM.h, mm_l)
                        v.wait_ge(sVE.h, max(ve_o, sVE.n))
                        sVE.inc(v.tensor_tensor(AO128[0:64, hc, :], aotmp[0:64, :],
                                                pB[4][0:64, 256:256 + QB], ALU.mult))
                    VEC(fvd)
                else:
                    def fvd(v, mm_l=mm_l, ve_o=ve_o):
                        v.wait_ge(sMM.h, mm_l)
                        v.wait_ge(sVE.h, max(ve_o, sVE.n))
                        sVE.inc(v.tensor_tensor(aotmp[0:64, :], aotmp[0:64, :],
                                                pB[4][0:64, 256:256 + QB], ALU.mult))
                    VEC(fvd)
                    ve_d = sVE.n

                    def ftsh(t, ve_d=ve_d):
                        t.wait_ge(sVE.h, ve_d)
                        sMM.inc(t.matmul(pB[5][0:128, 0:QB], shifteye, aotmp, start=True, stop=True))
                    T(ftsh)
                    mm_sh = sMM.n

                    def fvc(v, mm_sh=mm_sh, hc=hc):
                        v.wait_ge(sMM.h, mm_sh)
                        v.wait_ge(sVE.h, sVE.n)
                        sVE.inc(v.tensor_copy(AO128[64:128, hc, :], pB[5][64:128, 0:QB]))
                    VEC(fvc)
            heads_ve = sVE.n
            qt_free[qslot_t] = sMM.n
            tbi = (qb * QB) // TB
            colofs = (qb * QB) % TB
            for fc in range(DC):
                slot = fc % 2
                ve_war = sVE.n

                def fy(sy, fc=fc, slot=slot, ve_war=ve_war):
                    sy.wait_ge(sVE.h, ve_war)
                    sT[slot].issue(sy, STG[:, slot, 0:QB],
                                   scr_in[fc * 128:(fc + 1) * 128, qsl])
                SY(fy)
                sy_in = sT[slot].n
                po = pB[5][0:128, (fc % 2) * QB:(fc % 2 + 1) * QB]
                ve_war2 = sVE.n

                def ft(t, fc=fc, po=po, heads_ve=heads_ve, ve_war2=ve_war2):
                    t.wait_ge(sVE.h, max(heads_ve, ve_war2))
                    t.wait_ge(sWC.h, wout_at)
                    for ci in range(DC):
                        ins = t.matmul(po, WOUT[:, ci, fc * 128:(fc + 1) * 128],
                                       AO128[:, ci, :], start=(ci == 0), stop=(ci == DC - 1))
                    sMM.inc(ins)
                T(ft)
                mm_at = sMM.n
                bcol = biasp[:, OBO + fc:OBO + fc + 1]

                def fv(v, fc=fc, slot=slot, po=po, mm_at=mm_at, sy_in=sy_in, bcol=bcol):
                    v.wait_ge(sMM.h, mm_at)
                    v.wait_ge(sT[slot].h, sy_in)
                    v.wait_ge(sVE.h, sVE.n)
                    sVE.inc(v.scalar_tensor_tensor(STG[:, slot, 0:QB], po, bcol,
                                                   STG[:, slot, 0:QB], ALU.add, ALU.add))
                VEC(fv)
                ve_evac = sVE.n
                st_mm = emit_stats(STG[:, slot, 0:QB], tbi, colofs, QB,
                                   fc == 0 and (qb * QB) % TB == 0,
                                   fc == DC - 1 and (qb * QB + QB) % TB == 0)

                def fy2(sy, fc=fc, slot=slot, ve_evac=ve_evac):
                    sy.wait_ge(sVE.h, ve_evac)
                    sT[slot].issue(sy, scr_out[fc * 128:(fc + 1) * 128, qsl],
                                   STG[:, slot, 0:QB])
                SY(fy2)
            if (qb * QB + QB) % TB == 0:
                emit_stats_flush(tbi, st_mm)

    def emit_y2():
        yv = 0
        for tbi in range(NTB):
            rows_ve = emit_ln_rows(tbi, sVE.n)
            sl = slice(tbi * TB, (tbi + 1) * TB)
            for ci in range(DC):
                ve_war = sVE.n

                def fy(sy, ci=ci, sl=sl, ve_war=ve_war):
                    sy.wait_ge(sVE.h, ve_war)
                    sT[ci % 2].issue(sy, Y2[:, ci, sl],
                                     scr_x1[ci * 128:(ci + 1) * 128, sl])
                SY(fy)
                sy_at = sT[ci % 2].n
                emit_ln_chunk(tbi, ci, rows_ve, Y2[:, ci, sl], Y2[:, ci, sl],
                              GAM_PLAIN, BET_PLAIN, wait_sy=(sT[ci % 2], sy_at))
                ve_n = sVE.n

                def fy2(sy, ci=ci, sl=sl, ve_n=ve_n):
                    sy.wait_ge(sVE.h, ve_n)
                    sT[2].issue(sy, scr_y2[ci * 128:(ci + 1) * 128, sl],
                                Y2[:, ci, sl])
                SY(fy2)
                for hf in range(NHF):
                    yv = emit_stats(Y2[:, ci, tbi * TB + hf * 256:tbi * TB + (hf + 1) * 256],
                                    tbi, hf * 256, 256, ci == 0 and hf == 0,
                                    ci == DC - 1 and hf == NHF - 1)
            emit_stats_flush(tbi, yv)
        return yv

    def emit_step9():
        for tbi in range(NTB):
            rows_ve = emit_ln_rows(tbi, sVE.n)
            for half in range(TB // QB):
                colofs = half * QB
                qsl = slice(tbi * TB + colofs, tbi * TB + colofs + QB)
                ts_slots = []
                for ci in range(DC):
                    ve_war = sVE.n

                    def fy(sy, ci=ci, qsl=qsl, ve_war=ve_war):
                        sy.wait_ge(sVE.h, ve_war)
                        sT[0].issue(sy, STG[:, 0, 0:QB],
                                    scr_xb[ci * 128:(ci + 1) * 128, qsl])
                        sT[1].issue(sy, STG[:, 1, 0:QB],
                                    scr_x1[ci * 128:(ci + 1) * 128, qsl])
                    SY(fy)
                    sy_at = (sT[0].n, sT[1].n)
                    slot = ci % 2
                    mm_war = sMM.n

                    syl = emit_lnbuf(slot, GAM_LN2, ci, mm_war)
                    pR = pB[slot][0:128, 0:QB]
                    pA = pB[2 + slot][0:128, 0:QB]
                    ve_war2 = sVE.n

                    def ftb(t, slot=slot, syl=syl, rows_ve=rows_ve, colofs=colofs,
                            pR=pR, pA=pA, ve_war2=ve_war2):
                        t.wait_ge(sSC.h, syl)
                        t.wait_ge(sVE.h, max(rows_ve, ve_war2))
                        t.matmul(pR, lnbuf[0:1, slot, :], mrowb[0:1, 0, colofs:colofs + QB],
                                 start=True, stop=True)
                        sMM.inc(t.matmul(pA, lnbuf[0:1, slot, :], mrowb[0:1, 1, colofs:colofs + QB],
                                         start=True, stop=True))
                    T(ftb)
                    mmb = sMM.n
                    bcol = biasp[:, OB_LNB + BET_LN2 + ci:OB_LNB + BET_LN2 + ci + 1]

                    def fvn(v, ci=ci, mmb=mmb, bcol=bcol, qsl=qsl, pR=pR, pA=pA, sy_at=sy_at):
                        v.wait_ge(sMM.h, mmb)
                        v.wait_ge(sT[0].h, sy_at[0])
                        v.wait_ge(sT[1].h, sy_at[1])
                        v.wait_ge(sVE.h, sVE.n)
                        sVE.inc(v.tensor_mul(STG[:, 2, 0:QB], STG[:, 0, 0:QB], pR))
                        v.wait_ge(sVE.h, sVE.n)
                        sVE.inc(v.scalar_tensor_tensor(Z[:, ci, qsl], STG[:, 2, 0:QB], bcol, pA,
                                                       ALU.add, ALU.add))
                        v.wait_ge(sVE.h, sVE.n)
                        sVE.inc(v.tensor_add(STG[:, 1, 0:QB], STG[:, 1, 0:QB], STG[:, 0, 0:QB]))
                    VEC(fvn)
                    ve_s2 = sVE.n
                    for ts in range(QB // 128):
                        tt = (tbi * TB + colofs) // 128 + ts
                        oslot = tt % 2
                        ocol = ci * 128
                        if ci == 0 and ts == 0:
                            ts_slots.append(None)
                        ve_war3 = sVE.n

                        def ftt(t, ts=ts, ve_s2=ve_s2, ve_war3=ve_war3):
                            t.wait_ge(sVE.h, max(ve_s2, ve_war3))
                            sMM.inc(t.transpose(pB[4][0:128, ts * 128:(ts + 1) * 128],
                                                STG[:, 1, ts * 128:(ts + 1) * 128], identf))
                        T(ftt)
                        mmt = sMM.n
                        sy_war = sSY.n

                        def fvt(v, ts=ts, oslot=oslot, ocol=ocol, mmt=mmt, sy_war=sy_war):
                            v.wait_ge(sMM.h, mmt)
                            v.wait_ge(sSY.h, sy_war)
                            v.wait_ge(sVE.h, sVE.n)
                            sVE.inc(v.tensor_copy(OSTG[:, oslot, ocol:ocol + 128],
                                                  pB[4][0:128, ts * 128:(ts + 1) * 128]))
                        VEC(fvt)
                ve_done = sVE.n
                for ts in range(QB // 128):
                    tt = (tbi * TB + colofs) // 128 + ts

                    def fyo(sy, tt=tt, ve_done=ve_done):
                        sy.wait_ge(sVE.h, ve_done)
                        sSY.issue(sy, out_d[tt], OSTG[:, tt % 2, :])
                    SY(fyo)
        s2_done = sSY.n

        for tt in range(TL // 128):
            ve_war = sVE.n

            def ftg(t, tt=tt, ve_war=ve_war):
                t.wait_ge(sVE.h, ve_war)
                t.wait_ge(sSC.h, GATEWB_AT)
                for ci in range(DC):
                    ins = t.matmul(pB[4][0:128, 0:NE], Z[:, ci, tt * 128:(tt + 1) * 128],
                                   gatewb[:, ci, :], start=(ci == 0), stop=(ci == DC - 1))
                sMM.inc(ins)
            T(ftg)
            mmg = sMM.n
            g0 = gts[:, tt, :]

            def fvg(v, mmg=mmg, g0=g0):
                v.wait_ge(sMM.h, mmg)
                v.wait_ge(sVE.h, sVE.n)
                sVE.inc(v.reduce_max(SQ[:, 0:1], pB[4][0:128, 0:NE], mybir.AxisListType.X))
                v.wait_ge(sVE.h, sVE.n)
                sVE.inc(v.tensor_scalar(STG[:, 0, 0:NE], pB[4][0:128, 0:NE], SQ[:, 0:1], None, ALU.is_ge))
                v.wait_ge(sVE.h, sVE.n)
                sVE.inc(v.tensor_scalar(STG[:, 1, 0:NE], STG[:, 0, 0:NE], -1.0e30, None, ALU.mult))
                v.wait_ge(sVE.h, sVE.n)
                sVE.inc(v.tensor_add(STG[:, 1, 0:NE], STG[:, 1, 0:NE], pB[4][0:128, 0:NE]))
                v.wait_ge(sVE.h, sVE.n)
                sVE.inc(v.reduce_max(SQ[:, 1:2], STG[:, 1, 0:NE], mybir.AxisListType.X))
                v.wait_ge(sVE.h, sVE.n)
                sVE.inc(v.tensor_sub(SQ[:, 2:3], SQ[:, 0:1], SQ[:, 1:2]))
            VEC(fvg)
            ve_d = sVE.n

            def fsg(s, ve_d=ve_d):
                s.wait_ge(sVE.h, ve_d)
                s.activation(SQ[:, 3:4], SQ[:, 2:3], AF.Sigmoid)
                s.wait_ge(sSC.h, sSC.n)
                sSC.inc(s.activation(SQ[:, 4:5], SQ[:, 2:3], AF.Sigmoid, scale=-1.0))
            SCL(fsg)
            sc_g = sSC.n

            def fvg5(v, g0=g0, sc_g=sc_g):
                v.wait_ge(sSC.h, sc_g)
                sVE.inc(v.tensor_scalar(STG[:, 2, 0:NE], STG[:, 1, 0:NE], SQ[:, 1:2], None, ALU.is_ge))
                v.wait_ge(sVE.h, sVE.n)
                sVE.inc(v.tensor_scalar(STG[:, 0, 0:NE], STG[:, 0, 0:NE], SQ[:, 3:4], None, ALU.mult))
                v.wait_ge(sVE.h, sVE.n)
                sVE.inc(v.scalar_tensor_tensor(g0, STG[:, 2, 0:NE], SQ[:, 4:5], STG[:, 0, 0:NE],
                                               ALU.mult, ALU.add))
            VEC(fvg5)
            ve_g = sVE.n

            def ftt(t, g0=g0, ve_g=ve_g):
                t.wait_ge(sVE.h, ve_g)
                sMM.inc(t.transpose(pB[2][0:NE, 0:128], g0, identf))
            T(ftt)
            mmt = sMM.n

            def fvt(v, tt=tt, mmt=mmt):
                v.wait_ge(sMM.h, mmt)
                v.wait_ge(sVE.h, sVE.n)
                sVE.inc(v.tensor_copy(gerow[0:NE, tt * 128:(tt + 1) * 128], pB[2][0:NE, 0:128]))
            VEC(fvt)

        if dbg_moe:
            gv_ = sVE.n

            def fdbg2(sy):
                sy.wait_ge(sVE.h, gv_)
                sSY.issue(sy, dbg_gts2, gts.rearrange("p t e -> p (t e)"))
            SY(fdbg2)
        def fyf(sy):
            sy.wait_ge(sMM.h, sMM.n)
            sSY.issue(sy, fc2bs32, d["fc2b"])
        SY(fyf)
        f32_at = sSY.n

        def fyc(s):
            s.wait_ge(sSY.h, f32_at)
            sSC.inc(s.activation(fc2bs[0:NE, :], fc2bs32[0:NE, :], AF.Identity))
        SCL(fyc)
        fc2b_at = sSC.n
        for tt in range(TL // 128):
            ve_war = sVE.n

            def ftb(t, tt=tt, ve_war=ve_war):
                t.wait_ge(sVE.h, ve_war)
                t.wait_ge(sSC.h, fc2b_at)
                for dh in range(NDH):
                    ins = t.matmul(pB[5 - dh][0:128, 0:DH],
                                   gerow[0:NE, tt * 128:(tt + 1) * 128],
                                   fc2bs[0:NE, dh * DH:(dh + 1) * DH], start=True, stop=True)
                sMM.inc(ins)
            T(ftb)
            mmb = sMM.n
            oslot = tt % 2
            os_war = os_last[oslot]

            def fvb(v, mmb=mmb, oslot=oslot, os_war=os_war, s2_done=s2_done):
                v.wait_ge(sMM.h, mmb)
                v.wait_ge(sSY.h, s2_done)
                v.wait_ge(sOS.h, os_war)
                for dh in range(NDH):
                    v.wait_ge(sVE.h, sVE.n)
                    sVE.inc(v.tensor_copy(OSTG[:, oslot, dh * DH:(dh + 1) * DH],
                                          pB[5 - dh][0:128, 0:DH]))
            VEC(fvb)
            veb = sVE.n

            def fgb(g, tt=tt, oslot=oslot, veb=veb, s2_done=s2_done):
                g.wait_ge(sVE.h, veb)
                g.wait_ge(sSY.h, s2_done)
                sOS.issue(g, out_d[tt], OSTG[:, oslot, :], accum_op=ALU.add)
            GP(fgb)
            os_last[oslot] = sOS.n
        return s2_done

    def emit_moe(s2_done):
        fc1q_mm = [sMM.n] * 4
        fc2q_mm = [sMM.n] * 4
        for e in range(NE):
            w1_at, w2_at = [], []
            for q in range(4):
                def fg(g, e=e, q=q, war=fc1q_mm[q]):
                    g.wait_ge(sMM.h, war)
                    sW.issue(g, FC1[:, q].rearrange("p c z -> p (c z)"),
                             d["fc1"][e, q])
                GP(fg)
                w1_at.append(sW.n)
            for q in range(4):
                def fg2(g, e=e, q=q, war=fc2q_mm[q]):
                    g.wait_ge(sMM.h, war)
                    sW.issue(g, FC2[:, q * (FFC // 4):(q + 1) * (FFC // 4), :].rearrange("p k z -> p (k z)"),
                             d["fc2"][e, q])
                GP(fg2)
                w2_at.append(sW.n)
            for tb in range(NMTB):
                msl = slice(tb * MTB, (tb + 1) * MTB)
                ntt = MTB // 128
                FH = FFC // 2
                for half in range(2):
                    for Fc2 in range(FH):
                        Fc = half * FH + Fc2
                        po = pB[4 + Fc % 2][0:128, 0:MTB]
                        sc_war = sSC.n
                        ve_war = sVE.n

                        def ft(t, e=e, tb=tb, Fc=Fc, po=po, sc_war=sc_war, ve_war=ve_war, msl=msl):
                            t.wait_ge(sW.h, w1_at[Fc * 4 // FFC])
                            t.wait_ge(sSC.h, max(0, sc_war - 1))
                            if tb == 0 and Fc <= 1:
                                t.wait_ge(sVE.h, ve_war)
                            FQ = FFC // 4
                            for ci in range(DC):
                                ins = t.matmul(po, FC1[:, Fc // FQ, ci, (Fc % FQ) * 128:(Fc % FQ + 1) * 128],
                                               Z[:, ci, msl], start=(ci == 0), stop=(ci == DC - 1))
                            sMM.inc(ins)
                        T(ft)
                        if tb == NMTB - 1:
                            fc1q_mm[Fc * 4 // FFC] = sMM.n
                        mm_at = sMM.n
                        b1 = biasp[:, OB_FC1 + e * FFC + Fc:OB_FC1 + e * FFC + Fc + 1]

                        def fvb1(v, po=po, mm_at=mm_at, b1=b1):
                            v.wait_ge(sMM.h, mm_at)
                            v.wait_ge(sVE.h, sVE.n)
                            sVE.inc(v.tensor_scalar(po, po, b1, None, ALU.add))
                        VEC(fvb1)
                        ve_b = sVE.n

                        def fs(s, half=half, Fc2=Fc2, po=po, ve_b=ve_b):
                            s.wait_ge(sVE.h, ve_b)
                            sSC.inc(s.activation(h2[:, half, Fc2, :], po, AF.Gelu))
                        SCL(fs)
                    h_done = sSC.n
                    if dbg_moe and e == 0 and tb == 0 and half == 1:
                        def fdbg(sy, h_done=h_done):
                            sy.wait_ge(sSC.h, h_done)
                            sy.wait_ge(sW.h, sW.n)
                            sSY.issue(sy, dbg_nx2, Z.rearrange("p c t -> p (c t)"))
                            sSY.issue(sy, dbg_gts, gts.rearrange("p t e -> p (t e)"))
                            sSY.issue(sy, dbg_h, h2.rearrange("p s f t -> p (s f t)"))
                            sSY.issue(sy, dbg_fc1, FC1.rearrange("p q c f -> p (q c f)"))
                            sSY.issue(sy, dbg_fc2, FC2.rearrange("p k z -> p (k z)"))
                            sSY.issue(sy, dbg_sw0, smallswin)
                        SY(fdbg)
                    if dbg_moe and e == 3 and tb == 0 and half == 1:
                        def fdbg3(sy, h_done=h_done):
                            sy.wait_ge(sSC.h, h_done)
                            sSY.issue(sy, dbg_sw1, smallswin)
                        SY(fdbg3)
                    ve_war2 = sVE.n
                    for Fc2 in range(FH):
                        Fc = half * FH + Fc2

                        def ftb(t, tb=tb, half=half, Fc=Fc, Fc2=Fc2, h_done=h_done,
                                ntt=ntt, ve_war2=ve_war2):
                            t.wait_ge(sSC.h, h_done)
                            t.wait_ge(sW.h, w2_at[Fc * 4 // FFC])
                            if Fc == 0:
                                t.wait_ge(sVE.h, ve_war2)
                            ins = None
                            for tt in range(ntt):
                                for dh in range(NDH):
                                    ins = t.matmul(pB[tt * NDH + dh][0:128, 0:DH],
                                                   h2[:, half, Fc2, tt * 128:(tt + 1) * 128],
                                                   FC2[:, Fc, dh * DH:(dh + 1) * DH],
                                                   start=(Fc == 0), stop=(Fc == FFC - 1))
                            sMM.inc(ins)
                        T(ftb)
                        if tb == NMTB - 1:
                            fc2q_mm[Fc * 4 // FFC] = sMM.n
                mmB = sMM.n
                for tt in range(ntt):
                    ti = tb * ntt + tt
                    gcol = gts[:, ti, e:e + 1]
                    oslot = ti % 2
                    os_war = os_last[oslot]

                    def fv(v, tt=tt, gcol=gcol, oslot=oslot, mmB=mmB, os_war=os_war, s2_done=s2_done):
                        v.wait_ge(sMM.h, mmB)
                        v.wait_ge(sSY.h, s2_done)
                        v.wait_ge(sOS.h, os_war)
                        v.wait_ge(sVE.h, sVE.n)
                        for dh in range(NDH):
                            sVE.inc(v.tensor_scalar(OSTG[:, oslot, dh * DH:(dh + 1) * DH],
                                                    pB[tt * NDH + dh][0:128, 0:DH],
                                                    gcol, None, ALU.mult))
                            if dh + 1 < NDH:
                                v.wait_ge(sVE.h, sVE.n)
                    VEC(fv)
                    vev = sVE.n

                    def fg3(g, ti=ti, oslot=oslot, vev=vev, s2_done=s2_done):
                        g.wait_ge(sVE.h, vev)
                        g.wait_ge(sSY.h, s2_done)
                        sOS.issue(g, out_d[ti], OSTG[:, oslot, :], accum_op=ALU.add)
                    GP(fg3)
                    os_last[oslot] = sOS.n

    def _finish():
        os_total = sOS.n
        sy_total = sSY.n

        def ffinX(sy):
            sy.wait_ge(sOS.h, os_total)
            sy.wait_ge(sSY.h, sy_total)
            sy.wait_ge(sQS.h, sQS.n)
            sy.wait_ge(sQT.h, sQT.n)
            sy.wait_ge(sW.h, sW.n)
            sy.wait_ge(sWC.h, sWC.n)
            sy.wait_ge(sCC.h, sCC.n)
            sy.wait_ge(sRW.h, sRW.n)
            sy.wait_ge(sLN.h, sLN.n)
            for s_ in sT:
                sy.wait_ge(s_.h, s_.n)
        SY(ffinX)
        b.out_names = ["o"]
        return nc, b

    if stop_after == "ln":
        return _finish()
    gc1 = emit_qkv(0)
    qs1_done = sQS.n
    emit_qbound()
    emit_bound(K[:, :, 0:TL], 1)
    bias1_ve = emit_bias_from_bounds(0)
    kv1_sy = emit_exchange(0, gc1)
    sc_w1 = sSC.n

    def fw1(g):
        g.wait_ge(sSC.h, sc_w1)
        sWC.issue(g, WOUT.rearrange("p c f -> p (c f)")[0:64], d["wout1"][0:64])
        sWC.issue(g, WOUT.rearrange("p c f -> p (c f)")[64:128], d["wout1"][64:128])
    GP(fw1)
    if stop_after == "qkv1":
        return _finish()
    emit_attn(0, kv1_sy, bias1_ve, sWC.n, qs_done=qs1_done)

    if stop_after == "attn1":
        return _finish()
    y2_ve = emit_y2()
    if stop_after == "y2a":
        return _finish()
    if False:
        _vals = [(sSY, sSY.n), (sW, sW.n), (sWC, sWC.n), (sRW, sRW.n), (sLN, sLN.n),
                 (sMM, sMM.n), (sSC, sSC.n), (sVE, sVE.n)] + [(s_, s_.n) for s_ in sT]

        def _stall(e):
            for s_, v_ in _vals:
                e.wait_ge(s_.h, v_)
        T(_stall); VEC(_stall); SCL(_stall); GP(_stall); SY(_stall)
    if stop_after == "y2c":
        for tbi in range(NTB):
            sl = slice(tbi * TB, (tbi + 1) * TB)
            for ci in range(DC):
                def fvz(v, ci=ci, sl=sl):
                    v.wait_ge(sVE.h, sVE.n)
                    sVE.inc(v.tensor_copy(Z[:, ci, sl], Y2[:, ci, sl]))
                VEC(fvz)
        return _finish()
    if stop_after == "y2d":
        for tbi in range(NTB):
            emit_ln_rows(tbi, sVE.n)
        return _finish()
    for tbi in range(NTB):
        rows_ve = emit_ln_rows(tbi, y2_ve if tbi == 0 else sVE.n)
        sl = slice(tbi * TB, (tbi + 1) * TB)
        for ci in range(DC):
            emit_ln_chunk(tbi, ci, rows_ve, Y2[:, ci, sl], Z[:, ci, sl], GAM_LN1, BET_LN1)
        if stop_after == "y2b" and tbi == 0:
            return _finish()

    if stop_after == "y2":
        return _finish()
    gc2 = emit_qkv(1, entry_dma_waits=[(s_, s_.n) for s_ in sT])
    qs2_done = sQS.n
    emit_qbound()
    emit_bound(K[:, :, 0:TL], 1)
    bias2_ve = emit_bias_from_bounds(1)
    kv2_sy = emit_exchange(1, gc2)
    sc_w2 = sSC.n

    def fw2(g):
        g.wait_ge(sSC.h, sc_w2)
        sWC.issue(g, WOUT.rearrange("p c f -> p (c f)")[0:64], d["wout2"][0:64])
        sWC.issue(g, WOUT.rearrange("p c f -> p (c f)")[64:128], d["wout2"][64:128])
        sWC.issue(g, masks.rearrange("p j q -> p (j q)")[0:64], d["masks"][0:64])
        sWC.issue(g, masks.rearrange("p j q -> p (j q)")[64:128], d["masks"][64:128])
    GP(fw2)
    if stop_after == "qkv2":
        return _finish()
    emit_attn(1, kv2_sy, bias2_ve, sWC.n, mask_at=sWC.n, qs_done=qs2_done)

    if stop_after == "attn2":
        return _finish()
    s2_done = emit_step9()
    if stop_after == "step9":
        return _finish()
    emit_moe(s2_done)

    os_total = sOS.n
    sy_total = sSY.n

    def ffin(sy):
        sy.wait_ge(sOS.h, os_total)
        sy.wait_ge(sSY.h, sy_total)
        sy.wait_ge(sQS.h, sQS.n)
        sy.wait_ge(sQT.h, sQT.n)
    SY(ffin)
    b.out_names = ["o"]
    return nc, b


def _prep_weights(cfg, inputs):
    c = cfg
    D, DC, FF, FFC, NE = c["D"], c["DC"], c["FF"], c["FFC"], c["NE"]
    FC3 = c["FC3"]

    def slices_layout(w):
        F = w.shape[1]
        nsl = F // 256
        r = w.reshape(DC, 128, nsl, 256).transpose(2, 1, 0, 3)
        return np.ascontiguousarray(r.reshape(nsl, 128, DC * 256))

    def wout_layout(w):
        r = w.reshape(DC, 128, D).transpose(1, 0, 2)
        return np.ascontiguousarray(r.reshape(128, DC * D))

    out = {}
    out["wqkv1"] = slices_layout(np.asarray(inputs["L0_attn_w_qkv"], np.float32))
    out["wqkv2"] = slices_layout(np.asarray(inputs["L0_ffn_Wqkv"], np.float32))
    out["wout1"] = wout_layout(np.asarray(inputs["L0_attn_w_out"], np.float32))
    out["wout2"] = wout_layout(np.asarray(inputs["L0_ffn_Wo"], np.float32))
    fc1 = np.asarray(inputs["L0_ffn_fc1_w"], np.float32)
    fc2 = np.asarray(inputs["L0_ffn_fc2_w"], np.float32)
    a = fc1.transpose(0, 2, 1).reshape(NE, DC, 128, FF).transpose(0, 2, 1, 3)
    a = a.reshape(NE, 128, DC, 4, FF // 4).transpose(0, 3, 1, 2, 4)
    out["fc1"] = np.ascontiguousarray(a.reshape(NE, 4, 128, DC * FF // 4))
    bq = fc2.transpose(0, 2, 1).reshape(NE, FFC, 128, D).transpose(0, 2, 1, 3)
    bq = bq.reshape(NE, 128, 4, FFC // 4, D).transpose(0, 2, 1, 3, 4)
    out["fc2"] = np.ascontiguousarray(bq.reshape(NE, 4, 128, FFC * D // 4))

    NBC = 2 * FC3 + 2 * DC + NE * FFC + 4 * DC
    bp = np.zeros((128, NBC), np.float32)

    def cols(vec):
        return np.asarray(vec, np.float32).reshape(-1, 128).T

    o = 0
    bp[:, o:o + FC3] = cols(inputs["L0_attn_b_qkv"]); o += FC3
    bp[:, o:o + DC] = cols(inputs["L0_attn_b_out"]); o += DC
    bp[:, o:o + FC3] = cols(inputs["L0_ffn_bqkv"]); o += FC3
    bp[:, o:o + DC] = cols(inputs["L0_ffn_bo"]); o += DC
    f1b = np.asarray(inputs["L0_ffn_fc1_b"], np.float32)
    bp[:, o:o + NE * FFC] = np.concatenate([cols(f1b[e]) for e in range(NE)], axis=1)
    o += NE * FFC
    o += DC
    bp[:, o:o + DC] = cols(inputs["L0_attn_ln_beta"]); o += DC
    bp[:, o:o + DC] = cols(inputs["L0_ffn_ln1_b"]); o += DC
    bp[:, o:o + DC] = cols(inputs["L0_ffn_ln2_b"]); o += DC
    out["biasp"] = bp

    lnr = np.zeros((1, 4 * D), np.float32)
    lnr[0, 0:D] = 1.0
    lnr[0, D:2 * D] = np.asarray(inputs["L0_attn_ln_gamma"], np.float32)
    lnr[0, 2 * D:3 * D] = np.asarray(inputs["L0_ffn_ln1_w"], np.float32)
    lnr[0, 3 * D:4 * D] = np.asarray(inputs["L0_ffn_ln2_w"], np.float32)
    out["lnrows"] = lnr
    out["fc2b"] = np.ascontiguousarray(np.asarray(inputs["L0_ffn_fc2_b"], np.float32))
    gw = np.asarray(inputs["L0_ffn_gate_W"], np.float32)
    out["gatew"] = np.ascontiguousarray(gw.reshape(DC, 128, NE).transpose(1, 0, 2).reshape(128, DC * NE))

    out["identb"] = np.eye(128, dtype=ml_dtypes.bfloat16)
    out["identf"] = np.eye(128, dtype=np.float32)
    she = np.zeros((64, 128), ml_dtypes.bfloat16)
    for i2 in range(64):
        she[i2, 64 + i2] = 1
    out["shifteye"] = she
    out["onescol"] = np.ones((128, 1), ml_dtypes.bfloat16)
    out["onesat64"] = np.ones((65, 128), np.float32)
    DCH = cfg["DC"]
    Hh = cfg["H"]
    bd = np.zeros((128, DCH * Hh), np.float32)
    for ci in range(DCH):
        for j in range(2):
            h = 2 * ci + j
            if h < Hh:
                bd[j * 64:(j + 1) * 64, ci * Hh + h] = 1.0
    out["bdiag"] = bd.astype(ml_dtypes.bfloat16)
    return out


def _masks_for(cfg, parity):
    NJ, QB, NMASK = cfg["NJ"], cfg["QB"], cfg["NMASK"]
    tri = np.zeros((NJ, 128, QB), np.float32)
    for j in range(NJ):
        for p in range(128):
            kk = min(p + 128 * j, QB)
            tri[j, p, :kk] = NEG_BIG
    zero = np.zeros((128, QB), np.float32)
    neg = np.full((128, QB), NEG_BIG, np.float32)
    tiles = []
    for j in range(NJ):
        tiles.append(tri[j] if parity == 0 else zero)
    for j in range(NJ):
        tiles.append(neg if parity == 0 else tri[j])
    for j in range(NJ):
        tiles.append(tri[j] if parity == 0 else neg)
    for j in range(NJ):
        tiles.append(zero if parity == 0 else tri[j])
    m = np.stack(tiles, 0)
    return np.ascontiguousarray(m.transpose(1, 0, 2).reshape(128, NMASK * QB)).astype(ml_dtypes.bfloat16)


def _in_maps_for(cfg, inputs, shared):
    c = cfg
    TQ, TL = c["TQ"], c["TL"]
    x = np.asarray(inputs["L0_attn_x"], np.float32)
    quarters = {0: (0, 3), 1: (1, 2)}
    in_maps = []
    for core in range(c["ncores"]):
        bi, par = core // 2, core % 2
        qlo, qhi = quarters[par]
        xb = x[bi]
        xloc = np.concatenate([xb[qlo * TQ:(qlo + 1) * TQ], xb[qhi * TQ:(qhi + 1) * TQ]], 0)
        fc = np.zeros((128, 2), np.float32)
        fc[:, 0] = 0.0 if par == 0 else 1.0
        fc[:, 1] = 1.0 if par == 0 else 0.0
        m = dict(shared)
        m["xfm"] = np.ascontiguousarray(xloc.T)
        m["fcol"] = fc
        m["masks"] = _masks_for(c, par)
        in_maps.append(m)
    return in_maps


def _assemble(cfg, results, Bn):
    c = cfg
    D, TQ, TL, NDH, DH = c["D"], c["TQ"], c["TL"], c["NDH"], c["DH"]
    quarters = {0: (0, 3), 1: (1, 2)}
    out = np.zeros((Bn, c["TG"], D), np.float32)
    for core in range(c["ncores"]):
        bi, par = core // 2, core % 2
        qlo, qhi = quarters[par]
        od = np.asarray(results[core]["o"])
        oloc = od.reshape(TL, D)
        out[bi, qlo * TQ:(qlo + 1) * TQ] = oloc[0:TQ]
        out[bi, qhi * TQ:(qhi + 1) * TQ] = oloc[TQ:TL]
    return out


def kernel(**inputs):
    cfg = make_cfg()
    x = np.asarray(inputs["L0_attn_x"], np.float32)
    shared = _prep_weights(cfg, inputs)
    nc, b = build_program(cfg)
    in_maps = _in_maps_for(cfg, inputs, shared)
    res = run_bass_kernel_spmd(nc, in_maps, core_ids=list(range(cfg["ncores"])))
    return _assemble(cfg, res.results, x.shape[0])


# version 12
# speedup vs baseline: 1.3707x; 1.3707x over previous
import contextlib
import numpy as np
import ml_dtypes

import concourse.bass as bass
import concourse.mybir as mybir
from concourse.bass_utils import run_bass_kernel_spmd

F32 = mybir.dt.float32
F32R = mybir.dt.float32r
BF16 = mybir.dt.bfloat16
AF = mybir.ActivationFunctionType
ALU = mybir.AluOpType

NEG_BIG = -3.0e30


def make_cfg(D=1024, H=16, TG=4096, FF=4096, NE=8, ncores=8):
    c = {}
    c["D"], c["H"], c["TG"], c["FF"], c["NE"], c["ncores"] = D, H, TG, FF, NE, ncores
    c["DC"] = D // 128
    c["HC"] = H // 2
    c["F3"] = 3 * D
    c["FC3"] = 3 * D // 128
    c["TQ"] = TG // 4
    c["TL"] = TG // 2
    c["KC"] = TG // 128
    c["KCS"] = c["TQ"] // 128
    c["FFC"] = FF // 128
    c["TB"] = min(256, c["TL"])
    c["NTB"] = c["TL"] // c["TB"]
    c["QB"] = min(256, c["TQ"])
    c["NQB"] = c["TL"] // c["QB"]
    c["NQBQ"] = c["TQ"] // c["QB"]
    c["NJ"] = c["QB"] // 128
    c["MTB"] = min(256, c["TL"])
    c["NMTB"] = c["TL"] // c["MTB"]
    c["DH"] = min(512, D)
    c["NDH"] = D // c["DH"]
    c["VW"] = 65
    c["NS"] = c["F3"] // 256
    c["NMASK"] = 4 * c["NJ"]
    c["PTN"] = min(16, c["KC"])
    return c


def attn_sched(cfg, att, i):
    KCS, KC, NJ, NQBQ = cfg["KCS"], cfg["KC"], cfg["NJ"], cfg["NQBQ"]
    if att == 0:
        return [(k, None) for k in range(KC)], []
    s0, s1, s2, s3 = 0, KCS, 2 * KCS, 3 * KCS
    if i < NQBQ:
        p = i * NJ
        main = [(s0 + k, None) for k in range(p)] + \
               [(s0 + p + j, 0 * NJ + j) for j in range(NJ)] + \
               [(s2 + p + j, 1 * NJ + j) for j in range(NJ)]
        plist = [s0 + k for k in range(p + NJ, KCS)] + [s2 + k for k in range(p)]
        parts = [(plist, 0)] if plist else []
        return main, parts
    ii = i - NQBQ
    p = ii * NJ
    main = [(s0 + k, None) for k in range(KCS)] + \
           [(s2 + k, None) for k in range(KCS)] + \
           [(s3 + k, None) for k in range(p)] + \
           [(s1 + p + j, 2 * NJ + j) for j in range(NJ)] + \
           [(s3 + p + j, 3 * NJ + j) for j in range(NJ)]
    plist = [s1 + k for k in range(p)] + [s3 + k for k in range(p + NJ, KCS)]
    parts = [(plist, 1)] if plist else []
    return main, parts


class Sem:
    def __init__(self, es, nc, name):
        self.h = es.enter_context(nc.semaphore(name))
        self.n = 0

    def inc(self, ins, k=1):
        ins.then_inc(self.h, k)
        self.n += k
        return self.n

    def issue(self, eng, out, in_, **kw):
        eng.wait_ge(self.h, self.n)
        return self.inc(eng.dma_start(out=out, in_=in_, **kw), 16)


class B:
    pass


def build_program(cfg, stop_after=None, dbg_out=False, dbg_moe=False):
    b = B()
    b.cfg = cfg
    c = cfg
    nc = bass.Bass("TRN2", target_bir_lowering=False, num_devices=c["ncores"],
                   dynamic_dma_scratch_size=16384)
    b.nc = nc
    es = contextlib.ExitStack()
    b.es = es
    D, DC, H, HC, F3, FC3 = c["D"], c["DC"], c["H"], c["HC"], c["F3"], c["FC3"]
    TG, TQ, TL, KC, KCS, VW = c["TG"], c["TQ"], c["TL"], c["KC"], c["KCS"], c["VW"]
    FF, FFC, NE = c["FF"], c["FFC"], c["NE"]
    TB, NTB, QB, NQB, NQBQ, NJ = c["TB"], c["NTB"], c["QB"], c["NQB"], c["NQBQ"], c["NJ"]
    MTB, NMTB, DH, NDH = c["MTB"], c["NMTB"], c["DH"], c["NDH"]
    NS, NMASK, PTN = c["NS"], c["NMASK"], c["PTN"]

    NBC = 2 * FC3 + 2 * DC + NE * FFC + 4 * DC
    OB_QKV1 = 0
    OB_OUT1 = FC3
    OB_QKV2 = FC3 + DC
    OB_OUT2 = 2 * FC3 + DC
    OB_FC1 = 2 * FC3 + 2 * DC
    OB_LNB = 2 * FC3 + 2 * DC + NE * FFC
    BET_PLAIN, BET_ATTN, BET_LN1, BET_LN2 = 0, DC, 2 * DC, 3 * DC
    GAM_PLAIN, GAM_ATTN, GAM_LN1, GAM_LN2 = 0, D, 2 * D, 3 * D

    d = {}
    d["xfm"] = nc.dram_tensor("xfm", [D, TL], F32, kind="ExternalInput").ap()
    d["wqkv1"] = nc.dram_tensor("wqkv1", [NS, 128, DC * 256], F32, kind="ExternalInput").ap()
    d["wqkv2"] = nc.dram_tensor("wqkv2", [NS, 128, DC * 256], F32, kind="ExternalInput").ap()
    d["wout1"] = nc.dram_tensor("wout1", [128, DC * D], F32, kind="ExternalInput").ap()
    d["wout2"] = nc.dram_tensor("wout2", [128, DC * D], F32, kind="ExternalInput").ap()
    d["fc1"] = nc.dram_tensor("fc1", [NE, 4, 128, DC * FF // 4], F32, kind="ExternalInput").ap()
    d["fc2"] = nc.dram_tensor("fc2", [NE, 4, 128, FFC * D // 4], F32, kind="ExternalInput").ap()
    d["biasp"] = nc.dram_tensor("biasp", [128, NBC], F32, kind="ExternalInput").ap()
    d["lnrows"] = nc.dram_tensor("lnrows", [1, 4 * D], F32, kind="ExternalInput").ap()
    d["fc2b"] = nc.dram_tensor("fc2b", [NE, D], F32, kind="ExternalInput").ap()
    d["gatew"] = nc.dram_tensor("gatew", [128, DC * NE], F32, kind="ExternalInput").ap()
    d["identb"] = nc.dram_tensor("identb", [128, 128], BF16, kind="ExternalInput").ap()
    d["identf"] = nc.dram_tensor("identf", [128, 128], F32, kind="ExternalInput").ap()
    d["shifteye"] = nc.dram_tensor("shifteye", [64, 128], BF16, kind="ExternalInput").ap()
    d["masks"] = nc.dram_tensor("masks", [128, NMASK * QB], BF16, kind="ExternalInput").ap()
    d["onescol"] = nc.dram_tensor("onescol", [128, 1], BF16, kind="ExternalInput").ap()
    d["onesat64"] = nc.dram_tensor("onesat64", [65, 128], F32, kind="ExternalInput").ap()
    d["fcol"] = nc.dram_tensor("fcol", [128, 2], F32, kind="ExternalInput").ap()
    d["bdiag"] = nc.dram_tensor("bdiag", [128, DC * H], BF16, kind="ExternalInput").ap()
    out_d = nc.dram_tensor("o", [TL // 128, 128, NDH * DH], F32, kind="ExternalOutput").ap()
    if dbg_moe:
        dbg_nx2 = nc.dram_tensor("dbg_nx2", [128, DC * TL], BF16, kind="ExternalOutput").ap()
        dbg_gts = nc.dram_tensor("dbg_gts", [128, (TL // 128) * NE], F32, kind="ExternalOutput").ap()
        dbg_gts2 = nc.dram_tensor("dbg_gts2", [128, (TL // 128) * NE], F32, kind="ExternalOutput").ap()
        dbg_h = nc.dram_tensor("dbg_h", [128, FFC * MTB], BF16, kind="ExternalOutput").ap()
        dbg_sw0 = nc.dram_tensor("dbg_sw0", [128, 4192], F32, kind="ExternalOutput").ap()
        dbg_sw1 = nc.dram_tensor("dbg_sw1", [128, 4192], F32, kind="ExternalOutput").ap()
        dbg_fc1 = nc.dram_tensor("dbg_fc1", [128, DC * FF], BF16, kind="ExternalOutput").ap()
        dbg_fc2 = nc.dram_tensor("dbg_fc2", [128, FFC * D], BF16, kind="ExternalOutput").ap()
    _scrkind = "ExternalOutput" if dbg_out else "Internal"
    scr_s = nc.dram_tensor("scr_s", [D, TL], F32, kind=_scrkind).ap()
    scr_x1 = nc.dram_tensor("scr_x1", [D, TL], F32, kind=_scrkind).ap()
    scr_y2 = nc.dram_tensor("scr_y2", [D, TL], F32, kind=_scrkind).ap()
    scr_xb = nc.dram_tensor("scr_xb", [D, TL], F32, kind=_scrkind).ap()
    scr_st = nc.dram_tensor("scr_st", [NTB, 2, TB], F32, kind="Internal").ap()
    scr_q = [nc.dram_tensor(f"scr_q{a}", [DC, 128, TL], BF16, kind="Internal").ap()
             for a in range(2)]
    EXW = DC * TL + 2 * KCS * H * VW
    send = [nc.dram_tensor(f"send{a}", [128, EXW], BF16, kind="Internal").ap() for a in range(2)]
    recv = [nc.dram_tensor(f"recv{a}", [2, 128, EXW], BF16, kind="Internal").ap() for a in range(2)]

    import math as _m
    cur = [16896]

    def place(name, shape, dtype, at=None):
        if at is None:
            at = cur[0]
        t = nc.alloc_sbuf_tensor_at(name, shape, dtype, offset=at)
        esz = 2 if dtype == BF16 else 4
        end = at + _m.prod(shape[1:]) * esz if len(shape) > 1 else at + esz
        cur[0] = max(cur[0], (end + 63) // 64 * 64)
        return t.ap()

    smallswin = place("smallswin", [128, 4192], F32, 16896) if dbg_moe else None
    cur[0] = 16896
    identb = place("identb", [128, 128], BF16)
    identf = place("identf", [128, 128], F32)
    shifteye = place("shifteye", [64, 128], BF16)
    onescol = place("onescol", [128, 1], BF16)
    onesat64 = place("onesat64", [65, 128], F32)
    fcol = place("fcol", [128, 2], F32)
    bdiag = place("bdiag", [128, DC, H], BF16)
    biasp = place("biasp", [128, NBC], F32)
    gatewb = place("gatewb", [128, DC, NE], BF16)
    gatewf = place("gatewf", [128, DC, NE], F32)
    lnbuf = place("lnbuf", [1, 2, 128], BF16)
    lnbuf32 = place("lnbuf32", [1, 2, 128], F32)
    mrow = place("mrow", [1, 3, TB], F32)
    mrowb = place("mrowb", [1, 2, TB], BF16)
    XR = place("XR", [128, 2, 256], BF16)
    qkn = place("qkn", [H, 8], F32)
    nbrow = place("nbrow", [1, H], F32)
    bb = place("bb", [128, 2 * H], F32)
    lrow65 = place("lrow65", [65, QB], F32)
    gts = place("gts", [128, TL // 128, NE], F32)
    aotmp = place("aotmp", [64, QB], BF16)
    QS = place("QS", [128, 2, TB], BF16)
    qtmp = place("qtmp", [128, 2], F32)
    qmaxcol = place("qmaxcol", [128, DC], F32)
    qmaxb = place("qmaxb", [128, DC], BF16)
    STR = place("STR", [33, 2, TB], F32)
    STG = place("STG", [128, 3, TB], F32)
    SQ = place("SQ", [128, TB], F32)
    VSTb = SQ.bitcast(BF16).rearrange("p (r t) -> p r t", r=2)
    FS = cur[0]
    WSL = place("WSL", [128, 2, DC, 256], BF16, FS)
    AO128 = place("AO128", [128, HC, QB], BF16, FS)
    OSTG = place("OSTG", [128, 2, NDH * DH], F32, FS)
    cur[0] = (FS + max(2 * DC * 256 * 2, HC * QB * 2, 2 * NDH * DH * 4) + 63) // 64 * 64
    CS = cur[0]
    Qt = place("Qt", [128, 2, DC, QB], BF16, CS)
    gerow = place("gerow", [NE, TL], BF16, CS)
    fc2bs = place("fc2bs", [NE, D], BF16, CS + TL * 2)
    fc2bs32 = place("fc2bs32", [NE, D], F32, CS + TL * 2 + D * 2)
    h2 = place("h2", [128, 2, FFC // 2, MTB], BF16, CS)
    cur[0] = (CS + max(2 * DC * QB * 2, TL * 2 + D * 2 + D * 4, FFC * MTB * 2) + 63) // 64 * 64
    DS = cur[0]
    Z = place("Z", [128, DC, TL], BF16, DS)
    WOUT = place("WOUT", [128, DC, D], BF16, DS)
    PT = place("PT", [128, PTN, QB], BF16, DS + DC * D * 2)
    masks = place("masks", [128, NMASK, QB], BF16, DS + DC * D * 2 + PTN * QB * 2)
    cur[0] = (DS + max(DC * TL * 2, DC * D * 2 + PTN * QB * 2 + NMASK * QB * 2) + 63) // 64 * 64
    AS = cur[0]
    X = place("X", [128, DC, TL], F32, AS)
    K = place("K", [128, DC, TG], BF16, AS)
    Y2 = place("Y2", [128, DC, TL], F32, AS)
    FC1 = place("FC1", [128, 4, DC, FF // 4], BF16, AS)
    cur[0] = (AS + max(DC * TL * 4, DC * TG * 2, DC * FF * 2) + 63) // 64 * 64
    BS = cur[0]
    Y = place("Y", [128, DC, TL], F32, BS)
    V = place("V", [128, KC, H * VW], BF16, BS)
    FC2 = place("FC2", [128, FFC, D], BF16, BS)
    cur[0] = BS + max(DC * TL * 4, KC * H * VW * 2, FFC * D * 2)
    assert cur[0] <= nc.SBUF_PARTITION_SIZE_BYTES, f"SBUF over budget: {cur[0]}"
    b.sbuf_used = cur[0]

    pB = [es.enter_context(nc.psum_tensor(f"pB{i}", [128, 512], F32)) for i in range(6)]
    sA1 = es.enter_context(nc.psum_tensor("sA1", [128, 512], F32))
    sA2 = es.enter_context(nc.psum_tensor("sA2", [128, 512], F32))
    pTRb = pB[3].ap().bitcast(BF16)

    sSY = Sem(es, nc, "sSY")
    sW = Sem(es, nc, "sW")
    sWC = Sem(es, nc, "sWC")
    sCC = Sem(es, nc, "sCC")
    sMM = Sem(es, nc, "sMM")
    sSC = Sem(es, nc, "sSC")
    sVE = Sem(es, nc, "sVE")
    sGC = Sem(es, nc, "sGC")
    sT = [Sem(es, nc, f"sT{i}") for i in range(3)]
    sLN = Sem(es, nc, "sLN")
    sRW = Sem(es, nc, "sRW")
    sOS = Sem(es, nc, "sOS")
    os_last = [0, 0]
    sQS = Sem(es, nc, "sQS")
    sQT = Sem(es, nc, "sQT")
    qt_free = [0, 0]
    blk = es.enter_context(nc.Block())

    def f32r(ap):
        return ap.bitcast(F32R)

    def T(fn): blk.tensor(fn)
    def VEC(fn): blk.vector(fn)
    def SCL(fn): blk.scalar(fn)
    def GP(fn): blk.gpsimd(fn)
    def SY(fn): blk.sync(fn)

    def ph0(sy):
        for name, dst in (("identb", identb), ("identf", identf),
                          ("shifteye", shifteye), ("onescol", onescol),
                          ("onesat64", onesat64), ("fcol", fcol),
                          ("biasp", biasp)):
            sSY.issue(sy, dst, d[name])
        sSY.issue(sy, bdiag, d["bdiag"].rearrange("p (c h) -> p c h", c=DC))
        sSY.issue(sy, X, d["xfm"].rearrange("(c p) t -> p c t", p=128))
    SY(ph0)
    CONSTS = sSY.n

    def ph0g(sy):
        sSY.issue(sy, gatewf, d["gatew"].rearrange("p (c e) -> p c e", c=DC))
    SY(ph0g)
    GATEWF_AT = sSY.n

    def ph0gc(s):
        s.wait_ge(sSY.h, GATEWF_AT)
        sSC.inc(s.activation(gatewb[:, :, :].rearrange("p c e -> p (c e)"),
                             gatewf[:, :, :].rearrange("p c e -> p (c e)"), AF.Identity))
    SCL(ph0gc)
    GATEWB_AT = sSC.n

    def ph0s(s):
        s.wait_ge(sSY.h, CONSTS)
    SCL(ph0s)

    def ph0v(v):
        v.wait_ge(sSY.h, CONSTS)
    VEC(ph0v)

    stat_flush = {}

    def emit_stats(src_ap, tbi, colofs, w, first, last):
        flush_ve = stat_flush.get(tbi % 4, 0) if first else None
        mm_war = sMM.n

        def fs(v):
            v.wait_ge(sVE.h, sVE.n)
            v.wait_ge(sMM.h, mm_war)
            v.tensor_copy(XR[:, 0, 0:w], src_ap)
            sVE.inc(v.tensor_mul(XR[:, 1, 0:w], src_ap, src_ap))
        VEC(fs)
        sc_at = sVE.n
        p0 = 32 * (tbi % 2)
        bc = 256 * ((tbi // 2) % 2)
        a1 = sA1[p0:p0 + 1, bc + colofs:bc + colofs + w]
        a2 = sA2[p0:p0 + 1, bc + colofs:bc + colofs + w]

        def ft(t):
            t.wait_ge(sVE.h, sc_at)
            if first and flush_ve is not None:
                t.wait_ge(sVE.h, flush_ve)
            t.matmul(a1, onescol, XR[:, 0, 0:w], start=first, stop=last)
            sMM.inc(t.matmul(a2, onescol, XR[:, 1, 0:w], start=first, stop=last))
        T(ft)
        return sMM.n

    flush_rw = {}

    def emit_stats_flush(tbi, stats_mm):
        p0 = 32 * (tbi % 2)
        bc = 256 * ((tbi // 2) % 2)
        rw_war = flush_rw.get(p0, 0)

        def fvc(v):
            v.wait_ge(sMM.h, stats_mm)
            v.wait_ge(sRW.h, rw_war)
            v.wait_ge(sVE.h, sVE.n)
            v.tensor_copy(STR[p0:p0 + 1, 0, :], sA1[p0:p0 + 1, bc:bc + TB])
            v.wait_ge(sVE.h, sVE.n)
            sVE.inc(v.tensor_copy(STR[p0:p0 + 1, 1, :], sA2[p0:p0 + 1, bc:bc + TB]))
        VEC(fvc)
        ve_c = sVE.n

        def fy(sy):
            sy.wait_ge(sVE.h, ve_c)
            sRW.issue(sy, scr_st[tbi, 0, :], STR[p0:p0 + 1, 0, :])
            sRW.issue(sy, scr_st[tbi, 1, :], STR[p0:p0 + 1, 1, :])
        SY(fy)
        stat_flush[tbi % 4] = sVE.n
        flush_rw[p0] = sRW.n

    def emit_ln_rows(tbi, stats_ve):
        ivD = 1.0 / D
        mm_war = sMM.n

        def fy(sy):
            sy.wait_ge(sMM.h, mm_war)
            sRW.issue(sy, mrow[0:1, 0, :], scr_st[tbi, 0, :])
            sRW.issue(sy, mrow[0:1, 1, :], scr_st[tbi, 1, :])
        SY(fy)
        sy_at = sRW.n

        def fv(v):
            v.wait_ge(sRW.h, sy_at)
            v.wait_ge(sVE.h, sVE.n)
            v.tensor_scalar(mrow[0:1, 0, :], mrow[0:1, 0, :], ivD, None, ALU.mult)
            v.wait_ge(sVE.h, sVE.n)
            sVE.inc(v.tensor_scalar(mrow[0:1, 1, :], mrow[0:1, 1, :], ivD, None, ALU.mult))
        VEC(fv)
        ve1 = sVE.n

        def fsq(s):
            s.wait_ge(sVE.h, ve1)
            sSC.inc(s.activation(mrow[0:1, 2, :], mrow[0:1, 0, :], AF.Square))
        SCL(fsq)
        sc1 = sSC.n

        def fv2(v):
            v.wait_ge(sSC.h, sc1)
            sVE.inc(v.scalar_tensor_tensor(mrow[0:1, 1, :], mrow[0:1, 1, :], 1e-5,
                                           mrow[0:1, 2, :], ALU.add, ALU.subtract))
        VEC(fv2)
        ve2 = sVE.n

        def fs2(s):
            s.wait_ge(sVE.h, ve2)
            sSC.inc(s.activation(mrow[0:1, 2, :], mrow[0:1, 1, :], AF.Sqrt))
        SCL(fs2)
        sc2 = sSC.n

        def fv3(v):
            v.wait_ge(sSC.h, sc2)
            sVE.inc(v.reciprocal(mrow[0:1, 2, :], mrow[0:1, 2, :]))
            v.wait_ge(sVE.h, sVE.n)
            sVE.inc(v.tensor_mul(mrow[0:1, 0, :], mrow[0:1, 0, :], mrow[0:1, 2, :]))
            v.wait_ge(sVE.h, sVE.n)
            sVE.inc(v.tensor_scalar(mrow[0:1, 0, :], mrow[0:1, 0, :], -1.0, None, ALU.mult))
        VEC(fv3)
        ve3 = sVE.n

        def fs3(s):
            s.wait_ge(sVE.h, ve3)
            s.activation(mrowb[0:1, 0, :], mrow[0:1, 2, :], AF.Identity)
            sSC.inc(s.activation(mrowb[0:1, 1, :], mrow[0:1, 0, :], AF.Identity))
        SCL(fs3)
        rows_sc = sSC.n

        def fv4(v):
            v.wait_ge(sSC.h, rows_sc)
            sVE.inc(v.engine_nop())
        VEC(fv4)
        return sVE.n

    lncast = {}

    def emit_lnbuf(slot, grow, ci, mm_war):
        cast_war = lncast.get(slot, 0)

        def fy(sy):
            sy.wait_ge(sSC.h, cast_war)
            sLN.issue(sy, lnbuf32[0:1, slot, :],
                      d["lnrows"][0:1, grow + ci * 128:grow + (ci + 1) * 128])
        SY(fy)
        ln_at = sLN.n

        def fc(s):
            s.wait_ge(sLN.h, ln_at)
            s.wait_ge(sMM.h, mm_war)
            sSC.inc(s.activation(lnbuf[0:1, slot, :], lnbuf32[0:1, slot, :], AF.Identity))
        SCL(fc)
        lncast[slot] = sSC.n
        return sSC.n

    def emit_ln_chunk(tbi, ci, rows_ve, src_ap, dst_ap, grow, bset, w=None,
                      colofs=0, wait_sy=None):
        w = w or TB
        slot = ci % 2
        mm_war = sMM.n
        sy_at = emit_lnbuf(slot, grow, ci, mm_war)
        pR = pB[slot][0:128, 0:w]
        pA = pB[2 + slot][0:128, 0:w]
        ve_war = sVE.n

        def ft(t):
            t.wait_ge(sSC.h, sy_at)
            t.wait_ge(sVE.h, max(rows_ve, ve_war))
            t.matmul(pR, lnbuf[0:1, slot, :], mrowb[0:1, 0, colofs:colofs + w],
                     start=True, stop=True)
            sMM.inc(t.matmul(pA, lnbuf[0:1, slot, :], mrowb[0:1, 1, colofs:colofs + w],
                             start=True, stop=True))
        T(ft)
        mm_at = sMM.n
        bcol = biasp[:, OB_LNB + bset + ci:OB_LNB + bset + ci + 1]

        def fv(v):
            v.wait_ge(sMM.h, mm_at)
            if wait_sy is not None:
                v.wait_ge(wait_sy[0].h, wait_sy[1])
            v.wait_ge(sVE.h, sVE.n)
            sVE.inc(v.tensor_mul(STG[:, 2, 0:w], src_ap, pR))
            v.wait_ge(sVE.h, sVE.n)
            sVE.inc(v.scalar_tensor_tensor(dst_ap, STG[:, 2, 0:w], bcol, pA, ALU.add, ALU.add))
        VEC(fv)
        return sVE.n

    NHF = TB // 256
    ve_stats = 0
    for tbi in range(NTB):
        for ci in range(DC):
            for hf in range(NHF):
                ve_stats = emit_stats(X[:, ci, tbi * TB + hf * 256:tbi * TB + (hf + 1) * 256],
                                      tbi, hf * 256, 256, ci == 0 and hf == 0,
                                      ci == DC - 1 and hf == NHF - 1)
        emit_stats_flush(tbi, ve_stats)

    ystats = []
    for tbi in range(NTB):
        rows_ve = emit_ln_rows(tbi, ve_stats if tbi == 0 else sVE.n)
        sl = slice(tbi * TB, (tbi + 1) * TB)
        yv = 0
        for ci in range(DC):
            emit_ln_chunk(tbi, ci, rows_ve, X[:, ci, sl], Y[:, ci, sl], GAM_PLAIN, BET_PLAIN)
            slot_war = sT[ci % 2].n

            def fv(v, ci=ci, sl=sl, slot_war=slot_war):
                v.wait_ge(sVE.h, sVE.n)
                v.wait_ge(sT[ci % 2].h, slot_war)
                sVE.inc(v.tensor_add(STG[:, ci % 2, :], X[:, ci, sl], Y[:, ci, sl]))
            VEC(fv)
            ve_s = sVE.n

            def fy(sy, ci=ci, sl=sl, ve_s=ve_s):
                sy.wait_ge(sVE.h, ve_s)
                sT[ci % 2].issue(sy, scr_s[ci * 128:(ci + 1) * 128, sl], STG[:, ci % 2, :])
            SY(fy)
            for hf in range(NHF):
                yv = emit_stats(Y[:, ci, tbi * TB + hf * 256:tbi * TB + (hf + 1) * 256],
                                tbi, hf * 256, 256, ci == 0 and hf == 0,
                                ci == DC - 1 and hf == NHF - 1)
        emit_stats_flush(tbi, yv)
        ystats.append(yv)

    for tbi in range(NTB):
        rows_ve = emit_ln_rows(tbi, ystats[tbi] if tbi == 0 else sVE.n)
        sl = slice(tbi * TB, (tbi + 1) * TB)
        for ci in range(DC):
            emit_ln_chunk(tbi, ci, rows_ve, Y[:, ci, sl], Z[:, ci, sl], GAM_ATTN, BET_ATTN)

    def emit_qkv(att, entry_dma_waits=()):
        wq = d["wqkv1"] if att == 0 else d["wqkv2"]
        OBQ = OB_QKV1 if att == 0 else OB_QKV2
        qs_last = [0, 0]
        po_free = [0, 0]
        ve_free = sVE.n
        mm_free = sMM.n

        def fg0(g):
            g.wait_ge(sVE.h, ve_free)
            g.wait_ge(sMM.h, mm_free)
            sGC.inc(g.memset(V.rearrange("p k (h w) -> p k h w", w=VW)[:, :, :, VW - 1:VW], 1.0))
        GP(fg0)
        gc_ones = sGC.n
        wsl_free = [sMM.n, sMM.n]
        z_ready = sVE.n
        for sl_i in range(NS):
            slot = sl_i % 2

            def fg(g, sl_i=sl_i, slot=slot, war=wsl_free[slot]):
                g.wait_ge(sMM.h, war)
                sW.issue(g, WSL[:, slot, :, :].rearrange("p c f -> p (c f)"),
                         wq[sl_i])
            GP(fg)
            w_at = sW.n
            for fi in range(2):
                fc = 2 * sl_i + fi
                for tbi in range(NTB):
                    pslot = (fc * NTB + tbi) % 2
                    po = pB[4 + pslot][0:128, 0:TB]
                    sc_war = po_free[pslot]

                    def ft(t, slot=slot, fi=fi, tbi=tbi, w_at=w_at, po=po, sc_war=sc_war):
                        t.wait_ge(sW.h, w_at)
                        t.wait_ge(sVE.h, max(z_ready, sc_war))
                        for sem_, val_ in entry_dma_waits:
                            t.wait_ge(sem_.h, val_)
                        for ci in range(DC):
                            ins = t.matmul(po, WSL[:, slot, ci, fi * 128:(fi + 1) * 128],
                                           Z[:, ci, tbi * TB:(tbi + 1) * TB],
                                           start=(ci == 0), stop=(ci == DC - 1))
                        sMM.inc(ins)
                    T(ft)
                    mm_at = sMM.n
                    bcol = biasp[:, OBQ + fc:OBQ + fc + 1]
                    sl2 = slice(tbi * TB, (tbi + 1) * TB)
                    if fc < DC:
                        qslot = (fc * NTB + tbi) % 2
                        qwar = qs_last[qslot]

                        def fv_(v, po=po, mm_at=mm_at, bcol=bcol, qslot=qslot, qwar=qwar,
                                fc=fc, tbi=tbi):
                            v.wait_ge(sMM.h, mm_at)
                            v.wait_ge(sQS.h, qwar)
                            v.wait_ge(sVE.h, sVE.n)
                            sVE.inc(v.tensor_scalar(QS[:, qslot, :], po, bcol, None, ALU.add))
                            v.wait_ge(sVE.h, sVE.n)
                            sVE.inc(v.tensor_mul(XR[:, 1, 0:TB], QS[:, qslot, :],
                                                 QS[:, qslot, :]))
                            v.wait_ge(sVE.h, sVE.n)
                            sVE.inc(v.reduce_max(qtmp[:, 0:1], XR[:, 1, 0:TB],
                                                 mybir.AxisListType.X))
                            v.wait_ge(sVE.h, sVE.n)
                            if tbi == 0:
                                sVE.inc(v.tensor_copy(qmaxcol[:, fc:fc + 1], qtmp[:, 0:1]))
                            else:
                                sVE.inc(v.tensor_max(qmaxcol[:, fc:fc + 1],
                                                     qmaxcol[:, fc:fc + 1], qtmp[:, 0:1]))
                        VEC(fv_)
                        ve_q = sVE.n

                        def fyq(sy, att=att, fc=fc, sl2=sl2, qslot=qslot, ve_q=ve_q):
                            sy.wait_ge(sVE.h, ve_q)
                            sQS.issue(sy, scr_q[att][fc, :, sl2], QS[:, qslot, :])
                        SY(fyq)
                        qs_last[qslot] = sQS.n
                        po_free[pslot] = sVE.n
                    elif fc < 2 * DC:
                        def fv_(v, po=po, mm_at=mm_at, bcol=bcol, fc=fc, sl2=sl2):
                            v.wait_ge(sMM.h, mm_at)
                            v.wait_ge(sVE.h, sVE.n)
                            sVE.inc(v.tensor_scalar(K[:, fc - DC, sl2], po, bcol, None, ALU.add))
                        VEC(fv_)
                        po_free[pslot] = sVE.n
                    else:
                        vslot = (fc * NTB + tbi) % 2

                        def fv_(v, po=po, mm_at=mm_at, bcol=bcol, vslot=vslot):
                            v.wait_ge(sMM.h, mm_at)
                            v.wait_ge(sVE.h, sVE.n)
                            sVE.inc(v.tensor_scalar(VSTb[:, vslot, 0:TB], po, bcol, None, ALU.add))
                        VEC(fv_)
                        sc_at = sVE.n
                        po_free[pslot] = sVE.n
                        vc = fc - 2 * DC
                        for ts in range(TB // 128):
                            tpos = (ts % 4) * 128
                            ve_war2 = sVE.n

                            def ft2(t, vslot=vslot, ts=ts, tpos=tpos, sc_at=sc_at, ve_war2=ve_war2):
                                t.wait_ge(sVE.h, max(sc_at, ve_war2))
                                sMM.inc(t.transpose(pTRb[:, tpos:tpos + 128],
                                                    VSTb[:, vslot, ts * 128:(ts + 1) * 128], identb))
                            T(ft2)
                            mm2 = sMM.n
                            kct = (tbi * TB) // 128 + ts
                            dst = V.rearrange("p k (h w) -> p k h w", w=VW)[:, kct, 2 * vc:2 * vc + 2, 0:64]

                            def fv(v, tpos=tpos, mm2=mm2, dst=dst):
                                v.wait_ge(sMM.h, mm2)
                                v.wait_ge(sVE.h, sVE.n)
                                src = pTRb[:, tpos:tpos + 128].rearrange("p (a f) -> p a f", a=2)
                                sVE.inc(v.tensor_copy(dst, src))
                            VEC(fv)
                    if fi == 1 and tbi == NTB - 1:
                        wsl_free[slot] = sMM.n
        return gc_ones

    def emit_bound(src, col):
        first = [True]
        for tbi in range(NTB):
            for ci in range(DC):
                mm_war = sMM.n

                def fv0(v, ci=ci, tbi=tbi, mm_war=mm_war):
                    v.wait_ge(sVE.h, sVE.n)
                    v.wait_ge(sMM.h, mm_war)
                    s_ = src[:, ci, tbi * TB:(tbi + 1) * TB]
                    sVE.inc(v.tensor_mul(XR[:, 1, 0:TB], s_, s_))
                VEC(fv0)
                sc_at2 = sVE.n

                ve_war2 = sVE.n

                def ft(t, ci=ci, sc_at2=sc_at2, ve_war2=ve_war2):
                    t.wait_ge(sVE.h, max(sc_at2, ve_war2 if ci == 0 else 0))
                    sMM.inc(t.matmul(pB[2][0:H, 0:TB], bdiag[:, ci, :], XR[:, 1, 0:TB],
                                     start=(ci == 0), stop=(ci == DC - 1)))
                T(ft)
            mm_at = sMM.n

            def fv(v, mm_at=mm_at, fr=first[0]):
                v.wait_ge(sMM.h, mm_at)
                v.wait_ge(sVE.h, sVE.n)
                sVE.inc(v.reduce_max(qkn[0:H, 4 + col:5 + col], pB[2][0:H, 0:TB], mybir.AxisListType.X))
                v.wait_ge(sVE.h, sVE.n)
                if fr:
                    sVE.inc(v.tensor_copy(qkn[0:H, col:col + 1], qkn[0:H, 4 + col:5 + col]))
                else:
                    sVE.inc(v.tensor_max(qkn[0:H, col:col + 1], qkn[0:H, col:col + 1],
                                         qkn[0:H, 4 + col:5 + col]))
            VEC(fv)
            first[0] = False
        return sVE.n

    def emit_qbound():
        ve_at = sVE.n

        def fsc(s, ve_at=ve_at):
            s.wait_ge(sVE.h, ve_at)
            s.wait_ge(sSC.h, sSC.n)
            sSC.inc(s.activation(qmaxb[:, :], qmaxcol[:, :], AF.Identity))
        SCL(fsc)
        sc_at = sSC.n
        mm_war = sMM.n

        def ftq(t, sc_at=sc_at, mm_war=mm_war):
            t.wait_ge(sSC.h, sc_at)
            t.wait_ge(sMM.h, mm_war)
            for hc in range(DC):
                ins = t.matmul(pB[2][0:H, 0:1], bdiag[:, hc, :], qmaxb[:, hc:hc + 1],
                               start=(hc == 0), stop=(hc == DC - 1))
            sMM.inc(ins)
        T(ftq)
        mm_at = sMM.n

        def fvq2(v, mm_at=mm_at):
            v.wait_ge(sMM.h, mm_at)
            v.wait_ge(sVE.h, sVE.n)
            sVE.inc(v.tensor_copy(qkn[0:H, 0:1], pB[2][0:H, 0:1]))
        VEC(fvq2)

    def emit_bias_from_bounds(att):
        def fv(v):
            v.wait_ge(sVE.h, sVE.n)
            sVE.inc(v.tensor_mul(qkn[0:H, 2:3], qkn[0:H, 0:1], qkn[0:H, 1:2]))
        VEC(fv)
        ve_at = sVE.n

        def fs(s):
            s.wait_ge(sVE.h, ve_at)
            sSC.inc(s.activation(qkn[0:H, 3:4], qkn[0:H, 2:3], AF.Sqrt))
        SCL(fs)
        sc_at = sSC.n

        def fv2(v):
            v.wait_ge(sSC.h, sc_at)
            sVE.inc(v.tensor_scalar(qkn[0:H, 3:4], qkn[0:H, 3:4], -0.1875, None, ALU.mult))
        VEC(fv2)
        ve2 = sVE.n

        def ft(t):
            t.wait_ge(sVE.h, ve2)
            sMM.inc(t.transpose(pB[2][0:1, 0:H], qkn[0:H, 3:4], identf[0:H, 0:H]))
        T(ft)
        mm1 = sMM.n

        def fv3(v):
            v.wait_ge(sMM.h, mm1)
            v.wait_ge(sVE.h, sVE.n)
            sVE.inc(v.tensor_copy(nbrow[0:1, :], pB[2][0:1, 0:H]))
        VEC(fv3)
        ve3 = sVE.n

        def ft2(t):
            t.wait_ge(sVE.h, ve3)
            sMM.inc(t.matmul(pB[2][0:128, 0:H], onesat64[0:1, :], nbrow[0:1, :],
                             start=True, stop=True))
        T(ft2)
        mm2 = sMM.n

        def fv4(v):
            v.wait_ge(sMM.h, mm2)
            v.wait_ge(sVE.h, sVE.n)
            sVE.inc(v.tensor_copy(bb[:, att * H:(att + 1) * H], pB[2][0:128, 0:H]))
        VEC(fv4)
        return sVE.n

    def emit_exchange(att, gc_ones):
        sd, rc = send[att], recv[att]
        sc_done = sSC.n
        ve_done = sVE.n

        def fy(sy):
            sy.wait_ge(sSC.h, sc_done)
            sy.wait_ge(sVE.h, ve_done)
            sy.wait_ge(sGC.h, gc_ones)
            sSY.issue(sy, sd[:, 0:DC * TL].rearrange("p (c t) -> p c t", c=DC),
                      K[:, :, 0:TL])
            sSY.issue(sy, sd[:, DC * TL:].rearrange("p (k w) -> p k w", k=2 * KCS),
                      V[:, 0:2 * KCS, :])
        SY(fy)
        sy_at = sSY.n

        def fg(g):
            g.wait_ge(sSY.h, sy_at)
            sCC.inc(g.collective_compute(
                "AllGather", ALU.bypass,
                replica_groups=[[2 * i, 2 * i + 1] for i in range(cfg["ncores"] // 2)],
                ins=[sd], outs=[rc]))
        GP(fg)
        cc_at = sCC.n

        def fy2(sy):
            sy.wait_ge(sCC.h, cc_at)
            for r in range(2):
                sSY.issue(sy, K[:, :, r * TL:(r + 1) * TL],
                          rc[r][:, 0:DC * TL].rearrange("p (c t) -> p c t", c=DC))
                sSY.issue(sy, V[:, r * 2 * KCS:(r + 1) * 2 * KCS, :],
                          rc[r][:, DC * TL:].rearrange("p (k w) -> p k w", k=2 * KCS))
        SY(fy2)
        return sSY.n

    def emit_attn(att, kv_ready_sy, bias_ve, wout_at, mask_at=0, qs_done=0):
        scr_in = scr_s if att == 0 else scr_y2
        scr_out = scr_x1 if att == 0 else scr_xb
        OBO = OB_OUT1 if att == 0 else OB_OUT2
        for qb in range(NQB):
            main, parts = attn_sched(cfg, att, qb)
            order = [kc for kc, m in main] + [kc for pl, fx in parts for kc in pl]
            mask_of = {pos: m for pos, (kc, m) in enumerate(main) if m is not None}
            U = len(order)
            nm = len(main)
            qsl = slice(qb * QB, (qb + 1) * QB)
            qslot_t = qb % 2
            war_mm = qt_free[qslot_t]

            def fql(sy, att=att, qsl=qsl, qslot_t=qslot_t, war_mm=war_mm, qs_done=qs_done):
                sy.wait_ge(sQS.h, qs_done)
                sy.wait_ge(sMM.h, war_mm)
                sQT.issue(sy, Qt[:, qslot_t],
                          scr_q[att][:, :, qsl].rearrange("c p t -> p c t"))
            SY(fql)
            qt_load = sQT.n
            for h in range(H):
                hp = (h % 2) * 64
                hc = h // 2
                qh = Qt[hp:hp + 64, qslot_t, hc, 0:QB]
                pAV = pB[2 + h % 2][0:65, 0:QB]
                ve_war0 = sVE.n
                grp_sc = []
                for r0 in range(0, U, PTN):
                    rend = min(r0 + PTN, U)
                    base_sc = sSC.n
                    exp_jobs = []
                    npr = 0
                    for g0 in range(r0, rend, 4):
                        grp = list(range(g0, min(g0 + 4, rend)))
                        war_sc = (grp_sc[-1] if grp_sc else 0) if g0 == r0 \
                            else base_sc + npr

                        def ft(t, grp=grp, war_sc=war_sc, h=h, hp=hp, hc=hc, qh=qh,
                               qt_load=qt_load):
                            t.wait_ge(sSC.h, war_sc)
                            if grp[0] == 0:
                                t.wait_ge(sSY.h, kv_ready_sy)
                                t.wait_ge(sVE.h, max(bias_ve, ve_war0))
                                if h == 0:
                                    t.wait_ge(sQT.h, qt_load)
                            for gi in grp:
                                kc = order[gi]
                                slot = gi % 4
                                ps = pB[slot // 2][0:128, (slot % 2) * QB:(slot % 2 + 1) * QB]
                                ins = t.matmul(ps, K[hp:hp + 64, hc, kc * 128:(kc + 1) * 128],
                                               qh, start=True, stop=True)
                            sMM.inc(ins)
                        T(ft)
                        mm_at = sMM.n
                        ve_mask = 0
                        if any(gi in mask_of for gi in grp):
                            def fvm(v, grp=grp, mm_at=mm_at):
                                v.wait_ge(sMM.h, mm_at)
                                v.wait_ge(sWC.h, mask_at)
                                v.wait_ge(sVE.h, sVE.n)
                                ins = None
                                for gi in grp:
                                    if gi in mask_of:
                                        slot = gi % 4
                                        ps = pB[slot // 2][0:128, (slot % 2) * QB:(slot % 2 + 1) * QB]
                                        ins = v.tensor_add(ps, ps, masks[:, mask_of[gi], :])
                                sVE.inc(ins)
                            VEC(fvm)
                            ve_mask = sVE.n
                        pairs = []
                        i2 = 0
                        while i2 < len(grp):
                            gi = grp[i2]
                            slot = gi % 4
                            if i2 + 1 < len(grp) and slot % 2 == 0:
                                ps = pB[slot // 2][0:128, 0:2 * QB]
                                dst = PT[:, gi - r0:gi - r0 + 2, 0:QB].rearrange(
                                    "p j q -> p (j q)")
                                pairs.append((dst, ps))
                                i2 += 2
                            else:
                                ps = pB[slot // 2][0:128, (slot % 2) * QB:(slot % 2 + 1) * QB]
                                pairs.append((PT[:, gi - r0, 0:QB], ps))
                                i2 += 1
                        exp_jobs.append((mm_at, ve_mask, pairs))
                        npr += len(pairs)
                        grp_sc.append(base_sc + npr)

                    def fs(s, jobs=exp_jobs, h=h):
                        bcol = bb[:, att * H + h:att * H + h + 1]
                        for mm_at_, ve_mask_, pairs_ in jobs:
                            s.wait_ge(sMM.h, mm_at_)
                            if ve_mask_:
                                s.wait_ge(sVE.h, ve_mask_)
                            for dst_, ps_ in pairs_:
                                sSC.inc(s.activation(dst_, ps_, AF.Exp, bias=bcol,
                                                     scale=0.125))
                    SCL(fs)
                    assert sSC.n == base_sc + npr
                    exp_done = sSC.n
                    mainpos = [i2 for i2 in range(r0, rend) if i2 < nm]
                    partpos = [i2 for i2 in range(r0, rend) if i2 >= nm]
                    if mainpos:
                        def ftav(t, h=h, pAV=pAV, exp_done=exp_done, mainpos=mainpos, r0=r0):
                            t.wait_ge(sSC.h, exp_done)
                            if mainpos[0] == 0:
                                t.wait_ge(sVE.h, ve_war0)
                            ins = None
                            for i2 in mainpos:
                                kc = order[i2]
                                ins = t.matmul(pAV, V[:, kc, 65 * h:65 * h + 65], PT[:, i2 - r0, 0:QB],
                                               start=(i2 == 0), stop=(i2 == nm - 1))
                            sMM.inc(ins)
                        T(ftav)
                    if partpos:
                        def ftp(t, h=h, exp_done=exp_done, partpos=partpos, r0=r0):
                            t.wait_ge(sSC.h, exp_done)
                            if partpos[0] == nm:
                                t.wait_ge(sVE.h, ve_war0)
                            ins = None
                            for i2 in partpos:
                                kc = order[i2]
                                ins = t.matmul(pB[4][0:65, 0:QB], V[:, kc, 65 * h:65 * h + 65],
                                               PT[:, i2 - r0, 0:QB],
                                               start=(i2 == nm), stop=(i2 == U - 1))
                            sMM.inc(ins)
                        T(ftp)
                if U > nm:
                    mmp = sMM.n
                    fx = parts[0][1]

                    def fvp(v, mmp=mmp, fx=fx, pAV=pAV):
                        v.wait_ge(sMM.h, mmp)
                        v.wait_ge(sVE.h, sVE.n)
                        sVE.inc(v.tensor_scalar(lrow65[:, 0:QB], pB[4][0:65, 0:QB],
                                                fcol[0:65, fx:fx + 1], None, ALU.mult))
                        v.wait_ge(sVE.h, sVE.n)
                        sVE.inc(v.tensor_add(pAV, pAV, lrow65[:, 0:QB]))
                    VEC(fvp)
                merged_ve = sVE.n
                av_mm = sMM.n

                def fvr(v, pAV=pAV, merged_ve=merged_ve, av_mm=av_mm):
                    v.wait_ge(sMM.h, av_mm)
                    v.wait_ge(sVE.h, max(merged_ve, sVE.n))
                    sVE.inc(v.tensor_copy(lrow65[64:65, :], pAV[64:65, :]))
                    v.wait_ge(sVE.h, sVE.n)
                    sVE.inc(v.reciprocal(lrow65[64:65, :], lrow65[64:65, :]))
                VEC(fvr)
                ve_r = sVE.n

                def ftl(t, ve_r=ve_r):
                    t.wait_ge(sVE.h, ve_r)
                    sMM.inc(t.matmul(pB[4][0:128, 256:256 + QB], onesat64[64:65, :],
                                     lrow65[64:65, :], start=True, stop=True))
                T(ftl)
                mm_l = sMM.n
                ve_bc = sVE.n

                def fvo(v, pAV=pAV, ve_bc=ve_bc):
                    v.wait_ge(sVE.h, max(ve_bc, sVE.n))
                    sVE.inc(v.tensor_copy(aotmp[0:64, :], pAV[0:64, :]))
                VEC(fvo)
                ve_o = sVE.n
                if h % 2 == 0:
                    def fvd(v, mm_l=mm_l, ve_o=ve_o, hc=hc):
                        v.wait_ge(sMM.h, mm_l)
                        v.wait_ge(sVE.h, max(ve_o, sVE.n))
                        sVE.inc(v.tensor_tensor(AO128[0:64, hc, :], aotmp[0:64, :],
                                                pB[4][0:64, 256:256 + QB], ALU.mult))
                    VEC(fvd)
                else:
                    def fvd(v, mm_l=mm_l, ve_o=ve_o):
                        v.wait_ge(sMM.h, mm_l)
                        v.wait_ge(sVE.h, max(ve_o, sVE.n))
                        sVE.inc(v.tensor_tensor(aotmp[0:64, :], aotmp[0:64, :],
                                                pB[4][0:64, 256:256 + QB], ALU.mult))
                    VEC(fvd)
                    ve_d = sVE.n

                    def ftsh(t, ve_d=ve_d):
                        t.wait_ge(sVE.h, ve_d)
                        sMM.inc(t.matmul(pB[5][0:128, 0:QB], shifteye, aotmp, start=True, stop=True))
                    T(ftsh)
                    mm_sh = sMM.n

                    def fvc(v, mm_sh=mm_sh, hc=hc):
                        v.wait_ge(sMM.h, mm_sh)
                        v.wait_ge(sVE.h, sVE.n)
                        sVE.inc(v.tensor_copy(AO128[64:128, hc, :], pB[5][64:128, 0:QB]))
                    VEC(fvc)
            heads_ve = sVE.n
            qt_free[qslot_t] = sMM.n
            tbi = (qb * QB) // TB
            colofs = (qb * QB) % TB
            for fc in range(DC):
                slot = fc % 2
                ve_war = sVE.n

                def fy(sy, fc=fc, slot=slot, ve_war=ve_war):
                    sy.wait_ge(sVE.h, ve_war)
                    sT[slot].issue(sy, STG[:, slot, 0:QB],
                                   scr_in[fc * 128:(fc + 1) * 128, qsl])
                SY(fy)
                sy_in = sT[slot].n
                po = pB[5][0:128, (fc % 2) * QB:(fc % 2 + 1) * QB]
                ve_war2 = sVE.n

                def ft(t, fc=fc, po=po, heads_ve=heads_ve, ve_war2=ve_war2):
                    t.wait_ge(sVE.h, max(heads_ve, ve_war2))
                    t.wait_ge(sWC.h, wout_at)
                    for ci in range(DC):
                        ins = t.matmul(po, WOUT[:, ci, fc * 128:(fc + 1) * 128],
                                       AO128[:, ci, :], start=(ci == 0), stop=(ci == DC - 1))
                    sMM.inc(ins)
                T(ft)
                mm_at = sMM.n
                bcol = biasp[:, OBO + fc:OBO + fc + 1]

                def fv(v, fc=fc, slot=slot, po=po, mm_at=mm_at, sy_in=sy_in, bcol=bcol):
                    v.wait_ge(sMM.h, mm_at)
                    v.wait_ge(sT[slot].h, sy_in)
                    v.wait_ge(sVE.h, sVE.n)
                    sVE.inc(v.scalar_tensor_tensor(STG[:, slot, 0:QB], po, bcol,
                                                   STG[:, slot, 0:QB], ALU.add, ALU.add))
                VEC(fv)
                ve_evac = sVE.n
                st_mm = emit_stats(STG[:, slot, 0:QB], tbi, colofs, QB,
                                   fc == 0 and (qb * QB) % TB == 0,
                                   fc == DC - 1 and (qb * QB + QB) % TB == 0)

                def fy2(sy, fc=fc, slot=slot, ve_evac=ve_evac):
                    sy.wait_ge(sVE.h, ve_evac)
                    sT[slot].issue(sy, scr_out[fc * 128:(fc + 1) * 128, qsl],
                                   STG[:, slot, 0:QB])
                SY(fy2)
            if (qb * QB + QB) % TB == 0:
                emit_stats_flush(tbi, st_mm)

    def emit_y2():
        yv = 0
        for tbi in range(NTB):
            rows_ve = emit_ln_rows(tbi, sVE.n)
            sl = slice(tbi * TB, (tbi + 1) * TB)
            for ci in range(DC):
                ve_war = sVE.n

                def fy(sy, ci=ci, sl=sl, ve_war=ve_war):
                    sy.wait_ge(sVE.h, ve_war)
                    sT[ci % 2].issue(sy, Y2[:, ci, sl],
                                     scr_x1[ci * 128:(ci + 1) * 128, sl])
                SY(fy)
                sy_at = sT[ci % 2].n
                emit_ln_chunk(tbi, ci, rows_ve, Y2[:, ci, sl], Y2[:, ci, sl],
                              GAM_PLAIN, BET_PLAIN, wait_sy=(sT[ci % 2], sy_at))
                ve_n = sVE.n

                def fy2(sy, ci=ci, sl=sl, ve_n=ve_n):
                    sy.wait_ge(sVE.h, ve_n)
                    sT[2].issue(sy, scr_y2[ci * 128:(ci + 1) * 128, sl],
                                Y2[:, ci, sl])
                SY(fy2)
                for hf in range(NHF):
                    yv = emit_stats(Y2[:, ci, tbi * TB + hf * 256:tbi * TB + (hf + 1) * 256],
                                    tbi, hf * 256, 256, ci == 0 and hf == 0,
                                    ci == DC - 1 and hf == NHF - 1)
            emit_stats_flush(tbi, yv)
        return yv

    def emit_step9():
        for tbi in range(NTB):
            rows_ve = emit_ln_rows(tbi, sVE.n)
            for half in range(TB // QB):
                colofs = half * QB
                qsl = slice(tbi * TB + colofs, tbi * TB + colofs + QB)
                ts_slots = []
                for ci in range(DC):
                    ve_war = sVE.n

                    def fy(sy, ci=ci, qsl=qsl, ve_war=ve_war):
                        sy.wait_ge(sVE.h, ve_war)
                        sT[0].issue(sy, STG[:, 0, 0:QB],
                                    scr_xb[ci * 128:(ci + 1) * 128, qsl])
                        sT[1].issue(sy, STG[:, 1, 0:QB],
                                    scr_x1[ci * 128:(ci + 1) * 128, qsl])
                    SY(fy)
                    sy_at = (sT[0].n, sT[1].n)
                    slot = ci % 2
                    mm_war = sMM.n

                    syl = emit_lnbuf(slot, GAM_LN2, ci, mm_war)
                    pR = pB[slot][0:128, 0:QB]
                    pA = pB[2 + slot][0:128, 0:QB]
                    ve_war2 = sVE.n

                    def ftb(t, slot=slot, syl=syl, rows_ve=rows_ve, colofs=colofs,
                            pR=pR, pA=pA, ve_war2=ve_war2):
                        t.wait_ge(sSC.h, syl)
                        t.wait_ge(sVE.h, max(rows_ve, ve_war2))
                        t.matmul(pR, lnbuf[0:1, slot, :], mrowb[0:1, 0, colofs:colofs + QB],
                                 start=True, stop=True)
                        sMM.inc(t.matmul(pA, lnbuf[0:1, slot, :], mrowb[0:1, 1, colofs:colofs + QB],
                                         start=True, stop=True))
                    T(ftb)
                    mmb = sMM.n
                    bcol = biasp[:, OB_LNB + BET_LN2 + ci:OB_LNB + BET_LN2 + ci + 1]

                    def fvn(v, ci=ci, mmb=mmb, bcol=bcol, qsl=qsl, pR=pR, pA=pA, sy_at=sy_at):
                        v.wait_ge(sMM.h, mmb)
                        v.wait_ge(sT[0].h, sy_at[0])
                        v.wait_ge(sT[1].h, sy_at[1])
                        v.wait_ge(sVE.h, sVE.n)
                        sVE.inc(v.tensor_mul(STG[:, 2, 0:QB], STG[:, 0, 0:QB], pR))
                        v.wait_ge(sVE.h, sVE.n)
                        sVE.inc(v.scalar_tensor_tensor(Z[:, ci, qsl], STG[:, 2, 0:QB], bcol, pA,
                                                       ALU.add, ALU.add))
                        v.wait_ge(sVE.h, sVE.n)
                        sVE.inc(v.tensor_add(STG[:, 1, 0:QB], STG[:, 1, 0:QB], STG[:, 0, 0:QB]))
                    VEC(fvn)
                    ve_s2 = sVE.n
                    for ts in range(QB // 128):
                        tt = (tbi * TB + colofs) // 128 + ts
                        oslot = tt % 2
                        ocol = ci * 128
                        if ci == 0 and ts == 0:
                            ts_slots.append(None)
                        ve_war3 = sVE.n

                        def ftt(t, ts=ts, ve_s2=ve_s2, ve_war3=ve_war3):
                            t.wait_ge(sVE.h, max(ve_s2, ve_war3))
                            sMM.inc(t.transpose(pB[4][0:128, ts * 128:(ts + 1) * 128],
                                                STG[:, 1, ts * 128:(ts + 1) * 128], identf))
                        T(ftt)
                        mmt = sMM.n
                        sy_war = sSY.n

                        def fvt(v, ts=ts, oslot=oslot, ocol=ocol, mmt=mmt, sy_war=sy_war):
                            v.wait_ge(sMM.h, mmt)
                            v.wait_ge(sSY.h, sy_war)
                            v.wait_ge(sVE.h, sVE.n)
                            sVE.inc(v.tensor_copy(OSTG[:, oslot, ocol:ocol + 128],
                                                  pB[4][0:128, ts * 128:(ts + 1) * 128]))
                        VEC(fvt)
                ve_done = sVE.n
                for ts in range(QB // 128):
                    tt = (tbi * TB + colofs) // 128 + ts

                    def fyo(sy, tt=tt, ve_done=ve_done):
                        sy.wait_ge(sVE.h, ve_done)
                        sSY.issue(sy, out_d[tt], OSTG[:, tt % 2, :])
                    SY(fyo)
        s2_done = sSY.n

        for tt in range(TL // 128):
            ve_war = sVE.n

            def ftg(t, tt=tt, ve_war=ve_war):
                t.wait_ge(sVE.h, ve_war)
                t.wait_ge(sSC.h, GATEWB_AT)
                for ci in range(DC):
                    ins = t.matmul(pB[4][0:128, 0:NE], Z[:, ci, tt * 128:(tt + 1) * 128],
                                   gatewb[:, ci, :], start=(ci == 0), stop=(ci == DC - 1))
                sMM.inc(ins)
            T(ftg)
            mmg = sMM.n
            g0 = gts[:, tt, :]

            def fvg(v, mmg=mmg, g0=g0):
                v.wait_ge(sMM.h, mmg)
                v.wait_ge(sVE.h, sVE.n)
                sVE.inc(v.reduce_max(SQ[:, 0:1], pB[4][0:128, 0:NE], mybir.AxisListType.X))
                v.wait_ge(sVE.h, sVE.n)
                sVE.inc(v.tensor_scalar(STG[:, 0, 0:NE], pB[4][0:128, 0:NE], SQ[:, 0:1], None, ALU.is_ge))
                v.wait_ge(sVE.h, sVE.n)
                sVE.inc(v.tensor_scalar(STG[:, 1, 0:NE], STG[:, 0, 0:NE], -1.0e30, None, ALU.mult))
                v.wait_ge(sVE.h, sVE.n)
                sVE.inc(v.tensor_add(STG[:, 1, 0:NE], STG[:, 1, 0:NE], pB[4][0:128, 0:NE]))
                v.wait_ge(sVE.h, sVE.n)
                sVE.inc(v.reduce_max(SQ[:, 1:2], STG[:, 1, 0:NE], mybir.AxisListType.X))
                v.wait_ge(sVE.h, sVE.n)
                sVE.inc(v.tensor_sub(SQ[:, 2:3], SQ[:, 0:1], SQ[:, 1:2]))
            VEC(fvg)
            ve_d = sVE.n

            def fsg(s, ve_d=ve_d):
                s.wait_ge(sVE.h, ve_d)
                s.activation(SQ[:, 3:4], SQ[:, 2:3], AF.Sigmoid)
                s.wait_ge(sSC.h, sSC.n)
                sSC.inc(s.activation(SQ[:, 4:5], SQ[:, 2:3], AF.Sigmoid, scale=-1.0))
            SCL(fsg)
            sc_g = sSC.n

            def fvg5(v, g0=g0, sc_g=sc_g):
                v.wait_ge(sSC.h, sc_g)
                sVE.inc(v.tensor_scalar(STG[:, 2, 0:NE], STG[:, 1, 0:NE], SQ[:, 1:2], None, ALU.is_ge))
                v.wait_ge(sVE.h, sVE.n)
                sVE.inc(v.tensor_scalar(STG[:, 0, 0:NE], STG[:, 0, 0:NE], SQ[:, 3:4], None, ALU.mult))
                v.wait_ge(sVE.h, sVE.n)
                sVE.inc(v.scalar_tensor_tensor(g0, STG[:, 2, 0:NE], SQ[:, 4:5], STG[:, 0, 0:NE],
                                               ALU.mult, ALU.add))
            VEC(fvg5)
            ve_g = sVE.n

            def ftt(t, g0=g0, ve_g=ve_g):
                t.wait_ge(sVE.h, ve_g)
                sMM.inc(t.transpose(pB[2][0:NE, 0:128], g0, identf))
            T(ftt)
            mmt = sMM.n

            def fvt(v, tt=tt, mmt=mmt):
                v.wait_ge(sMM.h, mmt)
                v.wait_ge(sVE.h, sVE.n)
                sVE.inc(v.tensor_copy(gerow[0:NE, tt * 128:(tt + 1) * 128], pB[2][0:NE, 0:128]))
            VEC(fvt)

        if dbg_moe:
            gv_ = sVE.n

            def fdbg2(sy):
                sy.wait_ge(sVE.h, gv_)
                sSY.issue(sy, dbg_gts2, gts.rearrange("p t e -> p (t e)"))
            SY(fdbg2)
        def fyf(sy):
            sy.wait_ge(sMM.h, sMM.n)
            sSY.issue(sy, fc2bs32, d["fc2b"])
        SY(fyf)
        f32_at = sSY.n

        def fyc(s):
            s.wait_ge(sSY.h, f32_at)
            sSC.inc(s.activation(fc2bs[0:NE, :], fc2bs32[0:NE, :], AF.Identity))
        SCL(fyc)
        fc2b_at = sSC.n
        for tt in range(TL // 128):
            ve_war = sVE.n

            def ftb(t, tt=tt, ve_war=ve_war):
                t.wait_ge(sVE.h, ve_war)
                t.wait_ge(sSC.h, fc2b_at)
                for dh in range(NDH):
                    ins = t.matmul(pB[5 - dh][0:128, 0:DH],
                                   gerow[0:NE, tt * 128:(tt + 1) * 128],
                                   fc2bs[0:NE, dh * DH:(dh + 1) * DH], start=True, stop=True)
                sMM.inc(ins)
            T(ftb)
            mmb = sMM.n
            oslot = tt % 2
            os_war = os_last[oslot]

            def fvb(v, mmb=mmb, oslot=oslot, os_war=os_war, s2_done=s2_done):
                v.wait_ge(sMM.h, mmb)
                v.wait_ge(sSY.h, s2_done)
                v.wait_ge(sOS.h, os_war)
                for dh in range(NDH):
                    v.wait_ge(sVE.h, sVE.n)
                    sVE.inc(v.tensor_copy(OSTG[:, oslot, dh * DH:(dh + 1) * DH],
                                          pB[5 - dh][0:128, 0:DH]))
            VEC(fvb)
            veb = sVE.n

            def fgb(g, tt=tt, oslot=oslot, veb=veb, s2_done=s2_done):
                g.wait_ge(sVE.h, veb)
                g.wait_ge(sSY.h, s2_done)
                sOS.issue(g, out_d[tt], OSTG[:, oslot, :], accum_op=ALU.add)
            GP(fgb)
            os_last[oslot] = sOS.n
        return s2_done

    def emit_moe(s2_done):
        fc1q_mm = [sMM.n] * 4
        fc2q_mm = [sMM.n] * 4
        for e in range(NE):
            w1_at, w2_at = [], []
            for q in range(4):
                def fg(g, e=e, q=q, war=fc1q_mm[q]):
                    g.wait_ge(sMM.h, war)
                    sW.issue(g, FC1[:, q].rearrange("p c z -> p (c z)"),
                             d["fc1"][e, q])
                GP(fg)
                w1_at.append(sW.n)
            for q in range(4):
                def fg2(g, e=e, q=q, war=fc2q_mm[q]):
                    g.wait_ge(sMM.h, war)
                    sW.issue(g, FC2[:, q * (FFC // 4):(q + 1) * (FFC // 4), :].rearrange("p k z -> p (k z)"),
                             d["fc2"][e, q])
                GP(fg2)
                w2_at.append(sW.n)
            for tb in range(NMTB):
                msl = slice(tb * MTB, (tb + 1) * MTB)
                ntt = MTB // 128
                FH = FFC // 2
                for half in range(2):
                    base_sc = sSC.n
                    acts = []
                    for Fc2 in range(FH):
                        Fc = half * FH + Fc2
                        po = pB[4 + Fc % 2][0:128, 0:MTB]
                        ve_war = sVE.n
                        gel_war = base_sc + Fc2 - 1 if Fc2 >= 2 else base_sc

                        def ft(t, e=e, tb=tb, Fc=Fc, po=po, gel_war=gel_war, ve_war=ve_war, msl=msl):
                            t.wait_ge(sW.h, w1_at[Fc * 4 // FFC])
                            t.wait_ge(sSC.h, gel_war)
                            if tb == 0 and Fc <= 1:
                                t.wait_ge(sVE.h, ve_war)
                            FQ = FFC // 4
                            for ci in range(DC):
                                ins = t.matmul(po, FC1[:, Fc // FQ, ci, (Fc % FQ) * 128:(Fc % FQ + 1) * 128],
                                               Z[:, ci, msl], start=(ci == 0), stop=(ci == DC - 1))
                            sMM.inc(ins)
                        T(ft)
                        if tb == NMTB - 1:
                            fc1q_mm[Fc * 4 // FFC] = sMM.n
                        mm_at = sMM.n
                        b1 = biasp[:, OB_FC1 + e * FFC + Fc:OB_FC1 + e * FFC + Fc + 1]

                        def fvb1(v, po=po, mm_at=mm_at, b1=b1):
                            v.wait_ge(sMM.h, mm_at)
                            v.wait_ge(sVE.h, sVE.n)
                            sVE.inc(v.tensor_scalar(po, po, b1, None, ALU.add))
                        VEC(fvb1)
                        acts.append((po, h2[:, half, Fc2, :], sVE.n))

                    def fgel(s, acts=acts):
                        for po_, dst_, ve_b_ in acts:
                            s.wait_ge(sVE.h, ve_b_)
                            sSC.inc(s.activation(dst_, po_, AF.Gelu))
                    SCL(fgel)
                    h_done = sSC.n
                    if dbg_moe and e == 0 and tb == 0 and half == 1:
                        def fdbg(sy, h_done=h_done):
                            sy.wait_ge(sSC.h, h_done)
                            sy.wait_ge(sW.h, sW.n)
                            sSY.issue(sy, dbg_nx2, Z.rearrange("p c t -> p (c t)"))
                            sSY.issue(sy, dbg_gts, gts.rearrange("p t e -> p (t e)"))
                            sSY.issue(sy, dbg_h, h2.rearrange("p s f t -> p (s f t)"))
                            sSY.issue(sy, dbg_fc1, FC1.rearrange("p q c f -> p (q c f)"))
                            sSY.issue(sy, dbg_fc2, FC2.rearrange("p k z -> p (k z)"))
                            sSY.issue(sy, dbg_sw0, smallswin)
                        SY(fdbg)
                    if dbg_moe and e == 3 and tb == 0 and half == 1:
                        def fdbg3(sy, h_done=h_done):
                            sy.wait_ge(sSC.h, h_done)
                            sSY.issue(sy, dbg_sw1, smallswin)
                        SY(fdbg3)
                    ve_war2 = sVE.n
                    for Fc2 in range(FH):
                        Fc = half * FH + Fc2

                        def ftb(t, tb=tb, half=half, Fc=Fc, Fc2=Fc2, h_done=h_done,
                                ntt=ntt, ve_war2=ve_war2):
                            t.wait_ge(sSC.h, h_done)
                            t.wait_ge(sW.h, w2_at[Fc * 4 // FFC])
                            if Fc == 0:
                                t.wait_ge(sVE.h, ve_war2)
                            ins = None
                            for tt in range(ntt):
                                for dh in range(NDH):
                                    ins = t.matmul(pB[tt * NDH + dh][0:128, 0:DH],
                                                   h2[:, half, Fc2, tt * 128:(tt + 1) * 128],
                                                   FC2[:, Fc, dh * DH:(dh + 1) * DH],
                                                   start=(Fc == 0), stop=(Fc == FFC - 1))
                            sMM.inc(ins)
                        T(ftb)
                        if tb == NMTB - 1:
                            fc2q_mm[Fc * 4 // FFC] = sMM.n
                mmB = sMM.n
                for tt in range(ntt):
                    ti = tb * ntt + tt
                    gcol = gts[:, ti, e:e + 1]
                    oslot = ti % 2
                    os_war = os_last[oslot]

                    def fv(v, tt=tt, gcol=gcol, oslot=oslot, mmB=mmB, os_war=os_war, s2_done=s2_done):
                        v.wait_ge(sMM.h, mmB)
                        v.wait_ge(sSY.h, s2_done)
                        v.wait_ge(sOS.h, os_war)
                        v.wait_ge(sVE.h, sVE.n)
                        for dh in range(NDH):
                            sVE.inc(v.tensor_scalar(OSTG[:, oslot, dh * DH:(dh + 1) * DH],
                                                    pB[tt * NDH + dh][0:128, 0:DH],
                                                    gcol, None, ALU.mult))
                            if dh + 1 < NDH:
                                v.wait_ge(sVE.h, sVE.n)
                    VEC(fv)
                    vev = sVE.n

                    def fg3(g, ti=ti, oslot=oslot, vev=vev, s2_done=s2_done):
                        g.wait_ge(sVE.h, vev)
                        g.wait_ge(sSY.h, s2_done)
                        sOS.issue(g, out_d[ti], OSTG[:, oslot, :], accum_op=ALU.add)
                    GP(fg3)
                    os_last[oslot] = sOS.n

    def _finish():
        os_total = sOS.n
        sy_total = sSY.n

        def ffinX(sy):
            sy.wait_ge(sOS.h, os_total)
            sy.wait_ge(sSY.h, sy_total)
            sy.wait_ge(sQS.h, sQS.n)
            sy.wait_ge(sQT.h, sQT.n)
            sy.wait_ge(sW.h, sW.n)
            sy.wait_ge(sWC.h, sWC.n)
            sy.wait_ge(sCC.h, sCC.n)
            sy.wait_ge(sRW.h, sRW.n)
            sy.wait_ge(sLN.h, sLN.n)
            for s_ in sT:
                sy.wait_ge(s_.h, s_.n)
        SY(ffinX)
        b.out_names = ["o"]
        return nc, b

    if stop_after == "ln":
        return _finish()
    gc1 = emit_qkv(0)
    qs1_done = sQS.n
    emit_qbound()
    emit_bound(K[:, :, 0:TL], 1)
    bias1_ve = emit_bias_from_bounds(0)
    kv1_sy = emit_exchange(0, gc1)
    sc_w1 = sSC.n

    def fw1(g):
        g.wait_ge(sSC.h, sc_w1)
        sWC.issue(g, WOUT.rearrange("p c f -> p (c f)")[0:64], d["wout1"][0:64])
        sWC.issue(g, WOUT.rearrange("p c f -> p (c f)")[64:128], d["wout1"][64:128])
    GP(fw1)
    if stop_after == "qkv1":
        return _finish()
    emit_attn(0, kv1_sy, bias1_ve, sWC.n, qs_done=qs1_done)

    if stop_after == "attn1":
        return _finish()
    y2_ve = emit_y2()
    if stop_after == "y2a":
        return _finish()
    if False:
        _vals = [(sSY, sSY.n), (sW, sW.n), (sWC, sWC.n), (sRW, sRW.n), (sLN, sLN.n),
                 (sMM, sMM.n), (sSC, sSC.n), (sVE, sVE.n)] + [(s_, s_.n) for s_ in sT]

        def _stall(e):
            for s_, v_ in _vals:
                e.wait_ge(s_.h, v_)
        T(_stall); VEC(_stall); SCL(_stall); GP(_stall); SY(_stall)
    if stop_after == "y2c":
        for tbi in range(NTB):
            sl = slice(tbi * TB, (tbi + 1) * TB)
            for ci in range(DC):
                def fvz(v, ci=ci, sl=sl):
                    v.wait_ge(sVE.h, sVE.n)
                    sVE.inc(v.tensor_copy(Z[:, ci, sl], Y2[:, ci, sl]))
                VEC(fvz)
        return _finish()
    if stop_after == "y2d":
        for tbi in range(NTB):
            emit_ln_rows(tbi, sVE.n)
        return _finish()
    for tbi in range(NTB):
        rows_ve = emit_ln_rows(tbi, y2_ve if tbi == 0 else sVE.n)
        sl = slice(tbi * TB, (tbi + 1) * TB)
        for ci in range(DC):
            emit_ln_chunk(tbi, ci, rows_ve, Y2[:, ci, sl], Z[:, ci, sl], GAM_LN1, BET_LN1)
        if stop_after == "y2b" and tbi == 0:
            return _finish()

    if stop_after == "y2":
        return _finish()
    gc2 = emit_qkv(1, entry_dma_waits=[(s_, s_.n) for s_ in sT])
    qs2_done = sQS.n
    emit_qbound()
    emit_bound(K[:, :, 0:TL], 1)
    bias2_ve = emit_bias_from_bounds(1)
    kv2_sy = emit_exchange(1, gc2)
    sc_w2 = sSC.n

    def fw2(g):
        g.wait_ge(sSC.h, sc_w2)
        sWC.issue(g, WOUT.rearrange("p c f -> p (c f)")[0:64], d["wout2"][0:64])
        sWC.issue(g, WOUT.rearrange("p c f -> p (c f)")[64:128], d["wout2"][64:128])
        sWC.issue(g, masks.rearrange("p j q -> p (j q)")[0:64], d["masks"][0:64])
        sWC.issue(g, masks.rearrange("p j q -> p (j q)")[64:128], d["masks"][64:128])
    GP(fw2)
    if stop_after == "qkv2":
        return _finish()
    emit_attn(1, kv2_sy, bias2_ve, sWC.n, mask_at=sWC.n, qs_done=qs2_done)

    if stop_after == "attn2":
        return _finish()
    s2_done = emit_step9()
    if stop_after == "step9":
        return _finish()
    emit_moe(s2_done)

    os_total = sOS.n
    sy_total = sSY.n

    def ffin(sy):
        sy.wait_ge(sOS.h, os_total)
        sy.wait_ge(sSY.h, sy_total)
        sy.wait_ge(sQS.h, sQS.n)
        sy.wait_ge(sQT.h, sQT.n)
    SY(ffin)
    b.out_names = ["o"]
    return nc, b


def _prep_weights(cfg, inputs):
    c = cfg
    D, DC, FF, FFC, NE = c["D"], c["DC"], c["FF"], c["FFC"], c["NE"]
    FC3 = c["FC3"]

    def slices_layout(w):
        F = w.shape[1]
        nsl = F // 256
        r = w.reshape(DC, 128, nsl, 256).transpose(2, 1, 0, 3)
        return np.ascontiguousarray(r.reshape(nsl, 128, DC * 256))

    def wout_layout(w):
        r = w.reshape(DC, 128, D).transpose(1, 0, 2)
        return np.ascontiguousarray(r.reshape(128, DC * D))

    out = {}
    out["wqkv1"] = slices_layout(np.asarray(inputs["L0_attn_w_qkv"], np.float32))
    out["wqkv2"] = slices_layout(np.asarray(inputs["L0_ffn_Wqkv"], np.float32))
    out["wout1"] = wout_layout(np.asarray(inputs["L0_attn_w_out"], np.float32))
    out["wout2"] = wout_layout(np.asarray(inputs["L0_ffn_Wo"], np.float32))
    fc1 = np.asarray(inputs["L0_ffn_fc1_w"], np.float32)
    fc2 = np.asarray(inputs["L0_ffn_fc2_w"], np.float32)
    a = fc1.transpose(0, 2, 1).reshape(NE, DC, 128, FF).transpose(0, 2, 1, 3)
    a = a.reshape(NE, 128, DC, 4, FF // 4).transpose(0, 3, 1, 2, 4)
    out["fc1"] = np.ascontiguousarray(a.reshape(NE, 4, 128, DC * FF // 4))
    bq = fc2.transpose(0, 2, 1).reshape(NE, FFC, 128, D).transpose(0, 2, 1, 3)
    bq = bq.reshape(NE, 128, 4, FFC // 4, D).transpose(0, 2, 1, 3, 4)
    out["fc2"] = np.ascontiguousarray(bq.reshape(NE, 4, 128, FFC * D // 4))

    NBC = 2 * FC3 + 2 * DC + NE * FFC + 4 * DC
    bp = np.zeros((128, NBC), np.float32)

    def cols(vec):
        return np.asarray(vec, np.float32).reshape(-1, 128).T

    o = 0
    bp[:, o:o + FC3] = cols(inputs["L0_attn_b_qkv"]); o += FC3
    bp[:, o:o + DC] = cols(inputs["L0_attn_b_out"]); o += DC
    bp[:, o:o + FC3] = cols(inputs["L0_ffn_bqkv"]); o += FC3
    bp[:, o:o + DC] = cols(inputs["L0_ffn_bo"]); o += DC
    f1b = np.asarray(inputs["L0_ffn_fc1_b"], np.float32)
    bp[:, o:o + NE * FFC] = np.concatenate([cols(f1b[e]) for e in range(NE)], axis=1)
    o += NE * FFC
    o += DC
    bp[:, o:o + DC] = cols(inputs["L0_attn_ln_beta"]); o += DC
    bp[:, o:o + DC] = cols(inputs["L0_ffn_ln1_b"]); o += DC
    bp[:, o:o + DC] = cols(inputs["L0_ffn_ln2_b"]); o += DC
    out["biasp"] = bp

    lnr = np.zeros((1, 4 * D), np.float32)
    lnr[0, 0:D] = 1.0
    lnr[0, D:2 * D] = np.asarray(inputs["L0_attn_ln_gamma"], np.float32)
    lnr[0, 2 * D:3 * D] = np.asarray(inputs["L0_ffn_ln1_w"], np.float32)
    lnr[0, 3 * D:4 * D] = np.asarray(inputs["L0_ffn_ln2_w"], np.float32)
    out["lnrows"] = lnr
    out["fc2b"] = np.ascontiguousarray(np.asarray(inputs["L0_ffn_fc2_b"], np.float32))
    gw = np.asarray(inputs["L0_ffn_gate_W"], np.float32)
    out["gatew"] = np.ascontiguousarray(gw.reshape(DC, 128, NE).transpose(1, 0, 2).reshape(128, DC * NE))

    out["identb"] = np.eye(128, dtype=ml_dtypes.bfloat16)
    out["identf"] = np.eye(128, dtype=np.float32)
    she = np.zeros((64, 128), ml_dtypes.bfloat16)
    for i2 in range(64):
        she[i2, 64 + i2] = 1
    out["shifteye"] = she
    out["onescol"] = np.ones((128, 1), ml_dtypes.bfloat16)
    out["onesat64"] = np.ones((65, 128), np.float32)
    DCH = cfg["DC"]
    Hh = cfg["H"]
    bd = np.zeros((128, DCH * Hh), np.float32)
    for ci in range(DCH):
        for j in range(2):
            h = 2 * ci + j
            if h < Hh:
                bd[j * 64:(j + 1) * 64, ci * Hh + h] = 1.0
    out["bdiag"] = bd.astype(ml_dtypes.bfloat16)
    return out


def _masks_for(cfg, parity):
    NJ, QB, NMASK = cfg["NJ"], cfg["QB"], cfg["NMASK"]
    tri = np.zeros((NJ, 128, QB), np.float32)
    for j in range(NJ):
        for p in range(128):
            kk = min(p + 128 * j, QB)
            tri[j, p, :kk] = NEG_BIG
    zero = np.zeros((128, QB), np.float32)
    neg = np.full((128, QB), NEG_BIG, np.float32)
    tiles = []
    for j in range(NJ):
        tiles.append(tri[j] if parity == 0 else zero)
    for j in range(NJ):
        tiles.append(neg if parity == 0 else tri[j])
    for j in range(NJ):
        tiles.append(tri[j] if parity == 0 else neg)
    for j in range(NJ):
        tiles.append(zero if parity == 0 else tri[j])
    m = np.stack(tiles, 0)
    return np.ascontiguousarray(m.transpose(1, 0, 2).reshape(128, NMASK * QB)).astype(ml_dtypes.bfloat16)


def _in_maps_for(cfg, inputs, shared):
    c = cfg
    TQ, TL = c["TQ"], c["TL"]
    x = np.asarray(inputs["L0_attn_x"], np.float32)
    quarters = {0: (0, 3), 1: (1, 2)}
    in_maps = []
    for core in range(c["ncores"]):
        bi, par = core // 2, core % 2
        qlo, qhi = quarters[par]
        xb = x[bi]
        xloc = np.concatenate([xb[qlo * TQ:(qlo + 1) * TQ], xb[qhi * TQ:(qhi + 1) * TQ]], 0)
        fc = np.zeros((128, 2), np.float32)
        fc[:, 0] = 0.0 if par == 0 else 1.0
        fc[:, 1] = 1.0 if par == 0 else 0.0
        m = dict(shared)
        m["xfm"] = np.ascontiguousarray(xloc.T)
        m["fcol"] = fc
        m["masks"] = _masks_for(c, par)
        in_maps.append(m)
    return in_maps


def _assemble(cfg, results, Bn):
    c = cfg
    D, TQ, TL, NDH, DH = c["D"], c["TQ"], c["TL"], c["NDH"], c["DH"]
    quarters = {0: (0, 3), 1: (1, 2)}
    out = np.zeros((Bn, c["TG"], D), np.float32)
    for core in range(c["ncores"]):
        bi, par = core // 2, core % 2
        qlo, qhi = quarters[par]
        od = np.asarray(results[core]["o"])
        oloc = od.reshape(TL, D)
        out[bi, qlo * TQ:(qlo + 1) * TQ] = oloc[0:TQ]
        out[bi, qhi * TQ:(qhi + 1) * TQ] = oloc[TQ:TL]
    return out


def kernel(**inputs):
    cfg = make_cfg()
    x = np.asarray(inputs["L0_attn_x"], np.float32)
    shared = _prep_weights(cfg, inputs)
    nc, b = build_program(cfg)
    in_maps = _in_maps_for(cfg, inputs, shared)
    res = run_bass_kernel_spmd(nc, in_maps, core_ids=list(range(cfg["ncores"])))
    return _assemble(cfg, res.results, x.shape[0])


# version 13
# speedup vs baseline: 1.4878x; 1.0854x over previous
import contextlib
import numpy as np
import ml_dtypes

import concourse.bass as bass
import concourse.mybir as mybir
from concourse.bass_utils import run_bass_kernel_spmd

F32 = mybir.dt.float32
F32R = mybir.dt.float32r
BF16 = mybir.dt.bfloat16
AF = mybir.ActivationFunctionType
ALU = mybir.AluOpType

NEG_BIG = -3.0e30


def make_cfg(D=1024, H=16, TG=4096, FF=4096, NE=8, ncores=8):
    c = {}
    c["D"], c["H"], c["TG"], c["FF"], c["NE"], c["ncores"] = D, H, TG, FF, NE, ncores
    c["DC"] = D // 128
    c["HC"] = H // 2
    c["F3"] = 3 * D
    c["FC3"] = 3 * D // 128
    c["TQ"] = TG // 4
    c["TL"] = TG // 2
    c["KC"] = TG // 128
    c["KCS"] = c["TQ"] // 128
    c["FFC"] = FF // 128
    c["TB"] = min(256, c["TL"])
    c["NTB"] = c["TL"] // c["TB"]
    c["QB"] = min(256, c["TQ"])
    c["NQB"] = c["TL"] // c["QB"]
    c["NQBQ"] = c["TQ"] // c["QB"]
    c["NJ"] = c["QB"] // 128
    c["MTB"] = min(256, c["TL"])
    c["NMTB"] = c["TL"] // c["MTB"]
    c["DH"] = min(512, D)
    c["NDH"] = D // c["DH"]
    c["VW"] = 65
    c["NS"] = c["F3"] // 256
    c["NMASK"] = 4 * c["NJ"]
    c["PTN"] = min(16, c["KC"])
    return c


def attn_sched(cfg, att, i):
    KCS, KC, NJ, NQBQ = cfg["KCS"], cfg["KC"], cfg["NJ"], cfg["NQBQ"]
    if att == 0:
        return [(k, None) for k in range(KC)], []
    s0, s1, s2, s3 = 0, KCS, 2 * KCS, 3 * KCS
    if i < NQBQ:
        p = i * NJ
        main = [(s0 + k, None) for k in range(p)] + \
               [(s0 + p + j, 0 * NJ + j) for j in range(NJ)] + \
               [(s2 + p + j, 1 * NJ + j) for j in range(NJ)]
        plist = [s0 + k for k in range(p + NJ, KCS)] + [s2 + k for k in range(p)]
        parts = [(plist, 0)] if plist else []
        return main, parts
    ii = i - NQBQ
    p = ii * NJ
    main = [(s0 + k, None) for k in range(KCS)] + \
           [(s2 + k, None) for k in range(KCS)] + \
           [(s3 + k, None) for k in range(p)] + \
           [(s1 + p + j, 2 * NJ + j) for j in range(NJ)] + \
           [(s3 + p + j, 3 * NJ + j) for j in range(NJ)]
    plist = [s1 + k for k in range(p)] + [s3 + k for k in range(p + NJ, KCS)]
    parts = [(plist, 1)] if plist else []
    return main, parts


class Sem:
    def __init__(self, es, nc, name):
        self.h = es.enter_context(nc.semaphore(name))
        self.n = 0

    def inc(self, ins, k=1):
        ins.then_inc(self.h, k)
        self.n += k
        return self.n

    def issue(self, eng, out, in_, **kw):
        eng.wait_ge(self.h, self.n)
        return self.inc(eng.dma_start(out=out, in_=in_, **kw), 16)


class B:
    pass


def build_program(cfg, stop_after=None, dbg_out=False, dbg_moe=False):
    b = B()
    b.cfg = cfg
    c = cfg
    nc = bass.Bass("TRN2", target_bir_lowering=False, num_devices=c["ncores"],
                   dynamic_dma_scratch_size=16384)
    b.nc = nc
    es = contextlib.ExitStack()
    b.es = es
    D, DC, H, HC, F3, FC3 = c["D"], c["DC"], c["H"], c["HC"], c["F3"], c["FC3"]
    TG, TQ, TL, KC, KCS, VW = c["TG"], c["TQ"], c["TL"], c["KC"], c["KCS"], c["VW"]
    FF, FFC, NE = c["FF"], c["FFC"], c["NE"]
    TB, NTB, QB, NQB, NQBQ, NJ = c["TB"], c["NTB"], c["QB"], c["NQB"], c["NQBQ"], c["NJ"]
    MTB, NMTB, DH, NDH = c["MTB"], c["NMTB"], c["DH"], c["NDH"]
    NS, NMASK, PTN = c["NS"], c["NMASK"], c["PTN"]

    NBC = 2 * FC3 + 2 * DC + NE * FFC + 4 * DC
    OB_QKV1 = 0
    OB_OUT1 = FC3
    OB_QKV2 = FC3 + DC
    OB_OUT2 = 2 * FC3 + DC
    OB_FC1 = 2 * FC3 + 2 * DC
    OB_LNB = 2 * FC3 + 2 * DC + NE * FFC
    BET_PLAIN, BET_ATTN, BET_LN1, BET_LN2 = 0, DC, 2 * DC, 3 * DC
    GAM_PLAIN, GAM_ATTN, GAM_LN1, GAM_LN2 = 0, D, 2 * D, 3 * D

    d = {}
    d["xfm"] = nc.dram_tensor("xfm", [D, TL], F32, kind="ExternalInput").ap()
    d["wqkv1"] = nc.dram_tensor("wqkv1", [NS, 128, DC * 256], F32, kind="ExternalInput").ap()
    d["wqkv2"] = nc.dram_tensor("wqkv2", [NS, 128, DC * 256], F32, kind="ExternalInput").ap()
    d["wout1"] = nc.dram_tensor("wout1", [128, DC * D], F32, kind="ExternalInput").ap()
    d["wout2"] = nc.dram_tensor("wout2", [128, DC * D], F32, kind="ExternalInput").ap()
    d["fc1"] = nc.dram_tensor("fc1", [NE, 4, 128, DC * FF // 4], F32, kind="ExternalInput").ap()
    d["fc2"] = nc.dram_tensor("fc2", [NE, 4, 128, FFC * D // 4], F32, kind="ExternalInput").ap()
    d["biasp"] = nc.dram_tensor("biasp", [128, NBC], F32, kind="ExternalInput").ap()
    d["lnrows"] = nc.dram_tensor("lnrows", [1, 4 * D], F32, kind="ExternalInput").ap()
    d["fc2b"] = nc.dram_tensor("fc2b", [NE, D], F32, kind="ExternalInput").ap()
    d["gatew"] = nc.dram_tensor("gatew", [128, DC * NE], F32, kind="ExternalInput").ap()
    d["identb"] = nc.dram_tensor("identb", [128, 128], BF16, kind="ExternalInput").ap()
    d["identf"] = nc.dram_tensor("identf", [128, 128], F32, kind="ExternalInput").ap()
    d["shifteye"] = nc.dram_tensor("shifteye", [64, 128], BF16, kind="ExternalInput").ap()
    d["masks"] = nc.dram_tensor("masks", [128, NMASK * QB], BF16, kind="ExternalInput").ap()
    d["onescol"] = nc.dram_tensor("onescol", [128, 1], BF16, kind="ExternalInput").ap()
    d["onesat64"] = nc.dram_tensor("onesat64", [65, 128], F32, kind="ExternalInput").ap()
    d["fcol"] = nc.dram_tensor("fcol", [128, 2], F32, kind="ExternalInput").ap()
    d["bdiag"] = nc.dram_tensor("bdiag", [128, DC * H], BF16, kind="ExternalInput").ap()
    out_d = nc.dram_tensor("o", [TL // 128, 128, NDH * DH], F32, kind="ExternalOutput").ap()
    if dbg_moe:
        dbg_nx2 = nc.dram_tensor("dbg_nx2", [128, DC * TL], BF16, kind="ExternalOutput").ap()
        dbg_gts = nc.dram_tensor("dbg_gts", [128, (TL // 128) * NE], F32, kind="ExternalOutput").ap()
        dbg_gts2 = nc.dram_tensor("dbg_gts2", [128, (TL // 128) * NE], F32, kind="ExternalOutput").ap()
        dbg_h = nc.dram_tensor("dbg_h", [128, FFC * MTB], BF16, kind="ExternalOutput").ap()
        dbg_sw0 = nc.dram_tensor("dbg_sw0", [128, 4192], F32, kind="ExternalOutput").ap()
        dbg_sw1 = nc.dram_tensor("dbg_sw1", [128, 4192], F32, kind="ExternalOutput").ap()
        dbg_fc1 = nc.dram_tensor("dbg_fc1", [128, DC * FF], BF16, kind="ExternalOutput").ap()
        dbg_fc2 = nc.dram_tensor("dbg_fc2", [128, FFC * D], BF16, kind="ExternalOutput").ap()
    _scrkind = "ExternalOutput" if dbg_out else "Internal"
    scr_s = nc.dram_tensor("scr_s", [D, TL], F32, kind=_scrkind).ap()
    scr_x1 = nc.dram_tensor("scr_x1", [D, TL], F32, kind=_scrkind).ap()
    scr_y2 = nc.dram_tensor("scr_y2", [D, TL], F32, kind=_scrkind).ap()
    scr_xb = nc.dram_tensor("scr_xb", [D, TL], F32, kind=_scrkind).ap()
    scr_st = nc.dram_tensor("scr_st", [NTB, 2, TB], F32, kind="Internal").ap()
    scr_q = [nc.dram_tensor(f"scr_q{a}", [DC, 128, TL], BF16, kind="Internal").ap()
             for a in range(2)]
    EXW = DC * TL + 2 * KCS * H * VW
    send = [nc.dram_tensor(f"send{a}", [128, EXW], BF16, kind="Internal").ap() for a in range(2)]
    recv = [nc.dram_tensor(f"recv{a}", [2, 128, EXW], BF16, kind="Internal").ap() for a in range(2)]

    import math as _m
    cur = [16896]

    def place(name, shape, dtype, at=None):
        if at is None:
            at = cur[0]
        t = nc.alloc_sbuf_tensor_at(name, shape, dtype, offset=at)
        esz = 2 if dtype == BF16 else 4
        end = at + _m.prod(shape[1:]) * esz if len(shape) > 1 else at + esz
        cur[0] = max(cur[0], (end + 63) // 64 * 64)
        return t.ap()

    smallswin = place("smallswin", [128, 4192], F32, 16896) if dbg_moe else None
    cur[0] = 16896
    identb = place("identb", [128, 128], BF16)
    identf = place("identf", [128, 128], F32)
    shifteye = place("shifteye", [64, 128], BF16)
    onescol = place("onescol", [128, 1], BF16)
    onesat64 = place("onesat64", [65, 128], F32)
    fcol = place("fcol", [128, 2], F32)
    bdiag = place("bdiag", [128, DC, H], BF16)
    biasp = place("biasp", [128, NBC], F32)
    gatewb = place("gatewb", [128, DC, NE], BF16)
    gatewf = place("gatewf", [128, DC, NE], F32)
    lnbuf = place("lnbuf", [1, 2, 128], BF16)
    lnbuf32 = place("lnbuf32", [1, 2, 128], F32)
    mrow = place("mrow", [1, 3, TB], F32)
    mrowb = place("mrowb", [1, 2, TB], BF16)
    XR = place("XR", [128, 2, 256], BF16)
    qkn = place("qkn", [H, 8], F32)
    nbrow = place("nbrow", [1, H], F32)
    bb = place("bb", [128, 2 * H], F32)
    lrow65 = place("lrow65", [65, QB], F32)
    gts = place("gts", [128, TL // 128, NE], F32)
    aotmp = place("aotmp", [64, QB], BF16)
    QS = place("QS", [128, 2, TB], BF16)
    qtmp = place("qtmp", [128, 2], F32)
    qmaxcol = place("qmaxcol", [128, DC], F32)
    qmaxb = place("qmaxb", [128, DC], BF16)
    STR = place("STR", [33, 2, TB], F32)
    STG = place("STG", [128, 3, TB], F32)
    SQ = place("SQ", [128, TB], F32)
    VSTb = SQ.bitcast(BF16).rearrange("p (r t) -> p r t", r=2)
    FS = cur[0]
    WSL = place("WSL", [128, 2, DC, 256], BF16, FS)
    AO128 = place("AO128", [128, HC, QB], BF16, FS)
    OSTG = place("OSTG", [128, 2, NDH * DH], F32, FS)
    cur[0] = (FS + max(2 * DC * 256 * 2, HC * QB * 2, 2 * NDH * DH * 4) + 63) // 64 * 64
    CS = cur[0]
    Qt = place("Qt", [128, 2, DC, QB], BF16, CS)
    gerow = place("gerow", [NE, TL], BF16, CS)
    fc2bs = place("fc2bs", [NE, D], BF16, CS + TL * 2)
    fc2bs32 = place("fc2bs32", [NE, D], F32, CS + TL * 2 + D * 2)
    h2 = place("h2", [128, 2, FFC // 2, MTB], BF16, CS)
    cur[0] = (CS + max(2 * DC * QB * 2, TL * 2 + D * 2 + D * 4, FFC * MTB * 2) + 63) // 64 * 64
    DS = cur[0]
    Z = place("Z", [128, DC, TL], BF16, DS)
    WOUT = place("WOUT", [128, DC, D], BF16, DS)
    PT = place("PT", [128, PTN, QB], BF16, DS + DC * D * 2)
    masks = place("masks", [128, NMASK, QB], BF16, DS + DC * D * 2 + PTN * QB * 2)
    cur[0] = (DS + max(DC * TL * 2, DC * D * 2 + PTN * QB * 2 + NMASK * QB * 2) + 63) // 64 * 64
    AS = cur[0]
    X = place("X", [128, DC, TL], F32, AS)
    K = place("K", [128, DC, TG], BF16, AS)
    Y2 = place("Y2", [128, DC, TL], F32, AS)
    FC1 = place("FC1", [128, 4, DC, FF // 4], BF16, AS)
    cur[0] = (AS + max(DC * TL * 4, DC * TG * 2, DC * FF * 2) + 63) // 64 * 64
    BS = cur[0]
    Y = place("Y", [128, DC, TL], F32, BS)
    V = place("V", [128, KC, H * VW], BF16, BS)
    FC2 = place("FC2", [128, FFC, D], BF16, BS)
    cur[0] = BS + max(DC * TL * 4, KC * H * VW * 2, FFC * D * 2)
    assert cur[0] <= nc.SBUF_PARTITION_SIZE_BYTES, f"SBUF over budget: {cur[0]}"
    b.sbuf_used = cur[0]

    pB = [es.enter_context(nc.psum_tensor(f"pB{i}", [128, 512], F32)) for i in range(6)]
    sA1 = es.enter_context(nc.psum_tensor("sA1", [128, 512], F32))
    sA2 = es.enter_context(nc.psum_tensor("sA2", [128, 512], F32))
    pTRb = pB[3].ap().bitcast(BF16)

    sSY = Sem(es, nc, "sSY")
    sW = Sem(es, nc, "sW")
    sWC = Sem(es, nc, "sWC")
    sCC = Sem(es, nc, "sCC")
    sMM = Sem(es, nc, "sMM")
    sSC = Sem(es, nc, "sSC")
    sVE = Sem(es, nc, "sVE")
    sGC = Sem(es, nc, "sGC")
    sT = [Sem(es, nc, f"sT{i}") for i in range(3)]
    sLN = Sem(es, nc, "sLN")
    sRW = Sem(es, nc, "sRW")
    sOS = Sem(es, nc, "sOS")
    os_last = [0, 0]
    sQS = Sem(es, nc, "sQS")
    sQT = Sem(es, nc, "sQT")
    qt_free = [0, 0]
    blk = es.enter_context(nc.Block())

    def f32r(ap):
        return ap.bitcast(F32R)

    def T(fn): blk.tensor(fn)
    def VEC(fn): blk.vector(fn)
    def SCL(fn): blk.scalar(fn)
    def GP(fn): blk.gpsimd(fn)
    def SY(fn): blk.sync(fn)

    def ph0(sy):
        for name, dst in (("identb", identb), ("identf", identf),
                          ("shifteye", shifteye), ("onescol", onescol),
                          ("onesat64", onesat64), ("fcol", fcol),
                          ("biasp", biasp)):
            sSY.issue(sy, dst, d[name])
        sSY.issue(sy, bdiag, d["bdiag"].rearrange("p (c h) -> p c h", c=DC))
        sSY.issue(sy, X, d["xfm"].rearrange("(c p) t -> p c t", p=128))
    SY(ph0)
    CONSTS = sSY.n

    def ph0g(sy):
        sSY.issue(sy, gatewf, d["gatew"].rearrange("p (c e) -> p c e", c=DC))
    SY(ph0g)
    GATEWF_AT = sSY.n

    def ph0gc(s):
        s.wait_ge(sSY.h, GATEWF_AT)
        sSC.inc(s.activation(gatewb[:, :, :].rearrange("p c e -> p (c e)"),
                             gatewf[:, :, :].rearrange("p c e -> p (c e)"), AF.Identity))
    SCL(ph0gc)
    GATEWB_AT = sSC.n

    def ph0s(s):
        s.wait_ge(sSY.h, CONSTS)
    SCL(ph0s)

    def ph0v(v):
        v.wait_ge(sSY.h, CONSTS)
    VEC(ph0v)

    stat_flush = {}

    def emit_stats(src_ap, tbi, colofs, w, first, last):
        flush_ve = stat_flush.get(tbi % 4, 0) if first else None
        mm_war = sMM.n

        def fs(v):
            v.wait_ge(sVE.h, sVE.n)
            v.wait_ge(sMM.h, mm_war)
            v.tensor_copy(XR[:, 0, 0:w], src_ap)
            sVE.inc(v.tensor_mul(XR[:, 1, 0:w], src_ap, src_ap))
        VEC(fs)
        sc_at = sVE.n
        p0 = 32 * (tbi % 2)
        bc = 256 * ((tbi // 2) % 2)
        a1 = sA1[p0:p0 + 1, bc + colofs:bc + colofs + w]
        a2 = sA2[p0:p0 + 1, bc + colofs:bc + colofs + w]

        def ft(t):
            t.wait_ge(sVE.h, sc_at)
            if first and flush_ve is not None:
                t.wait_ge(sVE.h, flush_ve)
            t.matmul(a1, onescol, XR[:, 0, 0:w], start=first, stop=last)
            sMM.inc(t.matmul(a2, onescol, XR[:, 1, 0:w], start=first, stop=last))
        T(ft)
        return sMM.n

    flush_rw = {}

    def emit_stats_flush(tbi, stats_mm):
        p0 = 32 * (tbi % 2)
        bc = 256 * ((tbi // 2) % 2)
        rw_war = flush_rw.get(p0, 0)

        def fvc(v):
            v.wait_ge(sMM.h, stats_mm)
            v.wait_ge(sRW.h, rw_war)
            v.wait_ge(sVE.h, sVE.n)
            v.tensor_copy(STR[p0:p0 + 1, 0, :], sA1[p0:p0 + 1, bc:bc + TB])
            v.wait_ge(sVE.h, sVE.n)
            sVE.inc(v.tensor_copy(STR[p0:p0 + 1, 1, :], sA2[p0:p0 + 1, bc:bc + TB]))
        VEC(fvc)
        ve_c = sVE.n

        def fy(sy):
            sy.wait_ge(sVE.h, ve_c)
            sRW.issue(sy, scr_st[tbi, 0, :], STR[p0:p0 + 1, 0, :])
            sRW.issue(sy, scr_st[tbi, 1, :], STR[p0:p0 + 1, 1, :])
        SY(fy)
        stat_flush[tbi % 4] = sVE.n
        flush_rw[p0] = sRW.n

    def emit_ln_rows(tbi, stats_ve):
        ivD = 1.0 / D
        mm_war = sMM.n

        def fy(sy):
            sy.wait_ge(sMM.h, mm_war)
            sRW.issue(sy, mrow[0:1, 0, :], scr_st[tbi, 0, :])
            sRW.issue(sy, mrow[0:1, 1, :], scr_st[tbi, 1, :])
        SY(fy)
        sy_at = sRW.n

        def fv(v):
            v.wait_ge(sRW.h, sy_at)
            v.wait_ge(sVE.h, sVE.n)
            v.tensor_scalar(mrow[0:1, 0, :], mrow[0:1, 0, :], ivD, None, ALU.mult)
            v.wait_ge(sVE.h, sVE.n)
            sVE.inc(v.tensor_scalar(mrow[0:1, 1, :], mrow[0:1, 1, :], ivD, None, ALU.mult))
        VEC(fv)
        ve1 = sVE.n

        def fsq(s):
            s.wait_ge(sVE.h, ve1)
            sSC.inc(s.activation(mrow[0:1, 2, :], mrow[0:1, 0, :], AF.Square))
        SCL(fsq)
        sc1 = sSC.n

        def fv2(v):
            v.wait_ge(sSC.h, sc1)
            sVE.inc(v.scalar_tensor_tensor(mrow[0:1, 1, :], mrow[0:1, 1, :], 1e-5,
                                           mrow[0:1, 2, :], ALU.add, ALU.subtract))
        VEC(fv2)
        ve2 = sVE.n

        def fs2(s):
            s.wait_ge(sVE.h, ve2)
            sSC.inc(s.activation(mrow[0:1, 2, :], mrow[0:1, 1, :], AF.Sqrt))
        SCL(fs2)
        sc2 = sSC.n

        def fv3(v):
            v.wait_ge(sSC.h, sc2)
            sVE.inc(v.reciprocal(mrow[0:1, 2, :], mrow[0:1, 2, :]))
            v.wait_ge(sVE.h, sVE.n)
            sVE.inc(v.tensor_mul(mrow[0:1, 0, :], mrow[0:1, 0, :], mrow[0:1, 2, :]))
            v.wait_ge(sVE.h, sVE.n)
            sVE.inc(v.tensor_scalar(mrow[0:1, 0, :], mrow[0:1, 0, :], -1.0, None, ALU.mult))
        VEC(fv3)
        ve3 = sVE.n

        def fs3(s):
            s.wait_ge(sVE.h, ve3)
            s.activation(mrowb[0:1, 0, :], mrow[0:1, 2, :], AF.Identity)
            sSC.inc(s.activation(mrowb[0:1, 1, :], mrow[0:1, 0, :], AF.Identity))
        SCL(fs3)
        rows_sc = sSC.n

        def fv4(v):
            v.wait_ge(sSC.h, rows_sc)
            sVE.inc(v.engine_nop())
        VEC(fv4)
        return sVE.n

    lncast = {}

    def emit_lnbuf(slot, grow, ci, mm_war):
        cast_war = lncast.get(slot, 0)

        def fy(sy):
            sy.wait_ge(sSC.h, cast_war)
            sLN.issue(sy, lnbuf32[0:1, slot, :],
                      d["lnrows"][0:1, grow + ci * 128:grow + (ci + 1) * 128])
        SY(fy)
        ln_at = sLN.n

        def fc(s):
            s.wait_ge(sLN.h, ln_at)
            s.wait_ge(sMM.h, mm_war)
            sSC.inc(s.activation(lnbuf[0:1, slot, :], lnbuf32[0:1, slot, :], AF.Identity))
        SCL(fc)
        lncast[slot] = sSC.n
        return sSC.n

    def emit_ln_chunk(tbi, ci, rows_ve, src_ap, dst_ap, grow, bset, w=None,
                      colofs=0, wait_sy=None):
        w = w or TB
        slot = ci % 2
        mm_war = sMM.n
        sy_at = emit_lnbuf(slot, grow, ci, mm_war)
        pR = pB[slot][0:128, 0:w]
        pA = pB[2 + slot][0:128, 0:w]
        ve_war = sVE.n

        def ft(t):
            t.wait_ge(sSC.h, sy_at)
            t.wait_ge(sVE.h, max(rows_ve, ve_war))
            t.matmul(pR, lnbuf[0:1, slot, :], mrowb[0:1, 0, colofs:colofs + w],
                     start=True, stop=True)
            sMM.inc(t.matmul(pA, lnbuf[0:1, slot, :], mrowb[0:1, 1, colofs:colofs + w],
                             start=True, stop=True))
        T(ft)
        mm_at = sMM.n
        bcol = biasp[:, OB_LNB + bset + ci:OB_LNB + bset + ci + 1]

        def fv(v):
            v.wait_ge(sMM.h, mm_at)
            if wait_sy is not None:
                v.wait_ge(wait_sy[0].h, wait_sy[1])
            v.wait_ge(sVE.h, sVE.n)
            sVE.inc(v.tensor_mul(STG[:, 2, 0:w], src_ap, pR))
            v.wait_ge(sVE.h, sVE.n)
            sVE.inc(v.scalar_tensor_tensor(dst_ap, STG[:, 2, 0:w], bcol, pA, ALU.add, ALU.add))
        VEC(fv)
        return sVE.n

    NHF = TB // 256
    ve_stats = 0
    for tbi in range(NTB):
        for ci in range(DC):
            for hf in range(NHF):
                ve_stats = emit_stats(X[:, ci, tbi * TB + hf * 256:tbi * TB + (hf + 1) * 256],
                                      tbi, hf * 256, 256, ci == 0 and hf == 0,
                                      ci == DC - 1 and hf == NHF - 1)
        emit_stats_flush(tbi, ve_stats)

    ystats = []
    for tbi in range(NTB):
        rows_ve = emit_ln_rows(tbi, ve_stats if tbi == 0 else sVE.n)
        sl = slice(tbi * TB, (tbi + 1) * TB)
        yv = 0
        for ci in range(DC):
            emit_ln_chunk(tbi, ci, rows_ve, X[:, ci, sl], Y[:, ci, sl], GAM_PLAIN, BET_PLAIN)
            slot_war = sT[ci % 2].n

            def fv(v, ci=ci, sl=sl, slot_war=slot_war):
                v.wait_ge(sVE.h, sVE.n)
                v.wait_ge(sT[ci % 2].h, slot_war)
                sVE.inc(v.tensor_add(STG[:, ci % 2, :], X[:, ci, sl], Y[:, ci, sl]))
            VEC(fv)
            ve_s = sVE.n

            def fy(sy, ci=ci, sl=sl, ve_s=ve_s):
                sy.wait_ge(sVE.h, ve_s)
                sT[ci % 2].issue(sy, scr_s[ci * 128:(ci + 1) * 128, sl], STG[:, ci % 2, :])
            SY(fy)
            for hf in range(NHF):
                yv = emit_stats(Y[:, ci, tbi * TB + hf * 256:tbi * TB + (hf + 1) * 256],
                                tbi, hf * 256, 256, ci == 0 and hf == 0,
                                ci == DC - 1 and hf == NHF - 1)
        emit_stats_flush(tbi, yv)
        ystats.append(yv)

    for tbi in range(NTB):
        rows_ve = emit_ln_rows(tbi, ystats[tbi] if tbi == 0 else sVE.n)
        sl = slice(tbi * TB, (tbi + 1) * TB)
        for ci in range(DC):
            emit_ln_chunk(tbi, ci, rows_ve, Y[:, ci, sl], Z[:, ci, sl], GAM_ATTN, BET_ATTN)

    def emit_qkv(att, entry_dma_waits=()):
        wq = d["wqkv1"] if att == 0 else d["wqkv2"]
        OBQ = OB_QKV1 if att == 0 else OB_QKV2
        qs_last = [0, 0]
        po_free = [0, 0]
        ve_free = sVE.n
        mm_free = sMM.n

        def fg0(g):
            g.wait_ge(sVE.h, ve_free)
            g.wait_ge(sMM.h, mm_free)
            sGC.inc(g.memset(V.rearrange("p k (h w) -> p k h w", w=VW)[:, :, :, VW - 1:VW], 1.0))
        GP(fg0)
        gc_ones = sGC.n
        wsl_free = [sMM.n, sMM.n]
        z_ready = sVE.n
        for sl_i in range(NS):
            slot = sl_i % 2

            def fg(g, sl_i=sl_i, slot=slot, war=wsl_free[slot]):
                g.wait_ge(sMM.h, war)
                sW.issue(g, WSL[:, slot, :, :].rearrange("p c f -> p (c f)"),
                         wq[sl_i])
            GP(fg)
            w_at = sW.n
            for fi in range(2):
                fc = 2 * sl_i + fi
                for tbi in range(NTB):
                    pslot = (fc * NTB + tbi) % 2
                    po = pB[4 + pslot][0:128, 0:TB]
                    sc_war = po_free[pslot]

                    def ft(t, slot=slot, fi=fi, tbi=tbi, w_at=w_at, po=po, sc_war=sc_war):
                        t.wait_ge(sW.h, w_at)
                        t.wait_ge(sVE.h, max(z_ready, sc_war))
                        for sem_, val_ in entry_dma_waits:
                            t.wait_ge(sem_.h, val_)
                        for ci in range(DC):
                            ins = t.matmul(po, WSL[:, slot, ci, fi * 128:(fi + 1) * 128],
                                           Z[:, ci, tbi * TB:(tbi + 1) * TB],
                                           start=(ci == 0), stop=(ci == DC - 1))
                        sMM.inc(ins)
                    T(ft)
                    mm_at = sMM.n
                    bcol = biasp[:, OBQ + fc:OBQ + fc + 1]
                    sl2 = slice(tbi * TB, (tbi + 1) * TB)
                    if fc < DC:
                        qslot = (fc * NTB + tbi) % 2
                        qwar = qs_last[qslot]

                        def fv_(v, po=po, mm_at=mm_at, bcol=bcol, qslot=qslot, qwar=qwar,
                                fc=fc, tbi=tbi):
                            v.wait_ge(sMM.h, mm_at)
                            v.wait_ge(sQS.h, qwar)
                            v.wait_ge(sVE.h, sVE.n)
                            sVE.inc(v.tensor_scalar(QS[:, qslot, :], po, bcol, None, ALU.add))
                            v.wait_ge(sVE.h, sVE.n)
                            sVE.inc(v.tensor_mul(XR[:, 1, 0:TB], QS[:, qslot, :],
                                                 QS[:, qslot, :]))
                            v.wait_ge(sVE.h, sVE.n)
                            sVE.inc(v.reduce_max(qtmp[:, 0:1], XR[:, 1, 0:TB],
                                                 mybir.AxisListType.X))
                            v.wait_ge(sVE.h, sVE.n)
                            if tbi == 0:
                                sVE.inc(v.tensor_copy(qmaxcol[:, fc:fc + 1], qtmp[:, 0:1]))
                            else:
                                sVE.inc(v.tensor_max(qmaxcol[:, fc:fc + 1],
                                                     qmaxcol[:, fc:fc + 1], qtmp[:, 0:1]))
                        VEC(fv_)
                        ve_q = sVE.n

                        def fyq(sy, att=att, fc=fc, sl2=sl2, qslot=qslot, ve_q=ve_q):
                            sy.wait_ge(sVE.h, ve_q)
                            sQS.issue(sy, scr_q[att][fc, :, sl2], QS[:, qslot, :])
                        SY(fyq)
                        qs_last[qslot] = sQS.n
                        po_free[pslot] = sVE.n
                    elif fc < 2 * DC:
                        def fv_(v, po=po, mm_at=mm_at, bcol=bcol, fc=fc, sl2=sl2):
                            v.wait_ge(sMM.h, mm_at)
                            v.wait_ge(sVE.h, sVE.n)
                            sVE.inc(v.tensor_scalar(K[:, fc - DC, sl2], po, bcol, None, ALU.add))
                        VEC(fv_)
                        po_free[pslot] = sVE.n
                    else:
                        vslot = (fc * NTB + tbi) % 2

                        def fv_(v, po=po, mm_at=mm_at, bcol=bcol, vslot=vslot):
                            v.wait_ge(sMM.h, mm_at)
                            v.wait_ge(sVE.h, sVE.n)
                            sVE.inc(v.tensor_scalar(VSTb[:, vslot, 0:TB], po, bcol, None, ALU.add))
                        VEC(fv_)
                        sc_at = sVE.n
                        po_free[pslot] = sVE.n
                        vc = fc - 2 * DC
                        for ts in range(TB // 128):
                            tpos = (ts % 4) * 128
                            ve_war2 = sVE.n

                            def ft2(t, vslot=vslot, ts=ts, tpos=tpos, sc_at=sc_at, ve_war2=ve_war2):
                                t.wait_ge(sVE.h, max(sc_at, ve_war2))
                                sMM.inc(t.transpose(pTRb[:, tpos:tpos + 128],
                                                    VSTb[:, vslot, ts * 128:(ts + 1) * 128], identb))
                            T(ft2)
                            mm2 = sMM.n
                            kct = (tbi * TB) // 128 + ts
                            dst = V.rearrange("p k (h w) -> p k h w", w=VW)[:, kct, 2 * vc:2 * vc + 2, 0:64]

                            def fv(v, tpos=tpos, mm2=mm2, dst=dst):
                                v.wait_ge(sMM.h, mm2)
                                v.wait_ge(sVE.h, sVE.n)
                                src = pTRb[:, tpos:tpos + 128].rearrange("p (a f) -> p a f", a=2)
                                sVE.inc(v.tensor_copy(dst, src))
                            VEC(fv)
                    if fi == 1 and tbi == NTB - 1:
                        wsl_free[slot] = sMM.n
        return gc_ones

    def emit_bound(src, col):
        first = [True]
        for tbi in range(NTB):
            for ci in range(DC):
                mm_war = sMM.n

                def fv0(v, ci=ci, tbi=tbi, mm_war=mm_war):
                    v.wait_ge(sVE.h, sVE.n)
                    v.wait_ge(sMM.h, mm_war)
                    s_ = src[:, ci, tbi * TB:(tbi + 1) * TB]
                    sVE.inc(v.tensor_mul(XR[:, 1, 0:TB], s_, s_))
                VEC(fv0)
                sc_at2 = sVE.n

                ve_war2 = sVE.n

                def ft(t, ci=ci, sc_at2=sc_at2, ve_war2=ve_war2):
                    t.wait_ge(sVE.h, max(sc_at2, ve_war2 if ci == 0 else 0))
                    sMM.inc(t.matmul(pB[2][0:H, 0:TB], bdiag[:, ci, :], XR[:, 1, 0:TB],
                                     start=(ci == 0), stop=(ci == DC - 1)))
                T(ft)
            mm_at = sMM.n

            def fv(v, mm_at=mm_at, fr=first[0]):
                v.wait_ge(sMM.h, mm_at)
                v.wait_ge(sVE.h, sVE.n)
                sVE.inc(v.reduce_max(qkn[0:H, 4 + col:5 + col], pB[2][0:H, 0:TB], mybir.AxisListType.X))
                v.wait_ge(sVE.h, sVE.n)
                if fr:
                    sVE.inc(v.tensor_copy(qkn[0:H, col:col + 1], qkn[0:H, 4 + col:5 + col]))
                else:
                    sVE.inc(v.tensor_max(qkn[0:H, col:col + 1], qkn[0:H, col:col + 1],
                                         qkn[0:H, 4 + col:5 + col]))
            VEC(fv)
            first[0] = False
        return sVE.n

    def emit_qbound():
        ve_at = sVE.n

        def fsc(s, ve_at=ve_at):
            s.wait_ge(sVE.h, ve_at)
            s.wait_ge(sSC.h, sSC.n)
            sSC.inc(s.activation(qmaxb[:, :], qmaxcol[:, :], AF.Identity))
        SCL(fsc)
        sc_at = sSC.n
        mm_war = sMM.n

        def ftq(t, sc_at=sc_at, mm_war=mm_war):
            t.wait_ge(sSC.h, sc_at)
            t.wait_ge(sMM.h, mm_war)
            for hc in range(DC):
                ins = t.matmul(pB[2][0:H, 0:1], bdiag[:, hc, :], qmaxb[:, hc:hc + 1],
                               start=(hc == 0), stop=(hc == DC - 1))
            sMM.inc(ins)
        T(ftq)
        mm_at = sMM.n

        def fvq2(v, mm_at=mm_at):
            v.wait_ge(sMM.h, mm_at)
            v.wait_ge(sVE.h, sVE.n)
            sVE.inc(v.tensor_copy(qkn[0:H, 0:1], pB[2][0:H, 0:1]))
        VEC(fvq2)

    def emit_bias_from_bounds(att):
        def fv(v):
            v.wait_ge(sVE.h, sVE.n)
            sVE.inc(v.tensor_mul(qkn[0:H, 2:3], qkn[0:H, 0:1], qkn[0:H, 1:2]))
        VEC(fv)
        ve_at = sVE.n

        def fs(s):
            s.wait_ge(sVE.h, ve_at)
            sSC.inc(s.activation(qkn[0:H, 3:4], qkn[0:H, 2:3], AF.Sqrt))
        SCL(fs)
        sc_at = sSC.n

        def fv2(v):
            v.wait_ge(sSC.h, sc_at)
            sVE.inc(v.tensor_scalar(qkn[0:H, 3:4], qkn[0:H, 3:4], -0.1875, None, ALU.mult))
        VEC(fv2)
        ve2 = sVE.n

        def ft(t):
            t.wait_ge(sVE.h, ve2)
            sMM.inc(t.transpose(pB[2][0:1, 0:H], qkn[0:H, 3:4], identf[0:H, 0:H]))
        T(ft)
        mm1 = sMM.n

        def fv3(v):
            v.wait_ge(sMM.h, mm1)
            v.wait_ge(sVE.h, sVE.n)
            sVE.inc(v.tensor_copy(nbrow[0:1, :], pB[2][0:1, 0:H]))
        VEC(fv3)
        ve3 = sVE.n

        def ft2(t):
            t.wait_ge(sVE.h, ve3)
            sMM.inc(t.matmul(pB[2][0:128, 0:H], onesat64[0:1, :], nbrow[0:1, :],
                             start=True, stop=True))
        T(ft2)
        mm2 = sMM.n

        def fv4(v):
            v.wait_ge(sMM.h, mm2)
            v.wait_ge(sVE.h, sVE.n)
            sVE.inc(v.tensor_copy(bb[:, att * H:(att + 1) * H], pB[2][0:128, 0:H]))
        VEC(fv4)
        return sVE.n

    def emit_exchange(att, gc_ones):
        sd, rc = send[att], recv[att]
        sc_done = sSC.n
        ve_done = sVE.n

        def fy(sy):
            sy.wait_ge(sSC.h, sc_done)
            sy.wait_ge(sVE.h, ve_done)
            sy.wait_ge(sGC.h, gc_ones)
            sSY.issue(sy, sd[:, 0:DC * TL].rearrange("p (c t) -> p c t", c=DC),
                      K[:, :, 0:TL])
            sSY.issue(sy, sd[:, DC * TL:].rearrange("p (k w) -> p k w", k=2 * KCS),
                      V[:, 0:2 * KCS, :])
        SY(fy)
        sy_at = sSY.n

        def fg(g):
            g.wait_ge(sSY.h, sy_at)
            sCC.inc(g.collective_compute(
                "AllGather", ALU.bypass,
                replica_groups=[[2 * i, 2 * i + 1] for i in range(cfg["ncores"] // 2)],
                ins=[sd], outs=[rc]))
        GP(fg)
        cc_at = sCC.n

        def fy2(sy):
            sy.wait_ge(sCC.h, cc_at)
            for r in range(2):
                sSY.issue(sy, K[:, :, r * TL:(r + 1) * TL],
                          rc[r][:, 0:DC * TL].rearrange("p (c t) -> p c t", c=DC))
                sSY.issue(sy, V[:, r * 2 * KCS:(r + 1) * 2 * KCS, :],
                          rc[r][:, DC * TL:].rearrange("p (k w) -> p k w", k=2 * KCS))
        SY(fy2)
        return sSY.n

    def emit_attn(att, kv_ready_sy, bias_ve, wout_at, mask_at=0, qs_done=0):
        scr_in = scr_s if att == 0 else scr_y2
        scr_out = scr_x1 if att == 0 else scr_xb
        OBO = OB_OUT1 if att == 0 else OB_OUT2
        for qb in range(NQB):
            main, parts = attn_sched(cfg, att, qb)
            order = [kc for kc, m in main] + [kc for pl, fx in parts for kc in pl]
            mask_of = {pos: m for pos, (kc, m) in enumerate(main) if m is not None}
            U = len(order)
            nm = len(main)
            qsl = slice(qb * QB, (qb + 1) * QB)
            qslot_t = qb % 2
            war_mm = qt_free[qslot_t]

            def fql(sy, att=att, qsl=qsl, qslot_t=qslot_t, war_mm=war_mm, qs_done=qs_done):
                sy.wait_ge(sQS.h, qs_done)
                sy.wait_ge(sMM.h, war_mm)
                sQT.issue(sy, Qt[:, qslot_t],
                          scr_q[att][:, :, qsl].rearrange("c p t -> p c t"))
            SY(fql)
            qt_load = sQT.n
            for h in range(H):
                hp = (h % 2) * 64
                hc = h // 2
                qh = Qt[hp:hp + 64, qslot_t, hc, 0:QB]
                pAV = pB[2 + h % 2][0:65, 0:QB]
                ve_war0 = sVE.n
                grp_sc = []
                for r0 in range(0, U, PTN):
                    rend = min(r0 + PTN, U)
                    base_sc = sSC.n
                    exp_jobs = []
                    npr = 0
                    for g0 in range(r0, rend, 2):
                        grp = list(range(g0, min(g0 + 2, rend)))
                        war_sc = grp_sc[-2] if len(grp_sc) >= 2 else 0

                        def ft(t, grp=grp, war_sc=war_sc, h=h, hp=hp, hc=hc, qh=qh,
                               qt_load=qt_load):
                            t.wait_ge(sSC.h, war_sc)
                            if grp[0] == 0:
                                t.wait_ge(sSY.h, kv_ready_sy)
                                t.wait_ge(sVE.h, max(bias_ve, ve_war0))
                                if h == 0:
                                    t.wait_ge(sQT.h, qt_load)
                            for gi in grp:
                                kc = order[gi]
                                slot = gi % 4
                                ps = pB[slot // 2][0:128, (slot % 2) * QB:(slot % 2 + 1) * QB]
                                ins = t.matmul(ps, K[hp:hp + 64, hc, kc * 128:(kc + 1) * 128],
                                               qh, start=True, stop=True)
                            sMM.inc(ins)
                        T(ft)
                        mm_at = sMM.n
                        ve_mask = 0
                        if any(gi in mask_of for gi in grp):
                            def fvm(v, grp=grp, mm_at=mm_at):
                                v.wait_ge(sMM.h, mm_at)
                                v.wait_ge(sWC.h, mask_at)
                                v.wait_ge(sVE.h, sVE.n)
                                ins = None
                                for gi in grp:
                                    if gi in mask_of:
                                        slot = gi % 4
                                        ps = pB[slot // 2][0:128, (slot % 2) * QB:(slot % 2 + 1) * QB]
                                        ins = v.tensor_add(ps, ps, masks[:, mask_of[gi], :])
                                sVE.inc(ins)
                            VEC(fvm)
                            ve_mask = sVE.n
                        pairs = []
                        i2 = 0
                        while i2 < len(grp):
                            gi = grp[i2]
                            slot = gi % 4
                            if i2 + 1 < len(grp) and slot % 2 == 0:
                                ps = pB[slot // 2][0:128, 0:2 * QB]
                                dst = PT[:, gi - r0:gi - r0 + 2, 0:QB].rearrange(
                                    "p j q -> p (j q)")
                                pairs.append((dst, ps))
                                i2 += 2
                            else:
                                ps = pB[slot // 2][0:128, (slot % 2) * QB:(slot % 2 + 1) * QB]
                                pairs.append((PT[:, gi - r0, 0:QB], ps))
                                i2 += 1
                        exp_jobs.append((mm_at, ve_mask, pairs))
                        npr += len(pairs)
                        grp_sc.append(base_sc + npr)

                    def fs(s, jobs=exp_jobs, h=h):
                        bcol = bb[:, att * H + h:att * H + h + 1]
                        for mm_at_, ve_mask_, pairs_ in jobs:
                            s.wait_ge(sMM.h, mm_at_)
                            if ve_mask_:
                                s.wait_ge(sVE.h, ve_mask_)
                            for dst_, ps_ in pairs_:
                                sSC.inc(s.activation(dst_, ps_, AF.Exp, bias=bcol,
                                                     scale=0.125))
                    SCL(fs)
                    assert sSC.n == base_sc + npr
                    exp_done = sSC.n
                    mainpos = [i2 for i2 in range(r0, rend) if i2 < nm]
                    partpos = [i2 for i2 in range(r0, rend) if i2 >= nm]
                    if mainpos:
                        def ftav(t, h=h, pAV=pAV, exp_done=exp_done, mainpos=mainpos, r0=r0):
                            t.wait_ge(sSC.h, exp_done)
                            if mainpos[0] == 0:
                                t.wait_ge(sVE.h, ve_war0)
                            ins = None
                            for i2 in mainpos:
                                kc = order[i2]
                                ins = t.matmul(pAV, V[:, kc, 65 * h:65 * h + 65], PT[:, i2 - r0, 0:QB],
                                               start=(i2 == 0), stop=(i2 == nm - 1))
                            sMM.inc(ins)
                        T(ftav)
                    if partpos:
                        def ftp(t, h=h, exp_done=exp_done, partpos=partpos, r0=r0):
                            t.wait_ge(sSC.h, exp_done)
                            if partpos[0] == nm:
                                t.wait_ge(sVE.h, ve_war0)
                            ins = None
                            for i2 in partpos:
                                kc = order[i2]
                                ins = t.matmul(pB[4][0:65, 0:QB], V[:, kc, 65 * h:65 * h + 65],
                                               PT[:, i2 - r0, 0:QB],
                                               start=(i2 == nm), stop=(i2 == U - 1))
                            sMM.inc(ins)
                        T(ftp)
                if U > nm:
                    mmp = sMM.n
                    fx = parts[0][1]

                    def fvp(v, mmp=mmp, fx=fx, pAV=pAV):
                        v.wait_ge(sMM.h, mmp)
                        v.wait_ge(sVE.h, sVE.n)
                        sVE.inc(v.tensor_scalar(lrow65[:, 0:QB], pB[4][0:65, 0:QB],
                                                fcol[0:65, fx:fx + 1], None, ALU.mult))
                        v.wait_ge(sVE.h, sVE.n)
                        sVE.inc(v.tensor_add(pAV, pAV, lrow65[:, 0:QB]))
                    VEC(fvp)
                merged_ve = sVE.n
                av_mm = sMM.n

                def fvr(v, pAV=pAV, merged_ve=merged_ve, av_mm=av_mm):
                    v.wait_ge(sMM.h, av_mm)
                    v.wait_ge(sVE.h, max(merged_ve, sVE.n))
                    sVE.inc(v.tensor_copy(lrow65[64:65, :], pAV[64:65, :]))
                    v.wait_ge(sVE.h, sVE.n)
                    sVE.inc(v.reciprocal(lrow65[64:65, :], lrow65[64:65, :]))
                VEC(fvr)
                ve_r = sVE.n

                def ftl(t, ve_r=ve_r):
                    t.wait_ge(sVE.h, ve_r)
                    sMM.inc(t.matmul(pB[4][0:128, 256:256 + QB], onesat64[64:65, :],
                                     lrow65[64:65, :], start=True, stop=True))
                T(ftl)
                mm_l = sMM.n
                ve_bc = sVE.n

                def fvo(v, pAV=pAV, ve_bc=ve_bc):
                    v.wait_ge(sVE.h, max(ve_bc, sVE.n))
                    sVE.inc(v.tensor_copy(aotmp[0:64, :], pAV[0:64, :]))
                VEC(fvo)
                ve_o = sVE.n
                if h % 2 == 0:
                    def fvd(v, mm_l=mm_l, ve_o=ve_o, hc=hc):
                        v.wait_ge(sMM.h, mm_l)
                        v.wait_ge(sVE.h, max(ve_o, sVE.n))
                        sVE.inc(v.tensor_tensor(AO128[0:64, hc, :], aotmp[0:64, :],
                                                pB[4][0:64, 256:256 + QB], ALU.mult))
                    VEC(fvd)
                else:
                    def fvd(v, mm_l=mm_l, ve_o=ve_o):
                        v.wait_ge(sMM.h, mm_l)
                        v.wait_ge(sVE.h, max(ve_o, sVE.n))
                        sVE.inc(v.tensor_tensor(aotmp[0:64, :], aotmp[0:64, :],
                                                pB[4][0:64, 256:256 + QB], ALU.mult))
                    VEC(fvd)
                    ve_d = sVE.n

                    def ftsh(t, ve_d=ve_d):
                        t.wait_ge(sVE.h, ve_d)
                        sMM.inc(t.matmul(pB[5][0:128, 0:QB], shifteye, aotmp, start=True, stop=True))
                    T(ftsh)
                    mm_sh = sMM.n

                    def fvc(v, mm_sh=mm_sh, hc=hc):
                        v.wait_ge(sMM.h, mm_sh)
                        v.wait_ge(sVE.h, sVE.n)
                        sVE.inc(v.tensor_copy(AO128[64:128, hc, :], pB[5][64:128, 0:QB]))
                    VEC(fvc)
            heads_ve = sVE.n
            qt_free[qslot_t] = sMM.n
            tbi = (qb * QB) // TB
            colofs = (qb * QB) % TB
            for fc in range(DC):
                slot = fc % 2
                ve_war = sVE.n

                def fy(sy, fc=fc, slot=slot, ve_war=ve_war):
                    sy.wait_ge(sVE.h, ve_war)
                    sT[slot].issue(sy, STG[:, slot, 0:QB],
                                   scr_in[fc * 128:(fc + 1) * 128, qsl])
                SY(fy)
                sy_in = sT[slot].n
                po = pB[5][0:128, (fc % 2) * QB:(fc % 2 + 1) * QB]
                ve_war2 = sVE.n

                def ft(t, fc=fc, po=po, heads_ve=heads_ve, ve_war2=ve_war2):
                    t.wait_ge(sVE.h, max(heads_ve, ve_war2))
                    t.wait_ge(sWC.h, wout_at)
                    for ci in range(DC):
                        ins = t.matmul(po, WOUT[:, ci, fc * 128:(fc + 1) * 128],
                                       AO128[:, ci, :], start=(ci == 0), stop=(ci == DC - 1))
                    sMM.inc(ins)
                T(ft)
                mm_at = sMM.n
                bcol = biasp[:, OBO + fc:OBO + fc + 1]

                def fv(v, fc=fc, slot=slot, po=po, mm_at=mm_at, sy_in=sy_in, bcol=bcol):
                    v.wait_ge(sMM.h, mm_at)
                    v.wait_ge(sT[slot].h, sy_in)
                    v.wait_ge(sVE.h, sVE.n)
                    sVE.inc(v.scalar_tensor_tensor(STG[:, slot, 0:QB], po, bcol,
                                                   STG[:, slot, 0:QB], ALU.add, ALU.add))
                VEC(fv)
                ve_evac = sVE.n
                st_mm = emit_stats(STG[:, slot, 0:QB], tbi, colofs, QB,
                                   fc == 0 and (qb * QB) % TB == 0,
                                   fc == DC - 1 and (qb * QB + QB) % TB == 0)

                def fy2(sy, fc=fc, slot=slot, ve_evac=ve_evac):
                    sy.wait_ge(sVE.h, ve_evac)
                    sT[slot].issue(sy, scr_out[fc * 128:(fc + 1) * 128, qsl],
                                   STG[:, slot, 0:QB])
                SY(fy2)
            if (qb * QB + QB) % TB == 0:
                emit_stats_flush(tbi, st_mm)

    def emit_y2():
        yv = 0
        for tbi in range(NTB):
            rows_ve = emit_ln_rows(tbi, sVE.n)
            sl = slice(tbi * TB, (tbi + 1) * TB)
            for ci in range(DC):
                ve_war = sVE.n

                def fy(sy, ci=ci, sl=sl, ve_war=ve_war):
                    sy.wait_ge(sVE.h, ve_war)
                    sT[ci % 2].issue(sy, Y2[:, ci, sl],
                                     scr_x1[ci * 128:(ci + 1) * 128, sl])
                SY(fy)
                sy_at = sT[ci % 2].n
                emit_ln_chunk(tbi, ci, rows_ve, Y2[:, ci, sl], Y2[:, ci, sl],
                              GAM_PLAIN, BET_PLAIN, wait_sy=(sT[ci % 2], sy_at))
                ve_n = sVE.n

                def fy2(sy, ci=ci, sl=sl, ve_n=ve_n):
                    sy.wait_ge(sVE.h, ve_n)
                    sT[2].issue(sy, scr_y2[ci * 128:(ci + 1) * 128, sl],
                                Y2[:, ci, sl])
                SY(fy2)
                for hf in range(NHF):
                    yv = emit_stats(Y2[:, ci, tbi * TB + hf * 256:tbi * TB + (hf + 1) * 256],
                                    tbi, hf * 256, 256, ci == 0 and hf == 0,
                                    ci == DC - 1 and hf == NHF - 1)
            emit_stats_flush(tbi, yv)
        return yv

    def emit_step9():
        for tbi in range(NTB):
            rows_ve = emit_ln_rows(tbi, sVE.n)
            for half in range(TB // QB):
                colofs = half * QB
                qsl = slice(tbi * TB + colofs, tbi * TB + colofs + QB)
                ts_slots = []
                for ci in range(DC):
                    ve_war = sVE.n

                    def fy(sy, ci=ci, qsl=qsl, ve_war=ve_war):
                        sy.wait_ge(sVE.h, ve_war)
                        sT[0].issue(sy, STG[:, 0, 0:QB],
                                    scr_xb[ci * 128:(ci + 1) * 128, qsl])
                        sT[1].issue(sy, STG[:, 1, 0:QB],
                                    scr_x1[ci * 128:(ci + 1) * 128, qsl])
                    SY(fy)
                    sy_at = (sT[0].n, sT[1].n)
                    slot = ci % 2
                    mm_war = sMM.n

                    syl = emit_lnbuf(slot, GAM_LN2, ci, mm_war)
                    pR = pB[slot][0:128, 0:QB]
                    pA = pB[2 + slot][0:128, 0:QB]
                    ve_war2 = sVE.n

                    def ftb(t, slot=slot, syl=syl, rows_ve=rows_ve, colofs=colofs,
                            pR=pR, pA=pA, ve_war2=ve_war2):
                        t.wait_ge(sSC.h, syl)
                        t.wait_ge(sVE.h, max(rows_ve, ve_war2))
                        t.matmul(pR, lnbuf[0:1, slot, :], mrowb[0:1, 0, colofs:colofs + QB],
                                 start=True, stop=True)
                        sMM.inc(t.matmul(pA, lnbuf[0:1, slot, :], mrowb[0:1, 1, colofs:colofs + QB],
                                         start=True, stop=True))
                    T(ftb)
                    mmb = sMM.n
                    bcol = biasp[:, OB_LNB + BET_LN2 + ci:OB_LNB + BET_LN2 + ci + 1]

                    def fvn(v, ci=ci, mmb=mmb, bcol=bcol, qsl=qsl, pR=pR, pA=pA, sy_at=sy_at):
                        v.wait_ge(sMM.h, mmb)
                        v.wait_ge(sT[0].h, sy_at[0])
                        v.wait_ge(sT[1].h, sy_at[1])
                        v.wait_ge(sVE.h, sVE.n)
                        sVE.inc(v.tensor_mul(STG[:, 2, 0:QB], STG[:, 0, 0:QB], pR))
                        v.wait_ge(sVE.h, sVE.n)
                        sVE.inc(v.scalar_tensor_tensor(Z[:, ci, qsl], STG[:, 2, 0:QB], bcol, pA,
                                                       ALU.add, ALU.add))
                        v.wait_ge(sVE.h, sVE.n)
                        sVE.inc(v.tensor_add(STG[:, 1, 0:QB], STG[:, 1, 0:QB], STG[:, 0, 0:QB]))
                    VEC(fvn)
                    ve_s2 = sVE.n
                    for ts in range(QB // 128):
                        tt = (tbi * TB + colofs) // 128 + ts
                        oslot = tt % 2
                        ocol = ci * 128
                        if ci == 0 and ts == 0:
                            ts_slots.append(None)
                        ve_war3 = sVE.n

                        def ftt(t, ts=ts, ve_s2=ve_s2, ve_war3=ve_war3):
                            t.wait_ge(sVE.h, max(ve_s2, ve_war3))
                            sMM.inc(t.transpose(pB[4][0:128, ts * 128:(ts + 1) * 128],
                                                STG[:, 1, ts * 128:(ts + 1) * 128], identf))
                        T(ftt)
                        mmt = sMM.n
                        sy_war = sSY.n

                        def fvt(v, ts=ts, oslot=oslot, ocol=ocol, mmt=mmt, sy_war=sy_war):
                            v.wait_ge(sMM.h, mmt)
                            v.wait_ge(sSY.h, sy_war)
                            v.wait_ge(sVE.h, sVE.n)
                            sVE.inc(v.tensor_copy(OSTG[:, oslot, ocol:ocol + 128],
                                                  pB[4][0:128, ts * 128:(ts + 1) * 128]))
                        VEC(fvt)
                ve_done = sVE.n
                for ts in range(QB // 128):
                    tt = (tbi * TB + colofs) // 128 + ts

                    def fyo(sy, tt=tt, ve_done=ve_done):
                        sy.wait_ge(sVE.h, ve_done)
                        sSY.issue(sy, out_d[tt], OSTG[:, tt % 2, :])
                    SY(fyo)
        s2_done = sSY.n

        for tt in range(TL // 128):
            ve_war = sVE.n

            def ftg(t, tt=tt, ve_war=ve_war):
                t.wait_ge(sVE.h, ve_war)
                t.wait_ge(sSC.h, GATEWB_AT)
                for ci in range(DC):
                    ins = t.matmul(pB[4][0:128, 0:NE], Z[:, ci, tt * 128:(tt + 1) * 128],
                                   gatewb[:, ci, :], start=(ci == 0), stop=(ci == DC - 1))
                sMM.inc(ins)
            T(ftg)
            mmg = sMM.n
            g0 = gts[:, tt, :]

            def fvg(v, mmg=mmg, g0=g0):
                v.wait_ge(sMM.h, mmg)
                v.wait_ge(sVE.h, sVE.n)
                sVE.inc(v.reduce_max(SQ[:, 0:1], pB[4][0:128, 0:NE], mybir.AxisListType.X))
                v.wait_ge(sVE.h, sVE.n)
                sVE.inc(v.tensor_scalar(STG[:, 0, 0:NE], pB[4][0:128, 0:NE], SQ[:, 0:1], None, ALU.is_ge))
                v.wait_ge(sVE.h, sVE.n)
                sVE.inc(v.tensor_scalar(STG[:, 1, 0:NE], STG[:, 0, 0:NE], -1.0e30, None, ALU.mult))
                v.wait_ge(sVE.h, sVE.n)
                sVE.inc(v.tensor_add(STG[:, 1, 0:NE], STG[:, 1, 0:NE], pB[4][0:128, 0:NE]))
                v.wait_ge(sVE.h, sVE.n)
                sVE.inc(v.reduce_max(SQ[:, 1:2], STG[:, 1, 0:NE], mybir.AxisListType.X))
                v.wait_ge(sVE.h, sVE.n)
                sVE.inc(v.tensor_sub(SQ[:, 2:3], SQ[:, 0:1], SQ[:, 1:2]))
            VEC(fvg)
            ve_d = sVE.n

            def fsg(s, ve_d=ve_d):
                s.wait_ge(sVE.h, ve_d)
                s.activation(SQ[:, 3:4], SQ[:, 2:3], AF.Sigmoid)
                s.wait_ge(sSC.h, sSC.n)
                sSC.inc(s.activation(SQ[:, 4:5], SQ[:, 2:3], AF.Sigmoid, scale=-1.0))
            SCL(fsg)
            sc_g = sSC.n

            def fvg5(v, g0=g0, sc_g=sc_g):
                v.wait_ge(sSC.h, sc_g)
                sVE.inc(v.tensor_scalar(STG[:, 2, 0:NE], STG[:, 1, 0:NE], SQ[:, 1:2], None, ALU.is_ge))
                v.wait_ge(sVE.h, sVE.n)
                sVE.inc(v.tensor_scalar(STG[:, 0, 0:NE], STG[:, 0, 0:NE], SQ[:, 3:4], None, ALU.mult))
                v.wait_ge(sVE.h, sVE.n)
                sVE.inc(v.scalar_tensor_tensor(g0, STG[:, 2, 0:NE], SQ[:, 4:5], STG[:, 0, 0:NE],
                                               ALU.mult, ALU.add))
            VEC(fvg5)
            ve_g = sVE.n

            def ftt(t, g0=g0, ve_g=ve_g):
                t.wait_ge(sVE.h, ve_g)
                sMM.inc(t.transpose(pB[2][0:NE, 0:128], g0, identf))
            T(ftt)
            mmt = sMM.n

            def fvt(v, tt=tt, mmt=mmt):
                v.wait_ge(sMM.h, mmt)
                v.wait_ge(sVE.h, sVE.n)
                sVE.inc(v.tensor_copy(gerow[0:NE, tt * 128:(tt + 1) * 128], pB[2][0:NE, 0:128]))
            VEC(fvt)

        if dbg_moe:
            gv_ = sVE.n

            def fdbg2(sy):
                sy.wait_ge(sVE.h, gv_)
                sSY.issue(sy, dbg_gts2, gts.rearrange("p t e -> p (t e)"))
            SY(fdbg2)
        def fyf(sy):
            sy.wait_ge(sMM.h, sMM.n)
            sSY.issue(sy, fc2bs32, d["fc2b"])
        SY(fyf)
        f32_at = sSY.n

        def fyc(s):
            s.wait_ge(sSY.h, f32_at)
            sSC.inc(s.activation(fc2bs[0:NE, :], fc2bs32[0:NE, :], AF.Identity))
        SCL(fyc)
        fc2b_at = sSC.n
        for tt in range(TL // 128):
            ve_war = sVE.n

            def ftb(t, tt=tt, ve_war=ve_war):
                t.wait_ge(sVE.h, ve_war)
                t.wait_ge(sSC.h, fc2b_at)
                for dh in range(NDH):
                    ins = t.matmul(pB[5 - dh][0:128, 0:DH],
                                   gerow[0:NE, tt * 128:(tt + 1) * 128],
                                   fc2bs[0:NE, dh * DH:(dh + 1) * DH], start=True, stop=True)
                sMM.inc(ins)
            T(ftb)
            mmb = sMM.n
            oslot = tt % 2
            os_war = os_last[oslot]

            def fvb(v, mmb=mmb, oslot=oslot, os_war=os_war, s2_done=s2_done):
                v.wait_ge(sMM.h, mmb)
                v.wait_ge(sSY.h, s2_done)
                v.wait_ge(sOS.h, os_war)
                for dh in range(NDH):
                    v.wait_ge(sVE.h, sVE.n)
                    sVE.inc(v.tensor_copy(OSTG[:, oslot, dh * DH:(dh + 1) * DH],
                                          pB[5 - dh][0:128, 0:DH]))
            VEC(fvb)
            veb = sVE.n

            def fgb(g, tt=tt, oslot=oslot, veb=veb, s2_done=s2_done):
                g.wait_ge(sVE.h, veb)
                g.wait_ge(sSY.h, s2_done)
                sOS.issue(g, out_d[tt], OSTG[:, oslot, :], accum_op=ALU.add)
            GP(fgb)
            os_last[oslot] = sOS.n
        return s2_done

    def emit_moe(s2_done):
        fc1q_mm = [sMM.n] * 4
        fc2q_mm = [sMM.n] * 4
        for e in range(NE):
            w1_at, w2_at = [], []
            for q in range(4):
                def fg(g, e=e, q=q, war=fc1q_mm[q]):
                    g.wait_ge(sMM.h, war)
                    sW.issue(g, FC1[:, q].rearrange("p c z -> p (c z)"),
                             d["fc1"][e, q])
                GP(fg)
                w1_at.append(sW.n)
            for q in range(4):
                def fg2(g, e=e, q=q, war=fc2q_mm[q]):
                    g.wait_ge(sMM.h, war)
                    sW.issue(g, FC2[:, q * (FFC // 4):(q + 1) * (FFC // 4), :].rearrange("p k z -> p (k z)"),
                             d["fc2"][e, q])
                GP(fg2)
                w2_at.append(sW.n)
            for tb in range(NMTB):
                msl = slice(tb * MTB, (tb + 1) * MTB)
                ntt = MTB // 128
                FH = FFC // 2
                for half in range(2):
                    base_sc = sSC.n
                    acts = []
                    for Fc2 in range(FH):
                        Fc = half * FH + Fc2
                        po = pB[4 + Fc % 2][0:128, 0:MTB]
                        ve_war = sVE.n
                        gel_war = base_sc + Fc2 - 1 if Fc2 >= 2 else base_sc

                        def ft(t, e=e, tb=tb, Fc=Fc, po=po, gel_war=gel_war, ve_war=ve_war, msl=msl):
                            t.wait_ge(sW.h, w1_at[Fc * 4 // FFC])
                            t.wait_ge(sSC.h, gel_war)
                            if tb == 0 and Fc <= 1:
                                t.wait_ge(sVE.h, ve_war)
                            FQ = FFC // 4
                            for ci in range(DC):
                                ins = t.matmul(po, FC1[:, Fc // FQ, ci, (Fc % FQ) * 128:(Fc % FQ + 1) * 128],
                                               Z[:, ci, msl], start=(ci == 0), stop=(ci == DC - 1))
                            sMM.inc(ins)
                        T(ft)
                        if tb == NMTB - 1:
                            fc1q_mm[Fc * 4 // FFC] = sMM.n
                        mm_at = sMM.n
                        b1 = biasp[:, OB_FC1 + e * FFC + Fc:OB_FC1 + e * FFC + Fc + 1]

                        def fvb1(v, po=po, mm_at=mm_at, b1=b1):
                            v.wait_ge(sMM.h, mm_at)
                            v.wait_ge(sVE.h, sVE.n)
                            sVE.inc(v.tensor_scalar(po, po, b1, None, ALU.add))
                        VEC(fvb1)
                        acts.append((po, h2[:, half, Fc2, :], sVE.n))

                    def fgel(s, acts=acts):
                        for po_, dst_, ve_b_ in acts:
                            s.wait_ge(sVE.h, ve_b_)
                            sSC.inc(s.activation(dst_, po_, AF.Gelu))
                    SCL(fgel)
                    h_done = sSC.n
                    if dbg_moe and e == 0 and tb == 0 and half == 1:
                        def fdbg(sy, h_done=h_done):
                            sy.wait_ge(sSC.h, h_done)
                            sy.wait_ge(sW.h, sW.n)
                            sSY.issue(sy, dbg_nx2, Z.rearrange("p c t -> p (c t)"))
                            sSY.issue(sy, dbg_gts, gts.rearrange("p t e -> p (t e)"))
                            sSY.issue(sy, dbg_h, h2.rearrange("p s f t -> p (s f t)"))
                            sSY.issue(sy, dbg_fc1, FC1.rearrange("p q c f -> p (q c f)"))
                            sSY.issue(sy, dbg_fc2, FC2.rearrange("p k z -> p (k z)"))
                            sSY.issue(sy, dbg_sw0, smallswin)
                        SY(fdbg)
                    if dbg_moe and e == 3 and tb == 0 and half == 1:
                        def fdbg3(sy, h_done=h_done):
                            sy.wait_ge(sSC.h, h_done)
                            sSY.issue(sy, dbg_sw1, smallswin)
                        SY(fdbg3)
                    ve_war2 = sVE.n
                    for Fc2 in range(FH):
                        Fc = half * FH + Fc2

                        def ftb(t, tb=tb, half=half, Fc=Fc, Fc2=Fc2, h_done=h_done,
                                ntt=ntt, ve_war2=ve_war2):
                            t.wait_ge(sSC.h, h_done)
                            t.wait_ge(sW.h, w2_at[Fc * 4 // FFC])
                            if Fc == 0:
                                t.wait_ge(sVE.h, ve_war2)
                            ins = None
                            for tt in range(ntt):
                                for dh in range(NDH):
                                    ins = t.matmul(pB[tt * NDH + dh][0:128, 0:DH],
                                                   h2[:, half, Fc2, tt * 128:(tt + 1) * 128],
                                                   FC2[:, Fc, dh * DH:(dh + 1) * DH],
                                                   start=(Fc == 0), stop=(Fc == FFC - 1))
                            sMM.inc(ins)
                        T(ftb)
                        if tb == NMTB - 1:
                            fc2q_mm[Fc * 4 // FFC] = sMM.n
                mmB = sMM.n
                for tt in range(ntt):
                    ti = tb * ntt + tt
                    gcol = gts[:, ti, e:e + 1]
                    oslot = ti % 2
                    os_war = os_last[oslot]

                    def fv(v, tt=tt, gcol=gcol, oslot=oslot, mmB=mmB, os_war=os_war, s2_done=s2_done):
                        v.wait_ge(sMM.h, mmB)
                        v.wait_ge(sSY.h, s2_done)
                        v.wait_ge(sOS.h, os_war)
                        v.wait_ge(sVE.h, sVE.n)
                        for dh in range(NDH):
                            sVE.inc(v.tensor_scalar(OSTG[:, oslot, dh * DH:(dh + 1) * DH],
                                                    pB[tt * NDH + dh][0:128, 0:DH],
                                                    gcol, None, ALU.mult))
                            if dh + 1 < NDH:
                                v.wait_ge(sVE.h, sVE.n)
                    VEC(fv)
                    vev = sVE.n

                    def fg3(g, ti=ti, oslot=oslot, vev=vev, s2_done=s2_done):
                        g.wait_ge(sVE.h, vev)
                        g.wait_ge(sSY.h, s2_done)
                        sOS.issue(g, out_d[ti], OSTG[:, oslot, :], accum_op=ALU.add)
                    GP(fg3)
                    os_last[oslot] = sOS.n

    def _finish():
        os_total = sOS.n
        sy_total = sSY.n

        def ffinX(sy):
            sy.wait_ge(sOS.h, os_total)
            sy.wait_ge(sSY.h, sy_total)
            sy.wait_ge(sQS.h, sQS.n)
            sy.wait_ge(sQT.h, sQT.n)
            sy.wait_ge(sW.h, sW.n)
            sy.wait_ge(sWC.h, sWC.n)
            sy.wait_ge(sCC.h, sCC.n)
            sy.wait_ge(sRW.h, sRW.n)
            sy.wait_ge(sLN.h, sLN.n)
            for s_ in sT:
                sy.wait_ge(s_.h, s_.n)
        SY(ffinX)
        b.out_names = ["o"]
        return nc, b

    if stop_after == "ln":
        return _finish()
    gc1 = emit_qkv(0)
    qs1_done = sQS.n
    emit_qbound()
    emit_bound(K[:, :, 0:TL], 1)
    bias1_ve = emit_bias_from_bounds(0)
    kv1_sy = emit_exchange(0, gc1)
    sc_w1 = sSC.n

    def fw1(g):
        g.wait_ge(sSC.h, sc_w1)
        sWC.issue(g, WOUT.rearrange("p c f -> p (c f)")[0:64], d["wout1"][0:64])
        sWC.issue(g, WOUT.rearrange("p c f -> p (c f)")[64:128], d["wout1"][64:128])
    GP(fw1)
    if stop_after == "qkv1":
        return _finish()
    emit_attn(0, kv1_sy, bias1_ve, sWC.n, qs_done=qs1_done)

    if stop_after == "attn1":
        return _finish()
    y2_ve = emit_y2()
    if stop_after == "y2a":
        return _finish()
    if False:
        _vals = [(sSY, sSY.n), (sW, sW.n), (sWC, sWC.n), (sRW, sRW.n), (sLN, sLN.n),
                 (sMM, sMM.n), (sSC, sSC.n), (sVE, sVE.n)] + [(s_, s_.n) for s_ in sT]

        def _stall(e):
            for s_, v_ in _vals:
                e.wait_ge(s_.h, v_)
        T(_stall); VEC(_stall); SCL(_stall); GP(_stall); SY(_stall)
    if stop_after == "y2c":
        for tbi in range(NTB):
            sl = slice(tbi * TB, (tbi + 1) * TB)
            for ci in range(DC):
                def fvz(v, ci=ci, sl=sl):
                    v.wait_ge(sVE.h, sVE.n)
                    sVE.inc(v.tensor_copy(Z[:, ci, sl], Y2[:, ci, sl]))
                VEC(fvz)
        return _finish()
    if stop_after == "y2d":
        for tbi in range(NTB):
            emit_ln_rows(tbi, sVE.n)
        return _finish()
    for tbi in range(NTB):
        rows_ve = emit_ln_rows(tbi, y2_ve if tbi == 0 else sVE.n)
        sl = slice(tbi * TB, (tbi + 1) * TB)
        for ci in range(DC):
            emit_ln_chunk(tbi, ci, rows_ve, Y2[:, ci, sl], Z[:, ci, sl], GAM_LN1, BET_LN1)
        if stop_after == "y2b" and tbi == 0:
            return _finish()

    if stop_after == "y2":
        return _finish()
    gc2 = emit_qkv(1, entry_dma_waits=[(s_, s_.n) for s_ in sT])
    qs2_done = sQS.n
    emit_qbound()
    emit_bound(K[:, :, 0:TL], 1)
    bias2_ve = emit_bias_from_bounds(1)
    kv2_sy = emit_exchange(1, gc2)
    sc_w2 = sSC.n

    def fw2(g):
        g.wait_ge(sSC.h, sc_w2)
        sWC.issue(g, WOUT.rearrange("p c f -> p (c f)")[0:64], d["wout2"][0:64])
        sWC.issue(g, WOUT.rearrange("p c f -> p (c f)")[64:128], d["wout2"][64:128])
        sWC.issue(g, masks.rearrange("p j q -> p (j q)")[0:64], d["masks"][0:64])
        sWC.issue(g, masks.rearrange("p j q -> p (j q)")[64:128], d["masks"][64:128])
    GP(fw2)
    if stop_after == "qkv2":
        return _finish()
    emit_attn(1, kv2_sy, bias2_ve, sWC.n, mask_at=sWC.n, qs_done=qs2_done)

    if stop_after == "attn2":
        return _finish()
    s2_done = emit_step9()
    if stop_after == "step9":
        return _finish()
    emit_moe(s2_done)

    os_total = sOS.n
    sy_total = sSY.n

    def ffin(sy):
        sy.wait_ge(sOS.h, os_total)
        sy.wait_ge(sSY.h, sy_total)
        sy.wait_ge(sQS.h, sQS.n)
        sy.wait_ge(sQT.h, sQT.n)
    SY(ffin)
    b.out_names = ["o"]
    return nc, b


def _prep_weights(cfg, inputs):
    c = cfg
    D, DC, FF, FFC, NE = c["D"], c["DC"], c["FF"], c["FFC"], c["NE"]
    FC3 = c["FC3"]

    def slices_layout(w):
        F = w.shape[1]
        nsl = F // 256
        r = w.reshape(DC, 128, nsl, 256).transpose(2, 1, 0, 3)
        return np.ascontiguousarray(r.reshape(nsl, 128, DC * 256))

    def wout_layout(w):
        r = w.reshape(DC, 128, D).transpose(1, 0, 2)
        return np.ascontiguousarray(r.reshape(128, DC * D))

    out = {}
    out["wqkv1"] = slices_layout(np.asarray(inputs["L0_attn_w_qkv"], np.float32))
    out["wqkv2"] = slices_layout(np.asarray(inputs["L0_ffn_Wqkv"], np.float32))
    out["wout1"] = wout_layout(np.asarray(inputs["L0_attn_w_out"], np.float32))
    out["wout2"] = wout_layout(np.asarray(inputs["L0_ffn_Wo"], np.float32))
    fc1 = np.asarray(inputs["L0_ffn_fc1_w"], np.float32)
    fc2 = np.asarray(inputs["L0_ffn_fc2_w"], np.float32)
    a = fc1.transpose(0, 2, 1).reshape(NE, DC, 128, FF).transpose(0, 2, 1, 3)
    a = a.reshape(NE, 128, DC, 4, FF // 4).transpose(0, 3, 1, 2, 4)
    out["fc1"] = np.ascontiguousarray(a.reshape(NE, 4, 128, DC * FF // 4))
    bq = fc2.transpose(0, 2, 1).reshape(NE, FFC, 128, D).transpose(0, 2, 1, 3)
    bq = bq.reshape(NE, 128, 4, FFC // 4, D).transpose(0, 2, 1, 3, 4)
    out["fc2"] = np.ascontiguousarray(bq.reshape(NE, 4, 128, FFC * D // 4))

    NBC = 2 * FC3 + 2 * DC + NE * FFC + 4 * DC
    bp = np.zeros((128, NBC), np.float32)

    def cols(vec):
        return np.asarray(vec, np.float32).reshape(-1, 128).T

    o = 0
    bp[:, o:o + FC3] = cols(inputs["L0_attn_b_qkv"]); o += FC3
    bp[:, o:o + DC] = cols(inputs["L0_attn_b_out"]); o += DC
    bp[:, o:o + FC3] = cols(inputs["L0_ffn_bqkv"]); o += FC3
    bp[:, o:o + DC] = cols(inputs["L0_ffn_bo"]); o += DC
    f1b = np.asarray(inputs["L0_ffn_fc1_b"], np.float32)
    bp[:, o:o + NE * FFC] = np.concatenate([cols(f1b[e]) for e in range(NE)], axis=1)
    o += NE * FFC
    o += DC
    bp[:, o:o + DC] = cols(inputs["L0_attn_ln_beta"]); o += DC
    bp[:, o:o + DC] = cols(inputs["L0_ffn_ln1_b"]); o += DC
    bp[:, o:o + DC] = cols(inputs["L0_ffn_ln2_b"]); o += DC
    out["biasp"] = bp

    lnr = np.zeros((1, 4 * D), np.float32)
    lnr[0, 0:D] = 1.0
    lnr[0, D:2 * D] = np.asarray(inputs["L0_attn_ln_gamma"], np.float32)
    lnr[0, 2 * D:3 * D] = np.asarray(inputs["L0_ffn_ln1_w"], np.float32)
    lnr[0, 3 * D:4 * D] = np.asarray(inputs["L0_ffn_ln2_w"], np.float32)
    out["lnrows"] = lnr
    out["fc2b"] = np.ascontiguousarray(np.asarray(inputs["L0_ffn_fc2_b"], np.float32))
    gw = np.asarray(inputs["L0_ffn_gate_W"], np.float32)
    out["gatew"] = np.ascontiguousarray(gw.reshape(DC, 128, NE).transpose(1, 0, 2).reshape(128, DC * NE))

    out["identb"] = np.eye(128, dtype=ml_dtypes.bfloat16)
    out["identf"] = np.eye(128, dtype=np.float32)
    she = np.zeros((64, 128), ml_dtypes.bfloat16)
    for i2 in range(64):
        she[i2, 64 + i2] = 1
    out["shifteye"] = she
    out["onescol"] = np.ones((128, 1), ml_dtypes.bfloat16)
    out["onesat64"] = np.ones((65, 128), np.float32)
    DCH = cfg["DC"]
    Hh = cfg["H"]
    bd = np.zeros((128, DCH * Hh), np.float32)
    for ci in range(DCH):
        for j in range(2):
            h = 2 * ci + j
            if h < Hh:
                bd[j * 64:(j + 1) * 64, ci * Hh + h] = 1.0
    out["bdiag"] = bd.astype(ml_dtypes.bfloat16)
    return out


def _masks_for(cfg, parity):
    NJ, QB, NMASK = cfg["NJ"], cfg["QB"], cfg["NMASK"]
    tri = np.zeros((NJ, 128, QB), np.float32)
    for j in range(NJ):
        for p in range(128):
            kk = min(p + 128 * j, QB)
            tri[j, p, :kk] = NEG_BIG
    zero = np.zeros((128, QB), np.float32)
    neg = np.full((128, QB), NEG_BIG, np.float32)
    tiles = []
    for j in range(NJ):
        tiles.append(tri[j] if parity == 0 else zero)
    for j in range(NJ):
        tiles.append(neg if parity == 0 else tri[j])
    for j in range(NJ):
        tiles.append(tri[j] if parity == 0 else neg)
    for j in range(NJ):
        tiles.append(zero if parity == 0 else tri[j])
    m = np.stack(tiles, 0)
    return np.ascontiguousarray(m.transpose(1, 0, 2).reshape(128, NMASK * QB)).astype(ml_dtypes.bfloat16)


def _in_maps_for(cfg, inputs, shared):
    c = cfg
    TQ, TL = c["TQ"], c["TL"]
    x = np.asarray(inputs["L0_attn_x"], np.float32)
    quarters = {0: (0, 3), 1: (1, 2)}
    in_maps = []
    for core in range(c["ncores"]):
        bi, par = core // 2, core % 2
        qlo, qhi = quarters[par]
        xb = x[bi]
        xloc = np.concatenate([xb[qlo * TQ:(qlo + 1) * TQ], xb[qhi * TQ:(qhi + 1) * TQ]], 0)
        fc = np.zeros((128, 2), np.float32)
        fc[:, 0] = 0.0 if par == 0 else 1.0
        fc[:, 1] = 1.0 if par == 0 else 0.0
        m = dict(shared)
        m["xfm"] = np.ascontiguousarray(xloc.T)
        m["fcol"] = fc
        m["masks"] = _masks_for(c, par)
        in_maps.append(m)
    return in_maps


def _assemble(cfg, results, Bn):
    c = cfg
    D, TQ, TL, NDH, DH = c["D"], c["TQ"], c["TL"], c["NDH"], c["DH"]
    quarters = {0: (0, 3), 1: (1, 2)}
    out = np.zeros((Bn, c["TG"], D), np.float32)
    for core in range(c["ncores"]):
        bi, par = core // 2, core % 2
        qlo, qhi = quarters[par]
        od = np.asarray(results[core]["o"])
        oloc = od.reshape(TL, D)
        out[bi, qlo * TQ:(qlo + 1) * TQ] = oloc[0:TQ]
        out[bi, qhi * TQ:(qhi + 1) * TQ] = oloc[TQ:TL]
    return out


def kernel(**inputs):
    cfg = make_cfg()
    x = np.asarray(inputs["L0_attn_x"], np.float32)
    shared = _prep_weights(cfg, inputs)
    nc, b = build_program(cfg)
    in_maps = _in_maps_for(cfg, inputs, shared)
    res = run_bass_kernel_spmd(nc, in_maps, core_ids=list(range(cfg["ncores"])))
    return _assemble(cfg, res.results, x.shape[0])


# version 14
# speedup vs baseline: 1.4989x; 1.0075x over previous
import contextlib
import numpy as np
import ml_dtypes

import concourse.bass as bass
import concourse.mybir as mybir
from concourse.bass_utils import run_bass_kernel_spmd

F32 = mybir.dt.float32
F32R = mybir.dt.float32r
BF16 = mybir.dt.bfloat16
AF = mybir.ActivationFunctionType
ALU = mybir.AluOpType

NEG_BIG = -3.0e30


def make_cfg(D=1024, H=16, TG=4096, FF=4096, NE=8, ncores=8):
    c = {}
    c["D"], c["H"], c["TG"], c["FF"], c["NE"], c["ncores"] = D, H, TG, FF, NE, ncores
    c["DC"] = D // 128
    c["HC"] = H // 2
    c["F3"] = 3 * D
    c["FC3"] = 3 * D // 128
    c["TQ"] = TG // 4
    c["TL"] = TG // 2
    c["KC"] = TG // 128
    c["KCS"] = c["TQ"] // 128
    c["FFC"] = FF // 128
    c["TB"] = min(256, c["TL"])
    c["NTB"] = c["TL"] // c["TB"]
    c["QB"] = min(256, c["TQ"])
    c["NQB"] = c["TL"] // c["QB"]
    c["NQBQ"] = c["TQ"] // c["QB"]
    c["NJ"] = c["QB"] // 128
    c["MTB"] = min(256, c["TL"])
    c["NMTB"] = c["TL"] // c["MTB"]
    c["DH"] = min(512, D)
    c["NDH"] = D // c["DH"]
    c["VW"] = 65
    c["NS"] = c["F3"] // 256
    c["NMASK"] = 4 * c["NJ"]
    c["PTN"] = min(16, c["KC"])
    return c


def attn_sched(cfg, att, i):
    KCS, KC, NJ, NQBQ = cfg["KCS"], cfg["KC"], cfg["NJ"], cfg["NQBQ"]
    if att == 0:
        return [(k, None) for k in range(KC)], []
    s0, s1, s2, s3 = 0, KCS, 2 * KCS, 3 * KCS
    if i < NQBQ:
        p = i * NJ
        main = [(s0 + k, None) for k in range(p)] + \
               [(s0 + p + j, 0 * NJ + j) for j in range(NJ)] + \
               [(s2 + p + j, 1 * NJ + j) for j in range(NJ)]
        plist = [s0 + k for k in range(p + NJ, KCS)] + [s2 + k for k in range(p)]
        parts = [(plist, 0)] if plist else []
        return main, parts
    ii = i - NQBQ
    p = ii * NJ
    main = [(s0 + k, None) for k in range(KCS)] + \
           [(s2 + k, None) for k in range(KCS)] + \
           [(s3 + k, None) for k in range(p)] + \
           [(s1 + p + j, 2 * NJ + j) for j in range(NJ)] + \
           [(s3 + p + j, 3 * NJ + j) for j in range(NJ)]
    plist = [s1 + k for k in range(p)] + [s3 + k for k in range(p + NJ, KCS)]
    parts = [(plist, 1)] if plist else []
    return main, parts


class Sem:
    def __init__(self, es, nc, name):
        self.h = es.enter_context(nc.semaphore(name))
        self.n = 0

    def inc(self, ins, k=1):
        ins.then_inc(self.h, k)
        self.n += k
        return self.n

    def issue(self, eng, out, in_, **kw):
        eng.wait_ge(self.h, self.n)
        return self.inc(eng.dma_start(out=out, in_=in_, **kw), 16)


class B:
    pass


def build_program(cfg, stop_after=None, dbg_out=False, dbg_moe=False):
    b = B()
    b.cfg = cfg
    c = cfg
    nc = bass.Bass("TRN2", target_bir_lowering=False, num_devices=c["ncores"],
                   dynamic_dma_scratch_size=16384)
    b.nc = nc
    es = contextlib.ExitStack()
    b.es = es
    D, DC, H, HC, F3, FC3 = c["D"], c["DC"], c["H"], c["HC"], c["F3"], c["FC3"]
    TG, TQ, TL, KC, KCS, VW = c["TG"], c["TQ"], c["TL"], c["KC"], c["KCS"], c["VW"]
    FF, FFC, NE = c["FF"], c["FFC"], c["NE"]
    TB, NTB, QB, NQB, NQBQ, NJ = c["TB"], c["NTB"], c["QB"], c["NQB"], c["NQBQ"], c["NJ"]
    MTB, NMTB, DH, NDH = c["MTB"], c["NMTB"], c["DH"], c["NDH"]
    NS, NMASK, PTN = c["NS"], c["NMASK"], c["PTN"]

    NBC = 2 * FC3 + 2 * DC + NE * FFC + 4 * DC
    OB_QKV1 = 0
    OB_OUT1 = FC3
    OB_QKV2 = FC3 + DC
    OB_OUT2 = 2 * FC3 + DC
    OB_FC1 = 2 * FC3 + 2 * DC
    OB_LNB = 2 * FC3 + 2 * DC + NE * FFC
    BET_PLAIN, BET_ATTN, BET_LN1, BET_LN2 = 0, DC, 2 * DC, 3 * DC
    GAM_PLAIN, GAM_ATTN, GAM_LN1, GAM_LN2 = 0, D, 2 * D, 3 * D

    d = {}
    d["xfm"] = nc.dram_tensor("xfm", [D, TL], F32, kind="ExternalInput").ap()
    d["wqkv1"] = nc.dram_tensor("wqkv1", [NS, 128, DC * 256], F32, kind="ExternalInput").ap()
    d["wqkv2"] = nc.dram_tensor("wqkv2", [NS, 128, DC * 256], F32, kind="ExternalInput").ap()
    d["wout1"] = nc.dram_tensor("wout1", [128, DC * D], F32, kind="ExternalInput").ap()
    d["wout2"] = nc.dram_tensor("wout2", [128, DC * D], F32, kind="ExternalInput").ap()
    d["fc1"] = nc.dram_tensor("fc1", [NE, 4, 128, DC * FF // 4], F32, kind="ExternalInput").ap()
    d["fc2"] = nc.dram_tensor("fc2", [NE, 4, 128, FFC * D // 4], F32, kind="ExternalInput").ap()
    d["biasp"] = nc.dram_tensor("biasp", [128, NBC], F32, kind="ExternalInput").ap()
    d["lnrows"] = nc.dram_tensor("lnrows", [1, 4 * D], F32, kind="ExternalInput").ap()
    d["fc2b"] = nc.dram_tensor("fc2b", [NE, D], F32, kind="ExternalInput").ap()
    d["gatew"] = nc.dram_tensor("gatew", [128, DC * NE], F32, kind="ExternalInput").ap()
    d["identb"] = nc.dram_tensor("identb", [128, 128], BF16, kind="ExternalInput").ap()
    d["identf"] = nc.dram_tensor("identf", [128, 128], F32, kind="ExternalInput").ap()
    d["shifteye"] = nc.dram_tensor("shifteye", [64, 128], BF16, kind="ExternalInput").ap()
    d["masks"] = nc.dram_tensor("masks", [128, NMASK * QB], BF16, kind="ExternalInput").ap()
    d["onescol"] = nc.dram_tensor("onescol", [128, 1], BF16, kind="ExternalInput").ap()
    d["onesat64"] = nc.dram_tensor("onesat64", [65, 128], F32, kind="ExternalInput").ap()
    d["fcol"] = nc.dram_tensor("fcol", [128, 2], F32, kind="ExternalInput").ap()
    d["bdiag"] = nc.dram_tensor("bdiag", [128, DC * H], BF16, kind="ExternalInput").ap()
    out_d = nc.dram_tensor("o", [TL // 128, 128, NDH * DH], F32, kind="ExternalOutput").ap()
    if dbg_moe:
        dbg_nx2 = nc.dram_tensor("dbg_nx2", [128, DC * TL], BF16, kind="ExternalOutput").ap()
        dbg_gts = nc.dram_tensor("dbg_gts", [128, (TL // 128) * NE], F32, kind="ExternalOutput").ap()
        dbg_gts2 = nc.dram_tensor("dbg_gts2", [128, (TL // 128) * NE], F32, kind="ExternalOutput").ap()
        dbg_h = nc.dram_tensor("dbg_h", [128, FFC * MTB], BF16, kind="ExternalOutput").ap()
        dbg_sw0 = nc.dram_tensor("dbg_sw0", [128, 4192], F32, kind="ExternalOutput").ap()
        dbg_sw1 = nc.dram_tensor("dbg_sw1", [128, 4192], F32, kind="ExternalOutput").ap()
        dbg_fc1 = nc.dram_tensor("dbg_fc1", [128, DC * FF], BF16, kind="ExternalOutput").ap()
        dbg_fc2 = nc.dram_tensor("dbg_fc2", [128, FFC * D], BF16, kind="ExternalOutput").ap()
    _scrkind = "ExternalOutput" if dbg_out else "Internal"
    scr_s = nc.dram_tensor("scr_s", [D, TL], F32, kind=_scrkind).ap()
    scr_x1 = nc.dram_tensor("scr_x1", [D, TL], F32, kind=_scrkind).ap()
    scr_y2 = nc.dram_tensor("scr_y2", [D, TL], F32, kind=_scrkind).ap()
    scr_xb = nc.dram_tensor("scr_xb", [D, TL], F32, kind=_scrkind).ap()
    scr_st = nc.dram_tensor("scr_st", [NTB, 2, TB], F32, kind="Internal").ap()
    scr_q = [nc.dram_tensor(f"scr_q{a}", [DC, 128, TL], BF16, kind="Internal").ap()
             for a in range(2)]
    EXW = DC * TL + 2 * KCS * H * VW
    send = [nc.dram_tensor(f"send{a}", [128, EXW], BF16, kind="Internal").ap() for a in range(2)]
    recv = [nc.dram_tensor(f"recv{a}", [2, 128, EXW], BF16, kind="Internal").ap() for a in range(2)]

    import math as _m
    cur = [16896]

    def place(name, shape, dtype, at=None):
        if at is None:
            at = cur[0]
        t = nc.alloc_sbuf_tensor_at(name, shape, dtype, offset=at)
        esz = 2 if dtype == BF16 else 4
        end = at + _m.prod(shape[1:]) * esz if len(shape) > 1 else at + esz
        cur[0] = max(cur[0], (end + 63) // 64 * 64)
        return t.ap()

    smallswin = place("smallswin", [128, 4192], F32, 16896) if dbg_moe else None
    cur[0] = 16896
    identb = place("identb", [128, 128], BF16)
    identf = place("identf", [128, 128], F32)
    shifteye = place("shifteye", [64, 128], BF16)
    onescol = place("onescol", [128, 1], BF16)
    onesat64 = place("onesat64", [65, 128], F32)
    fcol = place("fcol", [128, 2], F32)
    bdiag = place("bdiag", [128, DC, H], BF16)
    biasp = place("biasp", [128, NBC], F32)
    gatewb = place("gatewb", [128, DC, NE], BF16)
    gatewf = place("gatewf", [128, DC, NE], F32)
    lnbuf = place("lnbuf", [1, 2, 128], BF16)
    lnbuf32 = place("lnbuf32", [1, 2, 128], F32)
    mrow = place("mrow", [1, 3, TB], F32)
    mrowb = place("mrowb", [1, 2, TB], BF16)
    XR = place("XR", [128, 2, 256], BF16)
    qkn = place("qkn", [H, 8], F32)
    nbrow = place("nbrow", [1, H], F32)
    bb = place("bb", [128, 2 * H], F32)
    lrow65 = place("lrow65", [65, QB], F32)
    gts = place("gts", [128, TL // 128, NE], F32)
    aotmp = place("aotmp", [64, QB], BF16)
    QS = place("QS", [128, 2, TB], BF16)
    qtmp = place("qtmp", [128, 2], F32)
    qmaxcol = place("qmaxcol", [128, DC], F32)
    qmaxb = place("qmaxb", [128, DC], BF16)
    STR = place("STR", [33, 2, TB], F32)
    STG = place("STG", [128, 3, TB], F32)
    SQ = place("SQ", [128, TB], F32)
    VSTb = SQ.bitcast(BF16).rearrange("p (r t) -> p r t", r=2)
    FS = cur[0]
    WSL = place("WSL", [128, 2, DC, 256], BF16, FS)
    AO128 = place("AO128", [128, HC, QB], BF16, FS)
    OSTG = place("OSTG", [128, 2, NDH * DH], F32, FS)
    cur[0] = (FS + max(2 * DC * 256 * 2, HC * QB * 2, 2 * NDH * DH * 4) + 63) // 64 * 64
    CS = cur[0]
    Qt = place("Qt", [128, 2, DC, QB], BF16, CS)
    gerow = place("gerow", [NE, TL], BF16, CS)
    fc2bs = place("fc2bs", [NE, D], BF16, CS + TL * 2)
    fc2bs32 = place("fc2bs32", [NE, D], F32, CS + TL * 2 + D * 2)
    h2 = place("h2", [128, 2, FFC // 2, MTB], BF16, CS)
    cur[0] = (CS + max(2 * DC * QB * 2, TL * 2 + D * 2 + D * 4, FFC * MTB * 2) + 63) // 64 * 64
    DS = cur[0]
    Z = place("Z", [128, DC, TL], BF16, DS)
    WOUT = place("WOUT", [128, DC, D], BF16, DS)
    PT = place("PT", [128, PTN, QB], BF16, DS + DC * D * 2)
    masks = place("masks", [128, NMASK, QB], BF16, DS + DC * D * 2 + PTN * QB * 2)
    cur[0] = (DS + max(DC * TL * 2, DC * D * 2 + PTN * QB * 2 + NMASK * QB * 2) + 63) // 64 * 64
    AS = cur[0]
    X = place("X", [128, DC, TL], F32, AS)
    K = place("K", [128, DC, TG], BF16, AS)
    Y2 = place("Y2", [128, DC, TL], F32, AS)
    FC1 = place("FC1", [128, 4, DC, FF // 4], BF16, AS)
    cur[0] = (AS + max(DC * TL * 4, DC * TG * 2, DC * FF * 2) + 63) // 64 * 64
    BS = cur[0]
    Y = place("Y", [128, DC, TL], F32, BS)
    V = place("V", [128, KC, H * VW], BF16, BS)
    FC2 = place("FC2", [128, FFC, D], BF16, BS)
    cur[0] = BS + max(DC * TL * 4, KC * H * VW * 2, FFC * D * 2)
    assert cur[0] <= nc.SBUF_PARTITION_SIZE_BYTES, f"SBUF over budget: {cur[0]}"
    b.sbuf_used = cur[0]

    pB = [es.enter_context(nc.psum_tensor(f"pB{i}", [128, 512], F32)) for i in range(6)]
    sA1 = es.enter_context(nc.psum_tensor("sA1", [128, 512], F32))
    sA2 = es.enter_context(nc.psum_tensor("sA2", [128, 512], F32))
    pTRb = pB[3].ap().bitcast(BF16)

    sSY = Sem(es, nc, "sSY")
    sW = Sem(es, nc, "sW")
    sWC = Sem(es, nc, "sWC")
    sCC = Sem(es, nc, "sCC")
    sMM = Sem(es, nc, "sMM")
    sSC = Sem(es, nc, "sSC")
    sVE = Sem(es, nc, "sVE")
    sGC = Sem(es, nc, "sGC")
    sT = [Sem(es, nc, f"sT{i}") for i in range(3)]
    sLN = Sem(es, nc, "sLN")
    sRW = Sem(es, nc, "sRW")
    sOS = Sem(es, nc, "sOS")
    os_last = [0, 0]
    sQS = Sem(es, nc, "sQS")
    sQT = Sem(es, nc, "sQT")
    qt_free = [0, 0]
    blk = es.enter_context(nc.Block())

    def f32r(ap):
        return ap.bitcast(F32R)

    def T(fn): blk.tensor(fn)
    def VEC(fn): blk.vector(fn)
    def SCL(fn): blk.scalar(fn)
    def GP(fn): blk.gpsimd(fn)
    def SY(fn): blk.sync(fn)

    def ph0(sy):
        for name, dst in (("identb", identb), ("identf", identf),
                          ("shifteye", shifteye), ("onescol", onescol),
                          ("onesat64", onesat64), ("fcol", fcol),
                          ("biasp", biasp)):
            sSY.issue(sy, dst, d[name])
        sSY.issue(sy, bdiag, d["bdiag"].rearrange("p (c h) -> p c h", c=DC))
        sSY.issue(sy, X, d["xfm"].rearrange("(c p) t -> p c t", p=128))
    SY(ph0)
    CONSTS = sSY.n

    def ph0g(sy):
        sSY.issue(sy, gatewf, d["gatew"].rearrange("p (c e) -> p c e", c=DC))
    SY(ph0g)
    GATEWF_AT = sSY.n

    def ph0gc(s):
        s.wait_ge(sSY.h, GATEWF_AT)
        sSC.inc(s.activation(gatewb[:, :, :].rearrange("p c e -> p (c e)"),
                             gatewf[:, :, :].rearrange("p c e -> p (c e)"), AF.Identity))
    SCL(ph0gc)
    GATEWB_AT = sSC.n

    def ph0s(s):
        s.wait_ge(sSY.h, CONSTS)
    SCL(ph0s)

    def ph0v(v):
        v.wait_ge(sSY.h, CONSTS)
    VEC(ph0v)

    stat_flush = {}

    def emit_stats(src_ap, tbi, colofs, w, first, last):
        flush_ve = stat_flush.get(tbi % 4, 0) if first else None
        mm_war = sMM.n

        def fs(v):
            v.wait_ge(sVE.h, sVE.n)
            v.wait_ge(sMM.h, mm_war)
            v.tensor_copy(XR[:, 0, 0:w], src_ap)
            sVE.inc(v.tensor_mul(XR[:, 1, 0:w], src_ap, src_ap))
        VEC(fs)
        sc_at = sVE.n
        p0 = 32 * (tbi % 2)
        bc = 256 * ((tbi // 2) % 2)
        a1 = sA1[p0:p0 + 1, bc + colofs:bc + colofs + w]
        a2 = sA2[p0:p0 + 1, bc + colofs:bc + colofs + w]

        def ft(t):
            t.wait_ge(sVE.h, sc_at)
            if first and flush_ve is not None:
                t.wait_ge(sVE.h, flush_ve)
            t.matmul(a1, onescol, XR[:, 0, 0:w], start=first, stop=last)
            sMM.inc(t.matmul(a2, onescol, XR[:, 1, 0:w], start=first, stop=last))
        T(ft)
        return sMM.n

    flush_rw = {}

    def emit_stats_flush(tbi, stats_mm):
        p0 = 32 * (tbi % 2)
        bc = 256 * ((tbi // 2) % 2)
        rw_war = flush_rw.get(p0, 0)

        def fvc(v):
            v.wait_ge(sMM.h, stats_mm)
            v.wait_ge(sRW.h, rw_war)
            v.wait_ge(sVE.h, sVE.n)
            v.tensor_copy(STR[p0:p0 + 1, 0, :], sA1[p0:p0 + 1, bc:bc + TB])
            v.wait_ge(sVE.h, sVE.n)
            sVE.inc(v.tensor_copy(STR[p0:p0 + 1, 1, :], sA2[p0:p0 + 1, bc:bc + TB]))
        VEC(fvc)
        ve_c = sVE.n

        def fy(sy):
            sy.wait_ge(sVE.h, ve_c)
            sRW.issue(sy, scr_st[tbi, 0, :], STR[p0:p0 + 1, 0, :])
            sRW.issue(sy, scr_st[tbi, 1, :], STR[p0:p0 + 1, 1, :])
        SY(fy)
        stat_flush[tbi % 4] = sVE.n
        flush_rw[p0] = sRW.n

    def emit_ln_rows(tbi, stats_ve):
        ivD = 1.0 / D
        mm_war = sMM.n

        def fy(sy):
            sy.wait_ge(sMM.h, mm_war)
            sRW.issue(sy, mrow[0:1, 0, :], scr_st[tbi, 0, :])
            sRW.issue(sy, mrow[0:1, 1, :], scr_st[tbi, 1, :])
        SY(fy)
        sy_at = sRW.n

        def fv(v):
            v.wait_ge(sRW.h, sy_at)
            v.wait_ge(sVE.h, sVE.n)
            v.tensor_scalar(mrow[0:1, 0, :], mrow[0:1, 0, :], ivD, None, ALU.mult)
            v.wait_ge(sVE.h, sVE.n)
            sVE.inc(v.tensor_scalar(mrow[0:1, 1, :], mrow[0:1, 1, :], ivD, None, ALU.mult))
        VEC(fv)
        ve1 = sVE.n

        def fsq(s):
            s.wait_ge(sVE.h, ve1)
            sSC.inc(s.activation(mrow[0:1, 2, :], mrow[0:1, 0, :], AF.Square))
        SCL(fsq)
        sc1 = sSC.n

        def fv2(v):
            v.wait_ge(sSC.h, sc1)
            sVE.inc(v.scalar_tensor_tensor(mrow[0:1, 1, :], mrow[0:1, 1, :], 1e-5,
                                           mrow[0:1, 2, :], ALU.add, ALU.subtract))
        VEC(fv2)
        ve2 = sVE.n

        def fs2(s):
            s.wait_ge(sVE.h, ve2)
            sSC.inc(s.activation(mrow[0:1, 2, :], mrow[0:1, 1, :], AF.Sqrt))
        SCL(fs2)
        sc2 = sSC.n

        def fv3(v):
            v.wait_ge(sSC.h, sc2)
            sVE.inc(v.reciprocal(mrow[0:1, 2, :], mrow[0:1, 2, :]))
            v.wait_ge(sVE.h, sVE.n)
            sVE.inc(v.tensor_mul(mrow[0:1, 0, :], mrow[0:1, 0, :], mrow[0:1, 2, :]))
            v.wait_ge(sVE.h, sVE.n)
            sVE.inc(v.tensor_scalar(mrow[0:1, 0, :], mrow[0:1, 0, :], -1.0, None, ALU.mult))
        VEC(fv3)
        ve3 = sVE.n

        def fs3(s):
            s.wait_ge(sVE.h, ve3)
            s.activation(mrowb[0:1, 0, :], mrow[0:1, 2, :], AF.Identity)
            sSC.inc(s.activation(mrowb[0:1, 1, :], mrow[0:1, 0, :], AF.Identity))
        SCL(fs3)
        rows_sc = sSC.n

        def fv4(v):
            v.wait_ge(sSC.h, rows_sc)
            sVE.inc(v.engine_nop())
        VEC(fv4)
        return sVE.n

    lncast = {}

    def emit_lnbuf(slot, grow, ci, mm_war):
        cast_war = lncast.get(slot, 0)

        def fy(sy):
            sy.wait_ge(sSC.h, cast_war)
            sLN.issue(sy, lnbuf32[0:1, slot, :],
                      d["lnrows"][0:1, grow + ci * 128:grow + (ci + 1) * 128])
        SY(fy)
        ln_at = sLN.n

        def fc(s):
            s.wait_ge(sLN.h, ln_at)
            s.wait_ge(sMM.h, mm_war)
            sSC.inc(s.activation(lnbuf[0:1, slot, :], lnbuf32[0:1, slot, :], AF.Identity))
        SCL(fc)
        lncast[slot] = sSC.n
        return sSC.n

    def emit_ln_chunk(tbi, ci, rows_ve, src_ap, dst_ap, grow, bset, w=None,
                      colofs=0, wait_sy=None):
        w = w or TB
        slot = ci % 2
        mm_war = sMM.n
        sy_at = emit_lnbuf(slot, grow, ci, mm_war)
        pR = pB[slot][0:128, 0:w]
        pA = pB[2 + slot][0:128, 0:w]
        ve_war = sVE.n

        def ft(t):
            t.wait_ge(sSC.h, sy_at)
            t.wait_ge(sVE.h, max(rows_ve, ve_war))
            t.matmul(pR, lnbuf[0:1, slot, :], mrowb[0:1, 0, colofs:colofs + w],
                     start=True, stop=True)
            sMM.inc(t.matmul(pA, lnbuf[0:1, slot, :], mrowb[0:1, 1, colofs:colofs + w],
                             start=True, stop=True))
        T(ft)
        mm_at = sMM.n
        bcol = biasp[:, OB_LNB + bset + ci:OB_LNB + bset + ci + 1]

        def fv(v):
            v.wait_ge(sMM.h, mm_at)
            if wait_sy is not None:
                v.wait_ge(wait_sy[0].h, wait_sy[1])
            v.wait_ge(sVE.h, sVE.n)
            sVE.inc(v.tensor_mul(STG[:, 2, 0:w], src_ap, pR))
            v.wait_ge(sVE.h, sVE.n)
            sVE.inc(v.scalar_tensor_tensor(dst_ap, STG[:, 2, 0:w], bcol, pA, ALU.add, ALU.add))
        VEC(fv)
        return sVE.n

    NHF = TB // 256
    ve_stats = 0
    for tbi in range(NTB):
        for ci in range(DC):
            for hf in range(NHF):
                ve_stats = emit_stats(X[:, ci, tbi * TB + hf * 256:tbi * TB + (hf + 1) * 256],
                                      tbi, hf * 256, 256, ci == 0 and hf == 0,
                                      ci == DC - 1 and hf == NHF - 1)
        emit_stats_flush(tbi, ve_stats)

    ystats = []
    for tbi in range(NTB):
        rows_ve = emit_ln_rows(tbi, ve_stats if tbi == 0 else sVE.n)
        sl = slice(tbi * TB, (tbi + 1) * TB)
        yv = 0
        for ci in range(DC):
            emit_ln_chunk(tbi, ci, rows_ve, X[:, ci, sl], Y[:, ci, sl], GAM_PLAIN, BET_PLAIN)
            slot_war = sT[ci % 2].n

            def fv(v, ci=ci, sl=sl, slot_war=slot_war):
                v.wait_ge(sVE.h, sVE.n)
                v.wait_ge(sT[ci % 2].h, slot_war)
                sVE.inc(v.tensor_add(STG[:, ci % 2, :], X[:, ci, sl], Y[:, ci, sl]))
            VEC(fv)
            ve_s = sVE.n

            def fy(sy, ci=ci, sl=sl, ve_s=ve_s):
                sy.wait_ge(sVE.h, ve_s)
                sT[ci % 2].issue(sy, scr_s[ci * 128:(ci + 1) * 128, sl], STG[:, ci % 2, :])
            SY(fy)
            for hf in range(NHF):
                yv = emit_stats(Y[:, ci, tbi * TB + hf * 256:tbi * TB + (hf + 1) * 256],
                                tbi, hf * 256, 256, ci == 0 and hf == 0,
                                ci == DC - 1 and hf == NHF - 1)
        emit_stats_flush(tbi, yv)
        ystats.append(yv)

    for tbi in range(NTB):
        rows_ve = emit_ln_rows(tbi, ystats[tbi] if tbi == 0 else sVE.n)
        sl = slice(tbi * TB, (tbi + 1) * TB)
        for ci in range(DC):
            emit_ln_chunk(tbi, ci, rows_ve, Y[:, ci, sl], Z[:, ci, sl], GAM_ATTN, BET_ATTN)

    def emit_qkv(att, entry_dma_waits=(), mid_cb=None):
        wq = d["wqkv1"] if att == 0 else d["wqkv2"]
        OBQ = OB_QKV1 if att == 0 else OB_QKV2
        qs_last = [0, 0]
        po_free = [0, 0]
        ve_free = sVE.n
        mm_free = sMM.n

        def fg0(g):
            g.wait_ge(sVE.h, ve_free)
            g.wait_ge(sMM.h, mm_free)
            sGC.inc(g.memset(V.rearrange("p k (h w) -> p k h w", w=VW)[:, :, :, VW - 1:VW], 1.0))
        GP(fg0)
        gc_ones = sGC.n
        wsl_free = [sMM.n, sMM.n]
        z_ready = sVE.n
        nq = DC // 2
        sl_order = list(range(nq, NS)) + list(range(nq))
        for sl_pos, sl_i in enumerate(sl_order):
            if sl_pos == NS - nq and mid_cb is not None:
                mid_cb(gc_ones)
            slot = sl_pos % 2

            def fg(g, sl_i=sl_i, slot=slot, war=wsl_free[slot]):
                g.wait_ge(sMM.h, war)
                sW.issue(g, WSL[:, slot, :, :].rearrange("p c f -> p (c f)"),
                         wq[sl_i])
            GP(fg)
            w_at = sW.n
            for fi in range(2):
                fc = 2 * sl_i + fi
                for tbi in range(NTB):
                    pslot = (fc * NTB + tbi) % 2
                    po = pB[4 + pslot][0:128, 0:TB]
                    sc_war = po_free[pslot]

                    def ft(t, slot=slot, fi=fi, tbi=tbi, w_at=w_at, po=po, sc_war=sc_war):
                        t.wait_ge(sW.h, w_at)
                        t.wait_ge(sVE.h, max(z_ready, sc_war))
                        for sem_, val_ in entry_dma_waits:
                            t.wait_ge(sem_.h, val_)
                        for ci in range(DC):
                            ins = t.matmul(po, WSL[:, slot, ci, fi * 128:(fi + 1) * 128],
                                           Z[:, ci, tbi * TB:(tbi + 1) * TB],
                                           start=(ci == 0), stop=(ci == DC - 1))
                        sMM.inc(ins)
                    T(ft)
                    mm_at = sMM.n
                    bcol = biasp[:, OBQ + fc:OBQ + fc + 1]
                    sl2 = slice(tbi * TB, (tbi + 1) * TB)
                    if fc < DC:
                        qslot = (fc * NTB + tbi) % 2
                        qwar = qs_last[qslot]

                        def fv_(v, po=po, mm_at=mm_at, bcol=bcol, qslot=qslot, qwar=qwar,
                                fc=fc, tbi=tbi):
                            v.wait_ge(sMM.h, mm_at)
                            v.wait_ge(sQS.h, qwar)
                            v.wait_ge(sVE.h, sVE.n)
                            sVE.inc(v.tensor_scalar(QS[:, qslot, :], po, bcol, None, ALU.add))
                            v.wait_ge(sVE.h, sVE.n)
                            sVE.inc(v.tensor_mul(XR[:, 1, 0:TB], QS[:, qslot, :],
                                                 QS[:, qslot, :]))
                            v.wait_ge(sVE.h, sVE.n)
                            sVE.inc(v.reduce_max(qtmp[:, 0:1], XR[:, 1, 0:TB],
                                                 mybir.AxisListType.X))
                            v.wait_ge(sVE.h, sVE.n)
                            if tbi == 0:
                                sVE.inc(v.tensor_copy(qmaxcol[:, fc:fc + 1], qtmp[:, 0:1]))
                            else:
                                sVE.inc(v.tensor_max(qmaxcol[:, fc:fc + 1],
                                                     qmaxcol[:, fc:fc + 1], qtmp[:, 0:1]))
                        VEC(fv_)
                        ve_q = sVE.n

                        def fyq(sy, att=att, fc=fc, sl2=sl2, qslot=qslot, ve_q=ve_q):
                            sy.wait_ge(sVE.h, ve_q)
                            sQS.issue(sy, scr_q[att][fc, :, sl2], QS[:, qslot, :])
                        SY(fyq)
                        qs_last[qslot] = sQS.n
                        po_free[pslot] = sVE.n
                    elif fc < 2 * DC:
                        def fv_(v, po=po, mm_at=mm_at, bcol=bcol, fc=fc, sl2=sl2):
                            v.wait_ge(sMM.h, mm_at)
                            v.wait_ge(sVE.h, sVE.n)
                            sVE.inc(v.tensor_scalar(K[:, fc - DC, sl2], po, bcol, None, ALU.add))
                        VEC(fv_)
                        po_free[pslot] = sVE.n
                    else:
                        vslot = (fc * NTB + tbi) % 2

                        def fv_(v, po=po, mm_at=mm_at, bcol=bcol, vslot=vslot):
                            v.wait_ge(sMM.h, mm_at)
                            v.wait_ge(sVE.h, sVE.n)
                            sVE.inc(v.tensor_scalar(VSTb[:, vslot, 0:TB], po, bcol, None, ALU.add))
                        VEC(fv_)
                        sc_at = sVE.n
                        po_free[pslot] = sVE.n
                        vc = fc - 2 * DC
                        for ts in range(TB // 128):
                            tpos = (ts % 4) * 128
                            ve_war2 = sVE.n

                            def ft2(t, vslot=vslot, ts=ts, tpos=tpos, sc_at=sc_at, ve_war2=ve_war2):
                                t.wait_ge(sVE.h, max(sc_at, ve_war2))
                                sMM.inc(t.transpose(pTRb[:, tpos:tpos + 128],
                                                    VSTb[:, vslot, ts * 128:(ts + 1) * 128], identb))
                            T(ft2)
                            mm2 = sMM.n
                            kct = (tbi * TB) // 128 + ts
                            dst = V.rearrange("p k (h w) -> p k h w", w=VW)[:, kct, 2 * vc:2 * vc + 2, 0:64]

                            def fv(v, tpos=tpos, mm2=mm2, dst=dst):
                                v.wait_ge(sMM.h, mm2)
                                v.wait_ge(sVE.h, sVE.n)
                                src = pTRb[:, tpos:tpos + 128].rearrange("p (a f) -> p a f", a=2)
                                sVE.inc(v.tensor_copy(dst, src))
                            VEC(fv)
                    if fi == 1 and tbi == NTB - 1:
                        wsl_free[slot] = sMM.n
        return gc_ones

    def emit_bound(src, col, entry_sy=0):
        first = [True]
        for tbi in range(NTB):
            for ci in range(DC):
                mm_war = sMM.n

                def fv0(v, ci=ci, tbi=tbi, mm_war=mm_war):
                    v.wait_ge(sVE.h, sVE.n)
                    v.wait_ge(sMM.h, mm_war)
                    v.wait_ge(sSY.h, entry_sy)
                    s_ = src[:, ci, tbi * TB:(tbi + 1) * TB]
                    sVE.inc(v.tensor_mul(XR[:, 1, 0:TB], s_, s_))
                VEC(fv0)
                sc_at2 = sVE.n

                ve_war2 = sVE.n

                def ft(t, ci=ci, sc_at2=sc_at2, ve_war2=ve_war2):
                    t.wait_ge(sVE.h, max(sc_at2, ve_war2 if ci == 0 else 0))
                    sMM.inc(t.matmul(pB[2][0:H, 0:TB], bdiag[:, ci, :], XR[:, 1, 0:TB],
                                     start=(ci == 0), stop=(ci == DC - 1)))
                T(ft)
            mm_at = sMM.n

            def fv(v, mm_at=mm_at, fr=first[0]):
                v.wait_ge(sMM.h, mm_at)
                v.wait_ge(sVE.h, sVE.n)
                sVE.inc(v.reduce_max(qkn[0:H, 4 + col:5 + col], pB[2][0:H, 0:TB], mybir.AxisListType.X))
                v.wait_ge(sVE.h, sVE.n)
                if fr:
                    sVE.inc(v.tensor_copy(qkn[0:H, col:col + 1], qkn[0:H, 4 + col:5 + col]))
                else:
                    sVE.inc(v.tensor_max(qkn[0:H, col:col + 1], qkn[0:H, col:col + 1],
                                         qkn[0:H, 4 + col:5 + col]))
            VEC(fv)
            first[0] = False
        return sVE.n

    def emit_qbound():
        ve_at = sVE.n

        def fsc(s, ve_at=ve_at):
            s.wait_ge(sVE.h, ve_at)
            s.wait_ge(sSC.h, sSC.n)
            sSC.inc(s.activation(qmaxb[:, :], qmaxcol[:, :], AF.Identity))
        SCL(fsc)
        sc_at = sSC.n
        mm_war = sMM.n

        def ftq(t, sc_at=sc_at, mm_war=mm_war):
            t.wait_ge(sSC.h, sc_at)
            t.wait_ge(sMM.h, mm_war)
            for hc in range(DC):
                ins = t.matmul(pB[2][0:H, 0:1], bdiag[:, hc, :], qmaxb[:, hc:hc + 1],
                               start=(hc == 0), stop=(hc == DC - 1))
            sMM.inc(ins)
        T(ftq)
        mm_at = sMM.n

        def fvq2(v, mm_at=mm_at):
            v.wait_ge(sMM.h, mm_at)
            v.wait_ge(sVE.h, sVE.n)
            sVE.inc(v.tensor_copy(qkn[0:H, 0:1], pB[2][0:H, 0:1]))
        VEC(fvq2)

    def emit_bias_from_bounds(att):
        def fv(v):
            v.wait_ge(sVE.h, sVE.n)
            sVE.inc(v.tensor_mul(qkn[0:H, 2:3], qkn[0:H, 0:1], qkn[0:H, 1:2]))
        VEC(fv)
        ve_at = sVE.n

        def fs(s):
            s.wait_ge(sVE.h, ve_at)
            sSC.inc(s.activation(qkn[0:H, 3:4], qkn[0:H, 2:3], AF.Sqrt))
        SCL(fs)
        sc_at = sSC.n

        def fv2(v):
            v.wait_ge(sSC.h, sc_at)
            sVE.inc(v.tensor_scalar(qkn[0:H, 3:4], qkn[0:H, 3:4], -0.1875, None, ALU.mult))
        VEC(fv2)
        ve2 = sVE.n

        def ft(t):
            t.wait_ge(sVE.h, ve2)
            sMM.inc(t.transpose(pB[2][0:1, 0:H], qkn[0:H, 3:4], identf[0:H, 0:H]))
        T(ft)
        mm1 = sMM.n

        def fv3(v):
            v.wait_ge(sMM.h, mm1)
            v.wait_ge(sVE.h, sVE.n)
            sVE.inc(v.tensor_copy(nbrow[0:1, :], pB[2][0:1, 0:H]))
        VEC(fv3)
        ve3 = sVE.n

        def ft2(t):
            t.wait_ge(sVE.h, ve3)
            sMM.inc(t.matmul(pB[2][0:128, 0:H], onesat64[0:1, :], nbrow[0:1, :],
                             start=True, stop=True))
        T(ft2)
        mm2 = sMM.n

        def fv4(v):
            v.wait_ge(sMM.h, mm2)
            v.wait_ge(sVE.h, sVE.n)
            sVE.inc(v.tensor_copy(bb[:, att * H:(att + 1) * H], pB[2][0:128, 0:H]))
        VEC(fv4)
        return sVE.n

    exch_sy_at = {}

    def emit_exchange_send(att, gc_ones):
        sd = send[att]
        sc_done = sSC.n
        ve_done = sVE.n

        def fy(sy):
            sy.wait_ge(sSC.h, sc_done)
            sy.wait_ge(sVE.h, ve_done)
            sy.wait_ge(sGC.h, gc_ones)
            sSY.issue(sy, sd[:, 0:DC * TL].rearrange("p (c t) -> p c t", c=DC),
                      K[:, :, 0:TL])
            sSY.issue(sy, sd[:, DC * TL:].rearrange("p (k w) -> p k w", k=2 * KCS),
                      V[:, 0:2 * KCS, :])
        SY(fy)
        exch_sy_at[att] = sSY.n

    def emit_exchange_finish(att):
        sd, rc = send[att], recv[att]
        sy_at = exch_sy_at[att]

        def fg(g):
            g.wait_ge(sSY.h, sy_at)
            sCC.inc(g.collective_compute(
                "AllGather", ALU.bypass,
                replica_groups=[[2 * i, 2 * i + 1] for i in range(cfg["ncores"] // 2)],
                ins=[sd], outs=[rc]))
        GP(fg)
        cc_at = sCC.n

        def fy2(sy):
            sy.wait_ge(sCC.h, cc_at)
            for r in range(2):
                sSY.issue(sy, K[:, :, r * TL:(r + 1) * TL],
                          rc[r][:, 0:DC * TL].rearrange("p (c t) -> p c t", c=DC))
                sSY.issue(sy, V[:, r * 2 * KCS:(r + 1) * 2 * KCS, :],
                          rc[r][:, DC * TL:].rearrange("p (k w) -> p k w", k=2 * KCS))
        SY(fy2)
        return sSY.n

    def emit_attn(att, kv_ready_sy, bias_ve, wout_at, mask_at=0, qs_done=0):
        scr_in = scr_s if att == 0 else scr_y2
        scr_out = scr_x1 if att == 0 else scr_xb
        OBO = OB_OUT1 if att == 0 else OB_OUT2
        for qb in range(NQB):
            main, parts = attn_sched(cfg, att, qb)
            order = [kc for kc, m in main] + [kc for pl, fx in parts for kc in pl]
            mask_of = {pos: m for pos, (kc, m) in enumerate(main) if m is not None}
            U = len(order)
            nm = len(main)
            qsl = slice(qb * QB, (qb + 1) * QB)
            qslot_t = qb % 2
            war_mm = qt_free[qslot_t]

            def fql(sy, att=att, qsl=qsl, qslot_t=qslot_t, war_mm=war_mm, qs_done=qs_done):
                sy.wait_ge(sQS.h, qs_done)
                sy.wait_ge(sMM.h, war_mm)
                sQT.issue(sy, Qt[:, qslot_t],
                          scr_q[att][:, :, qsl].rearrange("c p t -> p c t"))
            SY(fql)
            qt_load = sQT.n
            for h in range(H):
                hp = (h % 2) * 64
                hc = h // 2
                qh = Qt[hp:hp + 64, qslot_t, hc, 0:QB]
                pAV = pB[2 + h % 2][0:65, 0:QB]
                ve_war0 = sVE.n
                grp_sc = []
                for r0 in range(0, U, PTN):
                    rend = min(r0 + PTN, U)
                    base_sc = sSC.n
                    exp_jobs = []
                    npr = 0
                    for g0 in range(r0, rend, 2):
                        grp = list(range(g0, min(g0 + 2, rend)))
                        war_sc = grp_sc[-2] if len(grp_sc) >= 2 else 0

                        def ft(t, grp=grp, war_sc=war_sc, h=h, hp=hp, hc=hc, qh=qh,
                               qt_load=qt_load):
                            t.wait_ge(sSC.h, war_sc)
                            if grp[0] == 0:
                                t.wait_ge(sSY.h, kv_ready_sy)
                                t.wait_ge(sVE.h, max(bias_ve, ve_war0))
                                if h == 0:
                                    t.wait_ge(sQT.h, qt_load)
                            for gi in grp:
                                kc = order[gi]
                                slot = gi % 4
                                ps = pB[slot // 2][0:128, (slot % 2) * QB:(slot % 2 + 1) * QB]
                                ins = t.matmul(ps, K[hp:hp + 64, hc, kc * 128:(kc + 1) * 128],
                                               qh, start=True, stop=True)
                            sMM.inc(ins)
                        T(ft)
                        mm_at = sMM.n
                        ve_mask = 0
                        if any(gi in mask_of for gi in grp):
                            def fvm(v, grp=grp, mm_at=mm_at):
                                v.wait_ge(sMM.h, mm_at)
                                v.wait_ge(sWC.h, mask_at)
                                v.wait_ge(sVE.h, sVE.n)
                                ins = None
                                for gi in grp:
                                    if gi in mask_of:
                                        slot = gi % 4
                                        ps = pB[slot // 2][0:128, (slot % 2) * QB:(slot % 2 + 1) * QB]
                                        ins = v.tensor_add(ps, ps, masks[:, mask_of[gi], :])
                                sVE.inc(ins)
                            VEC(fvm)
                            ve_mask = sVE.n
                        pairs = []
                        i2 = 0
                        while i2 < len(grp):
                            gi = grp[i2]
                            slot = gi % 4
                            if i2 + 1 < len(grp) and slot % 2 == 0:
                                ps = pB[slot // 2][0:128, 0:2 * QB]
                                dst = PT[:, gi - r0:gi - r0 + 2, 0:QB].rearrange(
                                    "p j q -> p (j q)")
                                pairs.append((dst, ps))
                                i2 += 2
                            else:
                                ps = pB[slot // 2][0:128, (slot % 2) * QB:(slot % 2 + 1) * QB]
                                pairs.append((PT[:, gi - r0, 0:QB], ps))
                                i2 += 1
                        exp_jobs.append((mm_at, ve_mask, pairs))
                        npr += len(pairs)
                        grp_sc.append(base_sc + npr)

                    def fs(s, jobs=exp_jobs, h=h):
                        bcol = bb[:, att * H + h:att * H + h + 1]
                        for mm_at_, ve_mask_, pairs_ in jobs:
                            s.wait_ge(sMM.h, mm_at_)
                            if ve_mask_:
                                s.wait_ge(sVE.h, ve_mask_)
                            for dst_, ps_ in pairs_:
                                sSC.inc(s.activation(dst_, ps_, AF.Exp, bias=bcol,
                                                     scale=0.125))
                    SCL(fs)
                    assert sSC.n == base_sc + npr
                    exp_done = sSC.n
                    mainpos = [i2 for i2 in range(r0, rend) if i2 < nm]
                    partpos = [i2 for i2 in range(r0, rend) if i2 >= nm]
                    if mainpos:
                        def ftav(t, h=h, pAV=pAV, exp_done=exp_done, mainpos=mainpos, r0=r0):
                            t.wait_ge(sSC.h, exp_done)
                            if mainpos[0] == 0:
                                t.wait_ge(sVE.h, ve_war0)
                            ins = None
                            for i2 in mainpos:
                                kc = order[i2]
                                ins = t.matmul(pAV, V[:, kc, 65 * h:65 * h + 65], PT[:, i2 - r0, 0:QB],
                                               start=(i2 == 0), stop=(i2 == nm - 1))
                            sMM.inc(ins)
                        T(ftav)
                    if partpos:
                        def ftp(t, h=h, exp_done=exp_done, partpos=partpos, r0=r0):
                            t.wait_ge(sSC.h, exp_done)
                            if partpos[0] == nm:
                                t.wait_ge(sVE.h, ve_war0)
                            ins = None
                            for i2 in partpos:
                                kc = order[i2]
                                ins = t.matmul(pB[4][0:65, 0:QB], V[:, kc, 65 * h:65 * h + 65],
                                               PT[:, i2 - r0, 0:QB],
                                               start=(i2 == nm), stop=(i2 == U - 1))
                            sMM.inc(ins)
                        T(ftp)
                if U > nm:
                    mmp = sMM.n
                    fx = parts[0][1]

                    def fvp(v, mmp=mmp, fx=fx, pAV=pAV):
                        v.wait_ge(sMM.h, mmp)
                        v.wait_ge(sVE.h, sVE.n)
                        sVE.inc(v.tensor_scalar(lrow65[:, 0:QB], pB[4][0:65, 0:QB],
                                                fcol[0:65, fx:fx + 1], None, ALU.mult))
                        v.wait_ge(sVE.h, sVE.n)
                        sVE.inc(v.tensor_add(pAV, pAV, lrow65[:, 0:QB]))
                    VEC(fvp)
                merged_ve = sVE.n
                av_mm = sMM.n

                def fvr(v, pAV=pAV, merged_ve=merged_ve, av_mm=av_mm):
                    v.wait_ge(sMM.h, av_mm)
                    v.wait_ge(sVE.h, max(merged_ve, sVE.n))
                    sVE.inc(v.tensor_copy(lrow65[64:65, :], pAV[64:65, :]))
                    v.wait_ge(sVE.h, sVE.n)
                    sVE.inc(v.reciprocal(lrow65[64:65, :], lrow65[64:65, :]))
                VEC(fvr)
                ve_r = sVE.n

                def ftl(t, ve_r=ve_r):
                    t.wait_ge(sVE.h, ve_r)
                    sMM.inc(t.matmul(pB[4][0:128, 256:256 + QB], onesat64[64:65, :],
                                     lrow65[64:65, :], start=True, stop=True))
                T(ftl)
                mm_l = sMM.n
                ve_bc = sVE.n

                def fvo(v, pAV=pAV, ve_bc=ve_bc):
                    v.wait_ge(sVE.h, max(ve_bc, sVE.n))
                    sVE.inc(v.tensor_copy(aotmp[0:64, :], pAV[0:64, :]))
                VEC(fvo)
                ve_o = sVE.n
                if h % 2 == 0:
                    def fvd(v, mm_l=mm_l, ve_o=ve_o, hc=hc):
                        v.wait_ge(sMM.h, mm_l)
                        v.wait_ge(sVE.h, max(ve_o, sVE.n))
                        sVE.inc(v.tensor_tensor(AO128[0:64, hc, :], aotmp[0:64, :],
                                                pB[4][0:64, 256:256 + QB], ALU.mult))
                    VEC(fvd)
                else:
                    def fvd(v, mm_l=mm_l, ve_o=ve_o):
                        v.wait_ge(sMM.h, mm_l)
                        v.wait_ge(sVE.h, max(ve_o, sVE.n))
                        sVE.inc(v.tensor_tensor(aotmp[0:64, :], aotmp[0:64, :],
                                                pB[4][0:64, 256:256 + QB], ALU.mult))
                    VEC(fvd)
                    ve_d = sVE.n

                    def ftsh(t, ve_d=ve_d):
                        t.wait_ge(sVE.h, ve_d)
                        sMM.inc(t.matmul(pB[5][0:128, 0:QB], shifteye, aotmp, start=True, stop=True))
                    T(ftsh)
                    mm_sh = sMM.n

                    def fvc(v, mm_sh=mm_sh, hc=hc):
                        v.wait_ge(sMM.h, mm_sh)
                        v.wait_ge(sVE.h, sVE.n)
                        sVE.inc(v.tensor_copy(AO128[64:128, hc, :], pB[5][64:128, 0:QB]))
                    VEC(fvc)
            heads_ve = sVE.n
            qt_free[qslot_t] = sMM.n
            tbi = (qb * QB) // TB
            colofs = (qb * QB) % TB
            for fc in range(DC):
                slot = fc % 2
                ve_war = sVE.n

                def fy(sy, fc=fc, slot=slot, ve_war=ve_war):
                    sy.wait_ge(sVE.h, ve_war)
                    sT[slot].issue(sy, STG[:, slot, 0:QB],
                                   scr_in[fc * 128:(fc + 1) * 128, qsl])
                SY(fy)
                sy_in = sT[slot].n
                po = pB[5][0:128, (fc % 2) * QB:(fc % 2 + 1) * QB]
                ve_war2 = sVE.n

                def ft(t, fc=fc, po=po, heads_ve=heads_ve, ve_war2=ve_war2):
                    t.wait_ge(sVE.h, max(heads_ve, ve_war2))
                    t.wait_ge(sWC.h, wout_at)
                    for ci in range(DC):
                        ins = t.matmul(po, WOUT[:, ci, fc * 128:(fc + 1) * 128],
                                       AO128[:, ci, :], start=(ci == 0), stop=(ci == DC - 1))
                    sMM.inc(ins)
                T(ft)
                mm_at = sMM.n
                bcol = biasp[:, OBO + fc:OBO + fc + 1]

                def fv(v, fc=fc, slot=slot, po=po, mm_at=mm_at, sy_in=sy_in, bcol=bcol):
                    v.wait_ge(sMM.h, mm_at)
                    v.wait_ge(sT[slot].h, sy_in)
                    v.wait_ge(sVE.h, sVE.n)
                    sVE.inc(v.scalar_tensor_tensor(STG[:, slot, 0:QB], po, bcol,
                                                   STG[:, slot, 0:QB], ALU.add, ALU.add))
                VEC(fv)
                ve_evac = sVE.n
                st_mm = emit_stats(STG[:, slot, 0:QB], tbi, colofs, QB,
                                   fc == 0 and (qb * QB) % TB == 0,
                                   fc == DC - 1 and (qb * QB + QB) % TB == 0)

                def fy2(sy, fc=fc, slot=slot, ve_evac=ve_evac):
                    sy.wait_ge(sVE.h, ve_evac)
                    sT[slot].issue(sy, scr_out[fc * 128:(fc + 1) * 128, qsl],
                                   STG[:, slot, 0:QB])
                SY(fy2)
            if (qb * QB + QB) % TB == 0:
                emit_stats_flush(tbi, st_mm)

    def emit_y2():
        yv = 0
        for tbi in range(NTB):
            rows_ve = emit_ln_rows(tbi, sVE.n)
            sl = slice(tbi * TB, (tbi + 1) * TB)
            for ci in range(DC):
                ve_war = sVE.n

                def fy(sy, ci=ci, sl=sl, ve_war=ve_war):
                    sy.wait_ge(sVE.h, ve_war)
                    sT[ci % 2].issue(sy, Y2[:, ci, sl],
                                     scr_x1[ci * 128:(ci + 1) * 128, sl])
                SY(fy)
                sy_at = sT[ci % 2].n
                emit_ln_chunk(tbi, ci, rows_ve, Y2[:, ci, sl], Y2[:, ci, sl],
                              GAM_PLAIN, BET_PLAIN, wait_sy=(sT[ci % 2], sy_at))
                ve_n = sVE.n

                def fy2(sy, ci=ci, sl=sl, ve_n=ve_n):
                    sy.wait_ge(sVE.h, ve_n)
                    sT[2].issue(sy, scr_y2[ci * 128:(ci + 1) * 128, sl],
                                Y2[:, ci, sl])
                SY(fy2)
                for hf in range(NHF):
                    yv = emit_stats(Y2[:, ci, tbi * TB + hf * 256:tbi * TB + (hf + 1) * 256],
                                    tbi, hf * 256, 256, ci == 0 and hf == 0,
                                    ci == DC - 1 and hf == NHF - 1)
            emit_stats_flush(tbi, yv)
        return yv

    def emit_step9():
        for tbi in range(NTB):
            rows_ve = emit_ln_rows(tbi, sVE.n)
            for half in range(TB // QB):
                colofs = half * QB
                qsl = slice(tbi * TB + colofs, tbi * TB + colofs + QB)
                ts_slots = []
                for ci in range(DC):
                    ve_war = sVE.n

                    def fy(sy, ci=ci, qsl=qsl, ve_war=ve_war):
                        sy.wait_ge(sVE.h, ve_war)
                        sT[0].issue(sy, STG[:, 0, 0:QB],
                                    scr_xb[ci * 128:(ci + 1) * 128, qsl])
                        sT[1].issue(sy, STG[:, 1, 0:QB],
                                    scr_x1[ci * 128:(ci + 1) * 128, qsl])
                    SY(fy)
                    sy_at = (sT[0].n, sT[1].n)
                    slot = ci % 2
                    mm_war = sMM.n

                    syl = emit_lnbuf(slot, GAM_LN2, ci, mm_war)
                    pR = pB[slot][0:128, 0:QB]
                    pA = pB[2 + slot][0:128, 0:QB]
                    ve_war2 = sVE.n

                    def ftb(t, slot=slot, syl=syl, rows_ve=rows_ve, colofs=colofs,
                            pR=pR, pA=pA, ve_war2=ve_war2):
                        t.wait_ge(sSC.h, syl)
                        t.wait_ge(sVE.h, max(rows_ve, ve_war2))
                        t.matmul(pR, lnbuf[0:1, slot, :], mrowb[0:1, 0, colofs:colofs + QB],
                                 start=True, stop=True)
                        sMM.inc(t.matmul(pA, lnbuf[0:1, slot, :], mrowb[0:1, 1, colofs:colofs + QB],
                                         start=True, stop=True))
                    T(ftb)
                    mmb = sMM.n
                    bcol = biasp[:, OB_LNB + BET_LN2 + ci:OB_LNB + BET_LN2 + ci + 1]

                    def fvn(v, ci=ci, mmb=mmb, bcol=bcol, qsl=qsl, pR=pR, pA=pA, sy_at=sy_at):
                        v.wait_ge(sMM.h, mmb)
                        v.wait_ge(sT[0].h, sy_at[0])
                        v.wait_ge(sT[1].h, sy_at[1])
                        v.wait_ge(sVE.h, sVE.n)
                        sVE.inc(v.tensor_mul(STG[:, 2, 0:QB], STG[:, 0, 0:QB], pR))
                        v.wait_ge(sVE.h, sVE.n)
                        sVE.inc(v.scalar_tensor_tensor(Z[:, ci, qsl], STG[:, 2, 0:QB], bcol, pA,
                                                       ALU.add, ALU.add))
                        v.wait_ge(sVE.h, sVE.n)
                        sVE.inc(v.tensor_add(STG[:, 1, 0:QB], STG[:, 1, 0:QB], STG[:, 0, 0:QB]))
                    VEC(fvn)
                    ve_s2 = sVE.n
                    for ts in range(QB // 128):
                        tt = (tbi * TB + colofs) // 128 + ts
                        oslot = tt % 2
                        ocol = ci * 128
                        if ci == 0 and ts == 0:
                            ts_slots.append(None)
                        ve_war3 = sVE.n

                        def ftt(t, ts=ts, ve_s2=ve_s2, ve_war3=ve_war3):
                            t.wait_ge(sVE.h, max(ve_s2, ve_war3))
                            sMM.inc(t.transpose(pB[4][0:128, ts * 128:(ts + 1) * 128],
                                                STG[:, 1, ts * 128:(ts + 1) * 128], identf))
                        T(ftt)
                        mmt = sMM.n
                        sy_war = sSY.n

                        def fvt(v, ts=ts, oslot=oslot, ocol=ocol, mmt=mmt, sy_war=sy_war):
                            v.wait_ge(sMM.h, mmt)
                            v.wait_ge(sSY.h, sy_war)
                            v.wait_ge(sVE.h, sVE.n)
                            sVE.inc(v.tensor_copy(OSTG[:, oslot, ocol:ocol + 128],
                                                  pB[4][0:128, ts * 128:(ts + 1) * 128]))
                        VEC(fvt)
                ve_done = sVE.n
                for ts in range(QB // 128):
                    tt = (tbi * TB + colofs) // 128 + ts

                    def fyo(sy, tt=tt, ve_done=ve_done):
                        sy.wait_ge(sVE.h, ve_done)
                        sSY.issue(sy, out_d[tt], OSTG[:, tt % 2, :])
                    SY(fyo)
        s2_done = sSY.n

        for tt in range(TL // 128):
            ve_war = sVE.n

            def ftg(t, tt=tt, ve_war=ve_war):
                t.wait_ge(sVE.h, ve_war)
                t.wait_ge(sSC.h, GATEWB_AT)
                for ci in range(DC):
                    ins = t.matmul(pB[4][0:128, 0:NE], Z[:, ci, tt * 128:(tt + 1) * 128],
                                   gatewb[:, ci, :], start=(ci == 0), stop=(ci == DC - 1))
                sMM.inc(ins)
            T(ftg)
            mmg = sMM.n
            g0 = gts[:, tt, :]

            def fvg(v, mmg=mmg, g0=g0):
                v.wait_ge(sMM.h, mmg)
                v.wait_ge(sVE.h, sVE.n)
                sVE.inc(v.reduce_max(SQ[:, 0:1], pB[4][0:128, 0:NE], mybir.AxisListType.X))
                v.wait_ge(sVE.h, sVE.n)
                sVE.inc(v.tensor_scalar(STG[:, 0, 0:NE], pB[4][0:128, 0:NE], SQ[:, 0:1], None, ALU.is_ge))
                v.wait_ge(sVE.h, sVE.n)
                sVE.inc(v.tensor_scalar(STG[:, 1, 0:NE], STG[:, 0, 0:NE], -1.0e30, None, ALU.mult))
                v.wait_ge(sVE.h, sVE.n)
                sVE.inc(v.tensor_add(STG[:, 1, 0:NE], STG[:, 1, 0:NE], pB[4][0:128, 0:NE]))
                v.wait_ge(sVE.h, sVE.n)
                sVE.inc(v.reduce_max(SQ[:, 1:2], STG[:, 1, 0:NE], mybir.AxisListType.X))
                v.wait_ge(sVE.h, sVE.n)
                sVE.inc(v.tensor_sub(SQ[:, 2:3], SQ[:, 0:1], SQ[:, 1:2]))
            VEC(fvg)
            ve_d = sVE.n

            def fsg(s, ve_d=ve_d):
                s.wait_ge(sVE.h, ve_d)
                s.activation(SQ[:, 3:4], SQ[:, 2:3], AF.Sigmoid)
                s.wait_ge(sSC.h, sSC.n)
                sSC.inc(s.activation(SQ[:, 4:5], SQ[:, 2:3], AF.Sigmoid, scale=-1.0))
            SCL(fsg)
            sc_g = sSC.n

            def fvg5(v, g0=g0, sc_g=sc_g):
                v.wait_ge(sSC.h, sc_g)
                sVE.inc(v.tensor_scalar(STG[:, 2, 0:NE], STG[:, 1, 0:NE], SQ[:, 1:2], None, ALU.is_ge))
                v.wait_ge(sVE.h, sVE.n)
                sVE.inc(v.tensor_scalar(STG[:, 0, 0:NE], STG[:, 0, 0:NE], SQ[:, 3:4], None, ALU.mult))
                v.wait_ge(sVE.h, sVE.n)
                sVE.inc(v.scalar_tensor_tensor(g0, STG[:, 2, 0:NE], SQ[:, 4:5], STG[:, 0, 0:NE],
                                               ALU.mult, ALU.add))
            VEC(fvg5)
            ve_g = sVE.n

            def ftt(t, g0=g0, ve_g=ve_g):
                t.wait_ge(sVE.h, ve_g)
                sMM.inc(t.transpose(pB[2][0:NE, 0:128], g0, identf))
            T(ftt)
            mmt = sMM.n

            def fvt(v, tt=tt, mmt=mmt):
                v.wait_ge(sMM.h, mmt)
                v.wait_ge(sVE.h, sVE.n)
                sVE.inc(v.tensor_copy(gerow[0:NE, tt * 128:(tt + 1) * 128], pB[2][0:NE, 0:128]))
            VEC(fvt)

        if dbg_moe:
            gv_ = sVE.n

            def fdbg2(sy):
                sy.wait_ge(sVE.h, gv_)
                sSY.issue(sy, dbg_gts2, gts.rearrange("p t e -> p (t e)"))
            SY(fdbg2)
        def fyf(sy):
            sy.wait_ge(sMM.h, sMM.n)
            sSY.issue(sy, fc2bs32, d["fc2b"])
        SY(fyf)
        f32_at = sSY.n

        def fyc(s):
            s.wait_ge(sSY.h, f32_at)
            sSC.inc(s.activation(fc2bs[0:NE, :], fc2bs32[0:NE, :], AF.Identity))
        SCL(fyc)
        fc2b_at = sSC.n
        for tt in range(TL // 128):
            ve_war = sVE.n

            def ftb(t, tt=tt, ve_war=ve_war):
                t.wait_ge(sVE.h, ve_war)
                t.wait_ge(sSC.h, fc2b_at)
                for dh in range(NDH):
                    ins = t.matmul(pB[5 - dh][0:128, 0:DH],
                                   gerow[0:NE, tt * 128:(tt + 1) * 128],
                                   fc2bs[0:NE, dh * DH:(dh + 1) * DH], start=True, stop=True)
                sMM.inc(ins)
            T(ftb)
            mmb = sMM.n
            oslot = tt % 2
            os_war = os_last[oslot]

            def fvb(v, mmb=mmb, oslot=oslot, os_war=os_war, s2_done=s2_done):
                v.wait_ge(sMM.h, mmb)
                v.wait_ge(sSY.h, s2_done)
                v.wait_ge(sOS.h, os_war)
                for dh in range(NDH):
                    v.wait_ge(sVE.h, sVE.n)
                    sVE.inc(v.tensor_copy(OSTG[:, oslot, dh * DH:(dh + 1) * DH],
                                          pB[5 - dh][0:128, 0:DH]))
            VEC(fvb)
            veb = sVE.n

            def fgb(g, tt=tt, oslot=oslot, veb=veb, s2_done=s2_done):
                g.wait_ge(sVE.h, veb)
                g.wait_ge(sSY.h, s2_done)
                sOS.issue(g, out_d[tt], OSTG[:, oslot, :], accum_op=ALU.add)
            GP(fgb)
            os_last[oslot] = sOS.n
        return s2_done

    def emit_moe(s2_done):
        fc1q_mm = [sMM.n] * 4
        fc2q_mm = [sMM.n] * 4
        for e in range(NE):
            w1_at, w2_at = [], []
            for q in range(4):
                def fg(g, e=e, q=q, war=fc1q_mm[q]):
                    g.wait_ge(sMM.h, war)
                    sW.issue(g, FC1[:, q].rearrange("p c z -> p (c z)"),
                             d["fc1"][e, q])
                GP(fg)
                w1_at.append(sW.n)
            for q in range(4):
                def fg2(g, e=e, q=q, war=fc2q_mm[q]):
                    g.wait_ge(sMM.h, war)
                    sW.issue(g, FC2[:, q * (FFC // 4):(q + 1) * (FFC // 4), :].rearrange("p k z -> p (k z)"),
                             d["fc2"][e, q])
                GP(fg2)
                w2_at.append(sW.n)
            for tb in range(NMTB):
                msl = slice(tb * MTB, (tb + 1) * MTB)
                ntt = MTB // 128
                FH = FFC // 2
                for half in range(2):
                    base_sc = sSC.n
                    acts = []
                    for Fc2 in range(FH):
                        Fc = half * FH + Fc2
                        po = pB[4 + Fc % 2][0:128, 0:MTB]
                        ve_war = sVE.n
                        gel_war = base_sc + Fc2 - 1 if Fc2 >= 2 else base_sc

                        def ft(t, e=e, tb=tb, Fc=Fc, po=po, gel_war=gel_war, ve_war=ve_war, msl=msl):
                            t.wait_ge(sW.h, w1_at[Fc * 4 // FFC])
                            t.wait_ge(sSC.h, gel_war)
                            if tb == 0 and Fc <= 1:
                                t.wait_ge(sVE.h, ve_war)
                            FQ = FFC // 4
                            for ci in range(DC):
                                ins = t.matmul(po, FC1[:, Fc // FQ, ci, (Fc % FQ) * 128:(Fc % FQ + 1) * 128],
                                               Z[:, ci, msl], start=(ci == 0), stop=(ci == DC - 1))
                            sMM.inc(ins)
                        T(ft)
                        if tb == NMTB - 1:
                            fc1q_mm[Fc * 4 // FFC] = sMM.n
                        mm_at = sMM.n
                        b1 = biasp[:, OB_FC1 + e * FFC + Fc:OB_FC1 + e * FFC + Fc + 1]

                        def fvb1(v, po=po, mm_at=mm_at, b1=b1):
                            v.wait_ge(sMM.h, mm_at)
                            v.wait_ge(sVE.h, sVE.n)
                            sVE.inc(v.tensor_scalar(po, po, b1, None, ALU.add))
                        VEC(fvb1)
                        acts.append((po, h2[:, half, Fc2, :], sVE.n))

                    def fgel(s, acts=acts):
                        for po_, dst_, ve_b_ in acts:
                            s.wait_ge(sVE.h, ve_b_)
                            sSC.inc(s.activation(dst_, po_, AF.Gelu))
                    SCL(fgel)
                    h_done = sSC.n
                    if dbg_moe and e == 0 and tb == 0 and half == 1:
                        def fdbg(sy, h_done=h_done):
                            sy.wait_ge(sSC.h, h_done)
                            sy.wait_ge(sW.h, sW.n)
                            sSY.issue(sy, dbg_nx2, Z.rearrange("p c t -> p (c t)"))
                            sSY.issue(sy, dbg_gts, gts.rearrange("p t e -> p (t e)"))
                            sSY.issue(sy, dbg_h, h2.rearrange("p s f t -> p (s f t)"))
                            sSY.issue(sy, dbg_fc1, FC1.rearrange("p q c f -> p (q c f)"))
                            sSY.issue(sy, dbg_fc2, FC2.rearrange("p k z -> p (k z)"))
                            sSY.issue(sy, dbg_sw0, smallswin)
                        SY(fdbg)
                    if dbg_moe and e == 3 and tb == 0 and half == 1:
                        def fdbg3(sy, h_done=h_done):
                            sy.wait_ge(sSC.h, h_done)
                            sSY.issue(sy, dbg_sw1, smallswin)
                        SY(fdbg3)
                    ve_war2 = sVE.n
                    for Fc2 in range(FH):
                        Fc = half * FH + Fc2

                        def ftb(t, tb=tb, half=half, Fc=Fc, Fc2=Fc2, h_done=h_done,
                                ntt=ntt, ve_war2=ve_war2):
                            t.wait_ge(sSC.h, h_done)
                            t.wait_ge(sW.h, w2_at[Fc * 4 // FFC])
                            if Fc == 0:
                                t.wait_ge(sVE.h, ve_war2)
                            ins = None
                            for tt in range(ntt):
                                for dh in range(NDH):
                                    ins = t.matmul(pB[tt * NDH + dh][0:128, 0:DH],
                                                   h2[:, half, Fc2, tt * 128:(tt + 1) * 128],
                                                   FC2[:, Fc, dh * DH:(dh + 1) * DH],
                                                   start=(Fc == 0), stop=(Fc == FFC - 1))
                            sMM.inc(ins)
                        T(ftb)
                        if tb == NMTB - 1:
                            fc2q_mm[Fc * 4 // FFC] = sMM.n
                mmB = sMM.n
                for tt in range(ntt):
                    ti = tb * ntt + tt
                    gcol = gts[:, ti, e:e + 1]
                    oslot = ti % 2
                    os_war = os_last[oslot]

                    def fv(v, tt=tt, gcol=gcol, oslot=oslot, mmB=mmB, os_war=os_war, s2_done=s2_done):
                        v.wait_ge(sMM.h, mmB)
                        v.wait_ge(sSY.h, s2_done)
                        v.wait_ge(sOS.h, os_war)
                        v.wait_ge(sVE.h, sVE.n)
                        for dh in range(NDH):
                            sVE.inc(v.tensor_scalar(OSTG[:, oslot, dh * DH:(dh + 1) * DH],
                                                    pB[tt * NDH + dh][0:128, 0:DH],
                                                    gcol, None, ALU.mult))
                            if dh + 1 < NDH:
                                v.wait_ge(sVE.h, sVE.n)
                    VEC(fv)
                    vev = sVE.n

                    def fg3(g, ti=ti, oslot=oslot, vev=vev, s2_done=s2_done):
                        g.wait_ge(sVE.h, vev)
                        g.wait_ge(sSY.h, s2_done)
                        sOS.issue(g, out_d[ti], OSTG[:, oslot, :], accum_op=ALU.add)
                    GP(fg3)
                    os_last[oslot] = sOS.n

    def _finish():
        os_total = sOS.n
        sy_total = sSY.n

        def ffinX(sy):
            sy.wait_ge(sOS.h, os_total)
            sy.wait_ge(sSY.h, sy_total)
            sy.wait_ge(sQS.h, sQS.n)
            sy.wait_ge(sQT.h, sQT.n)
            sy.wait_ge(sW.h, sW.n)
            sy.wait_ge(sWC.h, sWC.n)
            sy.wait_ge(sCC.h, sCC.n)
            sy.wait_ge(sRW.h, sRW.n)
            sy.wait_ge(sLN.h, sLN.n)
            for s_ in sT:
                sy.wait_ge(s_.h, s_.n)
        SY(ffinX)
        b.out_names = ["o"]
        return nc, b

    if stop_after == "ln":
        return _finish()
    gc1 = emit_qkv(0, mid_cb=lambda g: emit_exchange_send(0, g))
    kv1_sy = emit_exchange_finish(0)
    qs1_done = sQS.n
    emit_qbound()
    emit_bound(K[:, :, 0:TL], 1, entry_sy=kv1_sy)
    bias1_ve = emit_bias_from_bounds(0)
    sc_w1 = sSC.n

    def fw1(g):
        g.wait_ge(sSC.h, sc_w1)
        sWC.issue(g, WOUT.rearrange("p c f -> p (c f)")[0:64], d["wout1"][0:64])
        sWC.issue(g, WOUT.rearrange("p c f -> p (c f)")[64:128], d["wout1"][64:128])
    GP(fw1)
    if stop_after == "qkv1":
        return _finish()
    emit_attn(0, kv1_sy, bias1_ve, sWC.n, qs_done=qs1_done)

    if stop_after == "attn1":
        return _finish()
    y2_ve = emit_y2()
    if stop_after == "y2a":
        return _finish()
    if False:
        _vals = [(sSY, sSY.n), (sW, sW.n), (sWC, sWC.n), (sRW, sRW.n), (sLN, sLN.n),
                 (sMM, sMM.n), (sSC, sSC.n), (sVE, sVE.n)] + [(s_, s_.n) for s_ in sT]

        def _stall(e):
            for s_, v_ in _vals:
                e.wait_ge(s_.h, v_)
        T(_stall); VEC(_stall); SCL(_stall); GP(_stall); SY(_stall)
    if stop_after == "y2c":
        for tbi in range(NTB):
            sl = slice(tbi * TB, (tbi + 1) * TB)
            for ci in range(DC):
                def fvz(v, ci=ci, sl=sl):
                    v.wait_ge(sVE.h, sVE.n)
                    sVE.inc(v.tensor_copy(Z[:, ci, sl], Y2[:, ci, sl]))
                VEC(fvz)
        return _finish()
    if stop_after == "y2d":
        for tbi in range(NTB):
            emit_ln_rows(tbi, sVE.n)
        return _finish()
    for tbi in range(NTB):
        rows_ve = emit_ln_rows(tbi, y2_ve if tbi == 0 else sVE.n)
        sl = slice(tbi * TB, (tbi + 1) * TB)
        for ci in range(DC):
            emit_ln_chunk(tbi, ci, rows_ve, Y2[:, ci, sl], Z[:, ci, sl], GAM_LN1, BET_LN1)
        if stop_after == "y2b" and tbi == 0:
            return _finish()

    if stop_after == "y2":
        return _finish()
    gc2 = emit_qkv(1, entry_dma_waits=[(s_, s_.n) for s_ in sT],
                   mid_cb=lambda g: emit_exchange_send(1, g))
    kv2_sy = emit_exchange_finish(1)
    qs2_done = sQS.n
    emit_qbound()
    emit_bound(K[:, :, 0:TL], 1, entry_sy=kv2_sy)
    bias2_ve = emit_bias_from_bounds(1)
    sc_w2 = sSC.n

    def fw2(g):
        g.wait_ge(sSC.h, sc_w2)
        sWC.issue(g, WOUT.rearrange("p c f -> p (c f)")[0:64], d["wout2"][0:64])
        sWC.issue(g, WOUT.rearrange("p c f -> p (c f)")[64:128], d["wout2"][64:128])
        sWC.issue(g, masks.rearrange("p j q -> p (j q)")[0:64], d["masks"][0:64])
        sWC.issue(g, masks.rearrange("p j q -> p (j q)")[64:128], d["masks"][64:128])
    GP(fw2)
    if stop_after == "qkv2":
        return _finish()
    emit_attn(1, kv2_sy, bias2_ve, sWC.n, mask_at=sWC.n, qs_done=qs2_done)

    if stop_after == "attn2":
        return _finish()
    s2_done = emit_step9()
    if stop_after == "step9":
        return _finish()
    emit_moe(s2_done)

    os_total = sOS.n
    sy_total = sSY.n

    def ffin(sy):
        sy.wait_ge(sOS.h, os_total)
        sy.wait_ge(sSY.h, sy_total)
        sy.wait_ge(sQS.h, sQS.n)
        sy.wait_ge(sQT.h, sQT.n)
    SY(ffin)
    b.out_names = ["o"]
    return nc, b


def _prep_weights(cfg, inputs):
    c = cfg
    D, DC, FF, FFC, NE = c["D"], c["DC"], c["FF"], c["FFC"], c["NE"]
    FC3 = c["FC3"]

    def slices_layout(w):
        F = w.shape[1]
        nsl = F // 256
        r = w.reshape(DC, 128, nsl, 256).transpose(2, 1, 0, 3)
        return np.ascontiguousarray(r.reshape(nsl, 128, DC * 256))

    def wout_layout(w):
        r = w.reshape(DC, 128, D).transpose(1, 0, 2)
        return np.ascontiguousarray(r.reshape(128, DC * D))

    out = {}
    out["wqkv1"] = slices_layout(np.asarray(inputs["L0_attn_w_qkv"], np.float32))
    out["wqkv2"] = slices_layout(np.asarray(inputs["L0_ffn_Wqkv"], np.float32))
    out["wout1"] = wout_layout(np.asarray(inputs["L0_attn_w_out"], np.float32))
    out["wout2"] = wout_layout(np.asarray(inputs["L0_ffn_Wo"], np.float32))
    fc1 = np.asarray(inputs["L0_ffn_fc1_w"], np.float32)
    fc2 = np.asarray(inputs["L0_ffn_fc2_w"], np.float32)
    a = fc1.transpose(0, 2, 1).reshape(NE, DC, 128, FF).transpose(0, 2, 1, 3)
    a = a.reshape(NE, 128, DC, 4, FF // 4).transpose(0, 3, 1, 2, 4)
    out["fc1"] = np.ascontiguousarray(a.reshape(NE, 4, 128, DC * FF // 4))
    bq = fc2.transpose(0, 2, 1).reshape(NE, FFC, 128, D).transpose(0, 2, 1, 3)
    bq = bq.reshape(NE, 128, 4, FFC // 4, D).transpose(0, 2, 1, 3, 4)
    out["fc2"] = np.ascontiguousarray(bq.reshape(NE, 4, 128, FFC * D // 4))

    NBC = 2 * FC3 + 2 * DC + NE * FFC + 4 * DC
    bp = np.zeros((128, NBC), np.float32)

    def cols(vec):
        return np.asarray(vec, np.float32).reshape(-1, 128).T

    o = 0
    bp[:, o:o + FC3] = cols(inputs["L0_attn_b_qkv"]); o += FC3
    bp[:, o:o + DC] = cols(inputs["L0_attn_b_out"]); o += DC
    bp[:, o:o + FC3] = cols(inputs["L0_ffn_bqkv"]); o += FC3
    bp[:, o:o + DC] = cols(inputs["L0_ffn_bo"]); o += DC
    f1b = np.asarray(inputs["L0_ffn_fc1_b"], np.float32)
    bp[:, o:o + NE * FFC] = np.concatenate([cols(f1b[e]) for e in range(NE)], axis=1)
    o += NE * FFC
    o += DC
    bp[:, o:o + DC] = cols(inputs["L0_attn_ln_beta"]); o += DC
    bp[:, o:o + DC] = cols(inputs["L0_ffn_ln1_b"]); o += DC
    bp[:, o:o + DC] = cols(inputs["L0_ffn_ln2_b"]); o += DC
    out["biasp"] = bp

    lnr = np.zeros((1, 4 * D), np.float32)
    lnr[0, 0:D] = 1.0
    lnr[0, D:2 * D] = np.asarray(inputs["L0_attn_ln_gamma"], np.float32)
    lnr[0, 2 * D:3 * D] = np.asarray(inputs["L0_ffn_ln1_w"], np.float32)
    lnr[0, 3 * D:4 * D] = np.asarray(inputs["L0_ffn_ln2_w"], np.float32)
    out["lnrows"] = lnr
    out["fc2b"] = np.ascontiguousarray(np.asarray(inputs["L0_ffn_fc2_b"], np.float32))
    gw = np.asarray(inputs["L0_ffn_gate_W"], np.float32)
    out["gatew"] = np.ascontiguousarray(gw.reshape(DC, 128, NE).transpose(1, 0, 2).reshape(128, DC * NE))

    out["identb"] = np.eye(128, dtype=ml_dtypes.bfloat16)
    out["identf"] = np.eye(128, dtype=np.float32)
    she = np.zeros((64, 128), ml_dtypes.bfloat16)
    for i2 in range(64):
        she[i2, 64 + i2] = 1
    out["shifteye"] = she
    out["onescol"] = np.ones((128, 1), ml_dtypes.bfloat16)
    out["onesat64"] = np.ones((65, 128), np.float32)
    DCH = cfg["DC"]
    Hh = cfg["H"]
    bd = np.zeros((128, DCH * Hh), np.float32)
    for ci in range(DCH):
        for j in range(2):
            h = 2 * ci + j
            if h < Hh:
                bd[j * 64:(j + 1) * 64, ci * Hh + h] = 1.0
    out["bdiag"] = bd.astype(ml_dtypes.bfloat16)
    return out


def _masks_for(cfg, parity):
    NJ, QB, NMASK = cfg["NJ"], cfg["QB"], cfg["NMASK"]
    tri = np.zeros((NJ, 128, QB), np.float32)
    for j in range(NJ):
        for p in range(128):
            kk = min(p + 128 * j, QB)
            tri[j, p, :kk] = NEG_BIG
    zero = np.zeros((128, QB), np.float32)
    neg = np.full((128, QB), NEG_BIG, np.float32)
    tiles = []
    for j in range(NJ):
        tiles.append(tri[j] if parity == 0 else zero)
    for j in range(NJ):
        tiles.append(neg if parity == 0 else tri[j])
    for j in range(NJ):
        tiles.append(tri[j] if parity == 0 else neg)
    for j in range(NJ):
        tiles.append(zero if parity == 0 else tri[j])
    m = np.stack(tiles, 0)
    return np.ascontiguousarray(m.transpose(1, 0, 2).reshape(128, NMASK * QB)).astype(ml_dtypes.bfloat16)


def _in_maps_for(cfg, inputs, shared):
    c = cfg
    TQ, TL = c["TQ"], c["TL"]
    x = np.asarray(inputs["L0_attn_x"], np.float32)
    quarters = {0: (0, 3), 1: (1, 2)}
    in_maps = []
    for core in range(c["ncores"]):
        bi, par = core // 2, core % 2
        qlo, qhi = quarters[par]
        xb = x[bi]
        xloc = np.concatenate([xb[qlo * TQ:(qlo + 1) * TQ], xb[qhi * TQ:(qhi + 1) * TQ]], 0)
        fc = np.zeros((128, 2), np.float32)
        fc[:, 0] = 0.0 if par == 0 else 1.0
        fc[:, 1] = 1.0 if par == 0 else 0.0
        m = dict(shared)
        m["xfm"] = np.ascontiguousarray(xloc.T)
        m["fcol"] = fc
        m["masks"] = _masks_for(c, par)
        in_maps.append(m)
    return in_maps


def _assemble(cfg, results, Bn):
    c = cfg
    D, TQ, TL, NDH, DH = c["D"], c["TQ"], c["TL"], c["NDH"], c["DH"]
    quarters = {0: (0, 3), 1: (1, 2)}
    out = np.zeros((Bn, c["TG"], D), np.float32)
    for core in range(c["ncores"]):
        bi, par = core // 2, core % 2
        qlo, qhi = quarters[par]
        od = np.asarray(results[core]["o"])
        oloc = od.reshape(TL, D)
        out[bi, qlo * TQ:(qlo + 1) * TQ] = oloc[0:TQ]
        out[bi, qhi * TQ:(qhi + 1) * TQ] = oloc[TQ:TL]
    return out


def kernel(**inputs):
    cfg = make_cfg()
    x = np.asarray(inputs["L0_attn_x"], np.float32)
    shared = _prep_weights(cfg, inputs)
    nc, b = build_program(cfg)
    in_maps = _in_maps_for(cfg, inputs, shared)
    res = run_bass_kernel_spmd(nc, in_maps, core_ids=list(range(cfg["ncores"])))
    return _assemble(cfg, res.results, x.shape[0])
